# version 03
# baseline (speedup 1.0000x reference)
import numpy as np
import concourse.bass as bass
import concourse.mybir as mybir
from concourse.bass_utils import run_bass_kernel_spmd
from contextlib import ExitStack

F32 = mybir.dt.float32
BF16 = mybir.dt.bfloat16
AF = mybir.ActivationFunctionType
ALU = mybir.AluOpType
AX = mybir.AxisListType

NCORE = 8
S = 8192
D = 4096
TOK = S // NCORE
HALO = TOK + 512
NH = 32
DH = 128
WIN = 512
SCALE = 1.0 / float(np.sqrt(DH))
EPS = 1e-5


class Prog:

    ENGS = ("sync", "gp", "pe", "dv", "ac")

    def __init__(self, nc, stack):
        self.nc = nc
        self.stack = stack
        self.sem = {}
        self.cnt = {}
        self.waited = {e: {} for e in self.ENGS}
        for e in ("pe", "dv", "ac"):
            self.sem[e] = stack.enter_context(nc.semaphore(f"S_{e}"))
            self.cnt[e] = 0
        self.eng = {"sync": nc.sync, "gp": nc.gpsimd, "pe": nc.tensor,
                    "dv": nc.vector, "ac": nc.scalar}
        self.state = {}

    def _stream_sem(self, key):
        if key not in self.sem:
            self.sem[key] = self.stack.enter_context(
                self.nc.semaphore(f"S_{key.replace(':', '_')}"))
            self.cnt[key] = 0
        return self.sem[key]

    def _wait(self, e, semkey, val):
        if val > self.waited[e].get(semkey, 0):
            self.eng[e].wait_ge(self.sem[semkey], val)
            self.waited[e][semkey] = val

    def op(self, e, build, reads=(), writes=(), dma=False, stream=None):
        need = {}

        def add(dep):
            if dep is None:
                return
            dk, dc = dep
            if dc > need.get(dk, 0):
                need[dk] = dc

        for k in reads:
            st = self.state.get(k)
            if st:
                add(st[0])
        for k in writes:
            st = self.state.get(k)
            if st:
                add(st[0])
                for rk, rc in st[1].items():
                    add((rk, rc))
        if dma:
            semkey = f"{e}:{stream or 'main'}"
            self._stream_sem(semkey)
            add((semkey, self.cnt[semkey]))
        else:
            semkey = e
        for dk, dc in sorted(need.items()):
            self._wait(e, dk, dc)
        inst = build(self.eng[e])
        inc = 16 if dma else 1
        inst.then_inc(self.sem[semkey], inc)
        self.cnt[semkey] += inc
        me = (semkey, self.cnt[semkey])
        for k in reads:
            st = self.state.setdefault(k, [None, {}])
            st[1][semkey] = self.cnt[semkey]
        for k in writes:
            self.state[k] = [me, {}]
        return inst

    def hard_fence(self):
        for e in self.ENGS:
            for sk, c in list(self.cnt.items()):
                if c > 0:
                    self._wait(e, sk, c)

    def finish(self, out_keys):
        need = {}
        for k in out_keys:
            st = self.state.get(k)
            if st and st[0]:
                dk, dc = st[0]
                need[dk] = max(need.get(dk, 0), dc)
        for dk, dc in sorted(need.items()):
            self._wait("sync", dk, dc)



def e_ln(p, x_f32, xn_f32, tmp_col, key_x, key_xn, key_tmp, n=D, eps_col=None):
    p.op("dv", lambda e: e.reduce_sum(out=tmp_col[:, 0:1], in_=x_f32[:], axis=AX.X),
         reads=[key_x], writes=[key_tmp])
    p.op("dv", lambda e: e.tensor_scalar_mul(out=tmp_col[:, 0:1], in0=tmp_col[:, 0:1],
                                             scalar1=1.0 / n),
         reads=[key_tmp], writes=[key_tmp])
    p.op("dv", lambda e: e.tensor_scalar(out=xn_f32[:], in0=x_f32[:],
                                         scalar1=tmp_col[:, 0:1], scalar2=None,
                                         op0=ALU.subtract),
         reads=[key_x, key_tmp], writes=[key_xn])
    p.op("ac", lambda e: e.activation(out=x_f32[:], in_=xn_f32[:], func=AF.Square),
         reads=[key_xn], writes=[key_x])
    p.op("dv", lambda e: e.reduce_sum(out=tmp_col[:, 1:2], in_=x_f32[:], axis=AX.X),
         reads=[key_x], writes=[key_tmp])
    p.op("ac", lambda e: e.activation(out=tmp_col[:, 1:2], in_=tmp_col[:, 1:2],
                                      func=AF.Sqrt, scale=1.0 / n,
                                      bias=eps_col),
         reads=[key_tmp, "epscol"], writes=[key_tmp])
    p.op("dv", lambda e: e.reciprocal(out=tmp_col[:, 1:2], in_=tmp_col[:, 1:2]),
         reads=[key_tmp], writes=[key_tmp])
    p.op("dv", lambda e: e.tensor_scalar(out=xn_f32[:], in0=xn_f32[:],
                                         scalar1=tmp_col[:, 1:2], scalar2=None,
                                         op0=ALU.mult),
         reads=[key_xn, key_tmp], writes=[key_xn])


def e_mm_group(p, psum_ap, steps, key_psum, read_keys):
    n = len(steps)

    def build(e):
        inst = None
        for i, (lh, rh) in enumerate(steps):
            inst = e.matmul(psum_ap, lh, rh, start=(i == 0), stop=(i == n - 1))
        return inst

    return p.op("pe", build, reads=list(read_keys), writes=[key_psum])


def e_transpose(p, out_psum, in_sbuf, ident, key_out, read_keys):
    return p.op("pe", lambda e: e.transpose(out_psum, in_sbuf, ident),
                reads=list(read_keys), writes=[key_out])



def stage_stream(W):
    O, K = W.shape
    return np.ascontiguousarray(
        W.reshape(O // 128, 128, K // 128, 128).transpose(0, 3, 2, 1))


def stage_rhs(WT, m):
    K, M = WT.shape
    assert M == m
    return np.ascontiguousarray(WT.reshape(K // 128, 128, M).transpose(1, 0, 2))


def rope_cache_T(S_, Dh):
    inv_freq = 1.0 / (10000.0 ** (np.arange(0, Dh, 2, dtype=np.float32) / Dh))
    t = np.arange(S_, dtype=np.float32)
    f = np.outer(t, inv_freq)
    cos = np.zeros((S_, Dh), dtype=np.float32)
    sin = np.zeros((S_, Dh), dtype=np.float32)
    cos[:, 0::2] = np.cos(f)
    cos[:, 1::2] = np.cos(f)
    sin[:, 0::2] = np.sin(f)
    sin[:, 1::2] = np.sin(f)
    return np.ascontiguousarray(cos.T), np.ascontiguousarray(sin.T)



def e_rope(p, src, cos, sin, rt, t1, out_bf, keys, n, off=0):
    ksrc, kcos, ksin, krt, kt1, kout = keys
    p.op("dv", lambda e: e.tensor_scalar_mul(out=rt[0:64, 0:n],
                                             in0=src[64:128, off:off + n],
                                             scalar1=-1.0),
         reads=[ksrc], writes=[krt])
    p.op("ac", lambda e: e.copy(out=rt[64:128, 0:n], in_=src[0:64, off:off + n]),
         reads=[ksrc], writes=[krt])
    p.op("dv", lambda e: e.tensor_mul(out=t1[:, 0:n], in0=src[:, off:off + n],
                                      in1=cos[:, off:off + n]),
         reads=[ksrc, kcos], writes=[kt1])
    p.op("dv", lambda e: e.tensor_mul(out=rt[:, 0:n], in0=rt[:, 0:n],
                                      in1=sin[:, off:off + n]),
         reads=[krt, ksin], writes=[krt])
    p.op("dv", lambda e: e.tensor_add(out=out_bf[:, off:off + n], in0=t1[:, 0:n],
                                      in1=rt[:, 0:n]),
         reads=[kt1, krt], writes=[kout])


def emit_attn0(nc, p, st, io):
    sb = lambda name, shape, dt: st.enter_context(nc.sbuf_tensor(name, shape, dt))
    NT = HALO // 128

    identf = sb("la_idf", [128, 128], F32)
    identb = sb("la_idb", [128, 128], BF16)
    onesb = sb("la_ones", [128, 128], BF16)
    kmask = sb("la_kmask", [128, NT], F32)
    for nm, t in (("identf", identf), ("kmask", kmask)):
        p.op("sync", lambda e, nm=nm, t=t: e.dma_start(out=t[:], in_=io[nm][:, :]),
             writes=[nm], dma=True, stream="cst")
    for nm, t in (("identb", identb), ("onesb", onesb)):
        p.op("gp", lambda e, nm=nm, t=t: e.dma_start(out=t[:], in_=io[nm][:, :]),
             writes=[nm], dma=True, stream="cstb")

    krT = sb("la_krT", [128, HALO], BF16)
    v_s = sb("la_v", [128, NT, 128], BF16)
    blend = sb("la_blend", [128, NH, TOK], BF16)
    xnT = sb("la_xnT", [128, 32, HALO], BF16)

    with ExitStack() as ph:
        al = lambda name, shape, dt: ph.enter_context(nc.sbuf_tensor(name, shape, dt))
        alp = lambda name, shape, dt: ph.enter_context(nc.psum_tensor(name, shape, dt))
        x0b = [al("la_x00", [128, D], F32)]
        xnb = [al("la_xn0", [128, D], F32)]
        xbfb = [al("la_xb0", [128, D], BF16)]
        coltmp = al("la_col", [128, 3], F32)
        p.op("dv", lambda e: e.memset(coltmp[:, 2:3], EPS), writes=["epscol"])
        tpp = [alp("la_tp0", [128, 128], BF16), alp("la_tp1", [128, 128], BF16)]
        for t in range(NT):
            b = 0
            p.op("sync", lambda e, t=t, b=b: e.dma_start(
                out=x0b[b][:], in_=io["x0h"][128 * t:128 * (t + 1), :]),
                writes=[f"x0b{b}"], dma=True, stream="x")
            e_ln(p, x0b[b][:], xnb[b][:], coltmp, f"x0b{b}", f"xnb{b}", "coltmp",
                 eps_col=coltmp[:, 2:3])
            p.op("ac", lambda e, b=b: e.copy(out=xbfb[b][:], in_=xnb[b][:]),
                 reads=[f"xnb{b}"], writes=[f"xbf{b}"])
            for a in range(32):
                tb = a % 2
                e_transpose(p, tpp[tb][:], xbfb[b][:, 128 * a:128 * (a + 1)],
                            identb[:], f"tp{tb}", [f"xbf{b}", "identb"])
                p.op("dv", lambda e, a=a, t=t, tb=tb: e.tensor_copy(
                    out=xnT[:, a, 128 * t:128 * (t + 1)], in_=tpp[tb][:]),
                    reads=[f"tp{tb}"], writes=["xnT"])
        p.hard_fence()

    with ExitStack() as ph:
        al = lambda name, shape, dt: ph.enter_context(nc.sbuf_tensor(name, shape, dt))
        alp = lambda name, shape, dt: ph.enter_context(nc.psum_tensor(name, shape, dt))
        cosk = al("la_cosk", [128, HALO], F32)
        sink = al("la_sink", [128, HALO], F32)
        wkb = al("la_wk0b", [128, 32, 128], BF16)
        wvb = al("la_wv0b", [128, 32, 128], BF16)
        kps = alp("la_kps", [128, 512], F32)
        ktf = al("la_ktf", [128, HALO], F32)
        rt_t = al("la_rt", [128, 512], F32)
        t1_t = al("la_t1", [128, 512], F32)
        vps = alp("la_vps", [128, 128], F32)
        p.op("sync", lambda e: e.dma_start(out=cosk[:], in_=io["cosk"][:, :]),
             writes=["cosk"], dma=True, stream="cst")
        p.op("sync", lambda e: e.dma_start(out=sink[:], in_=io["sink"][:, :]),
             writes=["sink"], dma=True, stream="cst")
        p.op("gp", lambda e: e.dma_start(out=wkb[:], in_=io["wk_s"][0]),
             writes=["wkb"], dma=True, stream="w")
        p.op("gp", lambda e: e.dma_start(out=wvb[:], in_=io["wv_r"][:, :, :]),
             writes=["wvb"], dma=True, stream="w")
        for c in range(3):
            e_mm_group(p, kps[:, 0:512],
                       [(wkb[:, a, :], xnT[:, a, 512 * c:512 * (c + 1)])
                        for a in range(32)],
                       "kps", ["wkb", "xnT"])
            p.op("ac", lambda e, c=c: e.copy(out=ktf[:, 512 * c:512 * (c + 1)],
                                             in_=kps[:, 0:512]),
                 reads=["kps"], writes=["ktf"])
        for c in range(3):
            e_rope(p, ktf, cosk, sink, rt_t, t1_t, krT,
                   ("ktf", "cosk", "sink", "rt_t", "t1_t", "krT"), 512, off=512 * c)
        for t in range(NT):
            e_mm_group(p, vps[:],
                       [(xnT[:, a, 128 * t:128 * (t + 1)], wvb[:, a, :])
                        for a in range(32)],
                       "vps", ["xnT", "wvb"])
            p.op("ac", lambda e, t=t: e.copy(out=v_s[:, t, :], in_=vps[:]),
                 reads=["vps"], writes=["v_s"])
        p.hard_fence()

    with ExitStack() as ph:
        al = lambda name, shape, dt: ph.enter_context(nc.sbuf_tensor(name, shape, dt))
        alp = lambda name, shape, dt: ph.enter_context(nc.psum_tensor(name, shape, dt))
        cosq = al("la_cosq", [128, TOK], F32)
        sinq = al("la_sinq", [128, TOK], F32)
        alph = al("la_alph", [128, 1024], F32)
        wqb = [al("la_wqb0", [128, 32, 128], BF16)]
        qps = alp("la_qps", [128, 512], F32)
        qf = al("la_qf", [128, TOK], F32)
        qrT = al("la_qrT", [128, TOK], BF16)
        rt_t = al("la_rt2", [128, 512], F32)
        t1_t = al("la_t12", [128, 512], F32)
        scp = [alp("la_sc0", [128, 512], F32), alp("la_sc1", [128, 512], F32)]
        sig = al("la_sig", [128, 4, 512], BF16)
        rsp = alp("la_rsp", [128, 512], F32)
        wop = alp("la_wop", [128, 512], F32)
        rrec = al("la_rrec", [128, 512], F32)
        wout = [al("la_wo0", [128, 512], F32), al("la_wo1", [128, 512], F32)]
        for nm, t in (("cosq", cosq), ("sinq", sinq), ("alph", alph)):
            p.op("sync", lambda e, nm=nm, t=t: e.dma_start(out=t[:], in_=io[nm][:, :]),
                 writes=[nm], dma=True, stream="cst")
        for h in range(NH):
            wb = 0
            p.op("gp", lambda e, h=h, wb=wb: e.dma_start(out=wqb[wb][:], in_=io["wq_s"][h]),
                 writes=[f"wqb{wb}"], dma=True, stream="w")
            for c in range(2):
                e_mm_group(p, qps[:, 0:512],
                           [(wqb[wb][:, a, :], xnT[:, a, 256 + 512 * c:256 + 512 * (c + 1)])
                            for a in range(32)],
                           "qps", [f"wqb{wb}", "xnT"])
                p.op("ac", lambda e, c=c: e.copy(out=qf[:, 512 * c:512 * (c + 1)],
                                                 in_=qps[:, 0:512]),
                     reads=["qps"], writes=["qf"])
            for c in range(2):
                e_rope(p, qf, cosq, sinq, rt_t, t1_t, qrT,
                       ("qf", "cosq", "sinq", "rt_t", "t1_t", "qrT"), 512, off=512 * c)
            for r in range(5):
                q0 = max(0, 256 * (r - 1))
                qlen = 256 if r in (0, 4) else 512
                rw = r % 2
                for kc in range(4):
                    b = kc % 2
                    e_mm_group(p, scp[b][:, 0:qlen],
                               [(krT[:, 256 * r + 128 * kc:256 * r + 128 * (kc + 1)],
                                 qrT[:, q0:q0 + qlen])],
                               f"scp{b}", ["krT", "qrT"])
                    p.op("ac", lambda e, r=r, kc=kc, b=b, qlen=qlen: e.activation(
                        out=sig[:, kc, 0:qlen], in_=scp[b][:, 0:qlen],
                        func=AF.Sigmoid,
                        bias=kmask[:, 2 * r + kc:2 * r + kc + 1], scale=SCALE),
                        reads=[f"scp{b}", "kmask"], writes=[f"sig{kc}"])
                e_mm_group(p, rsp[:, 0:qlen],
                           [(onesb[:], sig[:, kc, 0:qlen]) for kc in range(4)],
                           "rsp", ["onesb", "sig0", "sig1", "sig2", "sig3"])
                e_mm_group(p, wop[:, 0:qlen],
                           [(v_s[:, 2 * r + kc, :], sig[:, kc, 0:qlen])
                            for kc in range(4)],
                           "wop", ["v_s", "sig0", "sig1", "sig2", "sig3"])
                p.op("dv", lambda e, qlen=qlen: e.tensor_scalar_max(
                    out=rrec[:, 0:qlen], in0=rsp[:, 0:qlen], scalar1=1e-30),
                    reads=["rsp"], writes=["rrec"])
                p.op("dv", lambda e, qlen=qlen: e.reciprocal(
                    out=rrec[:, 0:qlen], in_=rrec[:, 0:qlen]),
                    reads=["rrec"], writes=["rrec"])
                p.op("dv", lambda e, rw=rw, qlen=qlen: e.tensor_mul(
                    out=wout[rw][:, 0:qlen], in0=wop[:, 0:qlen], in1=rrec[:, 0:qlen]),
                    reads=["wop", "rrec"], writes=[f"wout{rw}"])
                if r >= 1:
                    j = r - 1
                    offp = 0 if j == 0 else 256
                    pw = (r - 1) % 2
                    p.op("dv", lambda e, rw=rw, pw=pw, offp=offp: e.tensor_sub(
                        out=t1_t[:, 0:256], in0=wout[rw][:, 0:256],
                        in1=wout[pw][:, offp:offp + 256]),
                        reads=[f"wout{rw}", f"wout{pw}"], writes=["t1_t"])
                    p.op("dv", lambda e, j=j: e.tensor_mul(
                        out=t1_t[:, 0:256], in0=t1_t[:, 0:256],
                        in1=alph[:, 256 * j:256 * (j + 1)]),
                        reads=["t1_t", "alph"], writes=["t1_t"])
                    p.op("dv", lambda e, h=h, j=j, pw=pw, offp=offp: e.tensor_add(
                        out=blend[:, h, 256 * j:256 * (j + 1)], in0=t1_t[:, 0:256],
                        in1=wout[pw][:, offp:offp + 256]),
                        reads=["t1_t", f"wout{pw}"], writes=["blend"])
        p.hard_fence()

    with ExitStack() as ph:
        al = lambda name, shape, dt: ph.enter_context(nc.sbuf_tensor(name, shape, dt))
        alp = lambda name, shape, dt: ph.enter_context(nc.psum_tensor(name, shape, dt))
        wob = [al("la_woAb", [128, 32, 128], BF16), al("la_woBb", [128, 32, 128], BF16)]
        aop = alp("la_aop", [128, 512], F32)
        aof = al("la_aof", [128, 512], F32)
        tpf = [alp("la_tpf0", [128, 128], F32), alp("la_tpf1", [128, 128], F32)]
        stage = al("la_stage", [128, 4, 128], F32)
        for oc in range(32):
            b = oc % 2
            p.op("gp", lambda e, oc=oc, b=b: e.dma_start(
                out=wob[b][:], in_=io["wo_s"][oc]),
                writes=[f"wob{b}"], dma=True, stream="w")
            for qc in range(2):
                e_mm_group(p, aop[:, 0:512],
                           [(wob[b][:, a, :], blend[:, a, 512 * qc:512 * (qc + 1)])
                            for a in range(32)],
                           "aop", [f"wob{b}", "blend"])
                p.op("ac", lambda e: e.copy(out=aof[:], in_=aop[:, 0:512]),
                     reads=["aop"], writes=["aof"])
                for pc in range(4):
                    tb = pc % 2
                    e_transpose(p, tpf[tb][:], aof[:, 128 * pc:128 * (pc + 1)],
                                identf[:], f"tpf{tb}", ["aof", "identf"])
                    p.op("dv", lambda e, pc=pc, tb=tb: e.tensor_copy(
                        out=stage[:, pc, :], in_=tpf[tb][:]),
                        reads=[f"tpf{tb}"], writes=["stage"])
                p.op("gp", lambda e, oc=oc, qc=qc: e.dma_start(
                    out=io["a0"][512 * qc:512 * (qc + 1), 128 * oc:128 * (oc + 1)]
                    .rearrange("(pc pp) o -> pp pc o", pp=128),
                    in_=stage[:]),
                    reads=["stage"], writes=["hbm_a0"], dma=True, stream="out")
        p.hard_fence()
    p.finish(["hbm_a0"])



def build_LA():
    nc = bass.Bass("TRN2", target_bir_lowering=False)
    io = {}
    def inp(name, shape):
        io[name] = nc.dram_tensor(name, list(shape), F32, kind="ExternalInput").ap()
    inp("x0h", (HALO, D))
    inp("kmask", (128, HALO // 128))
    inp("cosq", (128, TOK)); inp("sinq", (128, TOK))
    inp("cosk", (128, HALO)); inp("sink", (128, HALO))
    inp("alph", (128, 1024)); inp("omal", (128, 1024))
    inp("wq_s", (32, 128, 32, 128))
    inp("wk_s", (1, 128, 32, 128))
    inp("wv_r", (128, 32, 128))
    inp("wo_s", (32, 128, 32, 128))
    inp("identf", (128, 128)); inp("onesb", (128, 128)); inp("identb", (128, 128))
    io["a0"] = nc.dram_tensor("a0", [TOK, D], F32, kind="ExternalOutput").ap()
    with ExitStack() as st:
        p = Prog(nc, st)
        emit_attn0(nc, p, st, io)
    return nc


def host_prep_LA(inputs):
    x = np.ascontiguousarray(inputs["L0_attn_x"].reshape(S, D).astype(np.float32))
    wq = inputs["L0_attn_Wq"].astype(np.float32)
    wk = inputs["L0_attn_Wk"].astype(np.float32)
    wv = inputs["L0_attn_Wv"].astype(np.float32)
    wo = inputs["L0_attn_Wo"].astype(np.float32)
    cosT, sinT = rope_cache_T(S, DH)
    cosP = np.pad(cosT, ((0, 0), (256, 256)))
    sinP = np.pad(sinT, ((0, 0), (256, 256)))
    xP = np.pad(x, ((256, 256), (0, 0)))
    shared = {
        "wq_s": stage_stream(wq),
        "wk_s": stage_stream(wk).reshape(1, 128, 32, 128),
        "wv_r": stage_rhs(np.ascontiguousarray(wv.T), 128),
        "wo_s": stage_stream(wo),
        "identf": np.eye(128, dtype=np.float32),
        "identb": np.eye(128, dtype=np.float32),
        "onesb": np.ones((128, 128), dtype=np.float32),
    }
    alpha = (np.arange(256, dtype=np.float32) / 255.0)
    percore = []
    for c in range(NCORE):
        h0 = 1024 * c
        kmask = np.zeros((128, HALO // 128), dtype=np.float32)
        for t in range(HALO // 128):
            rows = h0 - 256 + 128 * t + np.arange(128)
            kmask[:, t] = np.where((rows >= 0) & (rows < S), 0.0, -30000.0)
        al = np.tile(alpha, (128, 4)).reshape(128, 1024)
        om = 1.0 - al
        if c == 0:
            al[:, 0:256] = 1.0
            om[:, 0:256] = 0.0
        percore.append({
            "x0h": np.ascontiguousarray(xP[h0:h0 + HALO]),
            "kmask": kmask,
            "cosq": np.ascontiguousarray(cosP[:, h0 + 256:h0 + 256 + TOK]),
            "sinq": np.ascontiguousarray(sinP[:, h0 + 256:h0 + 256 + TOK]),
            "cosk": np.ascontiguousarray(cosP[:, h0:h0 + HALO]),
            "sink": np.ascontiguousarray(sinP[:, h0:h0 + HALO]),
            "alph": np.ascontiguousarray(al),
            "omal": np.ascontiguousarray(om),
        })
    return shared, percore



def emit_ffn0(nc, p, st, io):
    sb = lambda name, shape, dt: st.enter_context(nc.sbuf_tensor(name, shape, dt))
    NT8 = TOK // 128

    identb = sb("lb_idb", [128, 128], BF16)
    identf = sb("lb_idf", [128, 128], F32)
    p.op("gp", lambda e: e.dma_start(out=identb[:], in_=io["identb"][:, :]),
         writes=["identb"], dma=True, stream="cstb")
    p.op("sync", lambda e: e.dma_start(out=identf[:], in_=io["identf"][:, :]),
         writes=["identf"], dma=True, stream="cst")
    xnT = sb("lb_xnT", [128, 32, TOK], BF16)
    whT = sb("lb_whT", [128, NT8, 128], BF16)
    bcol = sb("lb_bcol", [128, 32], F32)
    p.op("sync", lambda e: e.dma_start(out=bcol[:], in_=io["bcol"][:, :]),
         writes=["bcol"], dma=True, stream="cst")

    with ExitStack() as ph:
        al = lambda name, shape, dt: ph.enter_context(nc.sbuf_tensor(name, shape, dt))
        alp = lambda name, shape, dt: ph.enter_context(nc.psum_tensor(name, shape, dt))
        x0 = al("lb_x0", [128, D], F32)
        xn = al("lb_xn", [128, D], F32)
        xb = al("lb_xb", [128, D], BF16)
        coltmp = al("lb_col", [128, 3], F32)
        p.op("dv", lambda e: e.memset(coltmp[:, 2:3], EPS), writes=["epscol"])
        tpp = [alp("lb_tp0", [128, 128], BF16), alp("lb_tp1", [128, 128], BF16)]
        for t in range(NT8):
            p.op("sync", lambda e, t=t: e.dma_start(
                out=x0[:], in_=io["x0o"][128 * t:128 * (t + 1), :]),
                writes=["x0b0"], dma=True, stream="x")
            e_ln(p, x0[:], xn[:], coltmp, "x0b0", "xnb0", "coltmp",
                 eps_col=coltmp[:, 2:3])
            p.op("ac", lambda e: e.copy(out=xb[:], in_=xn[:]),
                 reads=["xnb0"], writes=["xbf0"])
            for a in range(32):
                tb = a % 2
                e_transpose(p, tpp[tb][:], xb[:, 128 * a:128 * (a + 1)],
                            identb[:], f"tp{tb}", ["xbf0", "identb"])
                p.op("dv", lambda e, a=a, t=t, tb=tb: e.tensor_copy(
                    out=xnT[:, a, 128 * t:128 * (t + 1)], in_=tpp[tb][:]),
                    reads=[f"tp{tb}"], writes=["xnT"])
        p.hard_fence()

    with ExitStack() as ph:
        al = lambda name, shape, dt: ph.enter_context(nc.sbuf_tensor(name, shape, dt))
        alp = lambda name, shape, dt: ph.enter_context(nc.psum_tensor(name, shape, dt))
        wqfb = al("lb_wqfb", [128, 32, 256], BF16)
        wkfb = al("lb_wkfb", [128, 32, 256], BF16)
        afb = al("lb_afb", [128, 32, 128], BF16)
        p.op("gp", lambda e: e.dma_start(out=wqfb[:], in_=io["wqf_r"][:, :, :]),
             writes=["wqfb"], dma=True, stream="w")
        p.op("gp", lambda e: e.dma_start(out=wkfb[:], in_=io["wkf_r"][:, :, :]),
             writes=["wkfb"], dma=True, stream="w")
        p.op("gp", lambda e: e.dma_start(out=afb[:], in_=io["af_r"][:, :, :]),
             writes=["afb"], dma=True, stream="w")
        qp = alp("lb_qp", [128, 256], F32)
        kp = alp("lb_kp", [128, 256], F32)
        hp = alp("lb_hp", [128, 128], F32)
        tpb = alp("lb_tpb", [128, 128], BF16)
        qf = al("lb_qf", [128, 256], F32)
        kf = al("lb_kf", [128, 256], F32)
        s8 = al("lb_s8", [128, 8], F32)
        scol = al("lb_scol", [128, 2], F32)
        hf = al("lb_hf", [128, 128], F32)
        wexp = al("lb_wexp", [128, 128], F32)
        whb = al("lb_whb", [128, 128], BF16)
        for t in range(NT8):
            e_mm_group(p, qp[:], [(xnT[:, a, 128 * t:128 * (t + 1)], wqfb[:, a, :])
                                  for a in range(32)], "qp", ["xnT", "wqfb"])
            p.op("ac", lambda e: e.copy(out=qf[:], in_=qp[:]),
                 reads=["qp"], writes=["qf"])
            e_mm_group(p, kp[:], [(xnT[:, a, 128 * t:128 * (t + 1)], wkfb[:, a, :])
                                  for a in range(32)], "kp", ["xnT", "wkfb"])
            p.op("ac", lambda e: e.copy(out=kf[:], in_=kp[:]),
                 reads=["kp"], writes=["kf"])
            p.op("dv", lambda e: e.tensor_mul(out=qf[:], in0=qf[:], in1=kf[:]),
                 reads=["qf", "kf"], writes=["qf"])
            for e8 in range(8):
                p.op("dv", lambda e, e8=e8: e.reduce_sum(
                    out=s8[:, e8:e8 + 1], in_=qf[:, 32 * e8:32 * (e8 + 1)], axis=AX.X),
                    reads=["qf"], writes=["s8"])
            p.op("dv", lambda e: e.tensor_scalar_mul(
                out=s8[:], in0=s8[:], scalar1=1.0 / float(np.sqrt(32.0))),
                reads=["s8"], writes=["s8"])
            p.op("dv", lambda e: e.reduce_max(out=scol[:, 0:1], in_=s8[:], axis=AX.X),
                 reads=["s8"], writes=["scol"])
            p.op("ac", lambda e: e.mul(out=scol[:, 0:1], in_=scol[:, 0:1], mul=-1.0),
                 reads=["scol"], writes=["scol"])
            p.op("ac", lambda e: e.activation(out=s8[:], in_=s8[:], func=AF.Exp,
                                              bias=scol[:, 0:1]),
                 reads=["s8", "scol"], writes=["s8"])
            p.op("dv", lambda e: e.reduce_sum(out=scol[:, 1:2], in_=s8[:], axis=AX.X),
                 reads=["s8"], writes=["scol"])
            p.op("dv", lambda e: e.reciprocal(out=scol[:, 1:2], in_=scol[:, 1:2]),
                 reads=["scol"], writes=["scol"])
            p.op("dv", lambda e: e.tensor_scalar(out=s8[:], in0=s8[:],
                                                 scalar1=scol[:, 1:2], scalar2=None,
                                                 op0=ALU.mult),
                 reads=["s8", "scol"], writes=["s8"])
            e_mm_group(p, hp[:], [(xnT[:, a, 128 * t:128 * (t + 1)], afb[:, a, :])
                                  for a in range(32)], "hp", ["xnT", "afb"])
            p.op("ac", lambda e: e.copy(out=hf[:], in_=hp[:]),
                 reads=["hp"], writes=["hf"])
            for r_ in range(16):
                p.op("ac", lambda e, r_=r_: e.copy(out=wexp[:, 8 * r_:8 * (r_ + 1)],
                                                   in_=s8[:]),
                     reads=["s8"], writes=["wexp"])
            p.op("dv", lambda e: e.tensor_mul(out=whb[:], in0=hf[:], in1=wexp[:]),
                 reads=["hf", "wexp"], writes=["whb"])
            e_transpose(p, tpb[:], whb[:], identb[:], "tpb", ["whb", "identb"])
            p.op("dv", lambda e, t=t: e.tensor_copy(out=whT[:, t, :], in_=tpb[:]),
                 reads=["tpb"], writes=["whT"])
        p.hard_fence()

    with ExitStack() as ph:
        al = lambda name, shape, dt: ph.enter_context(nc.sbuf_tensor(name, shape, dt))
        alp = lambda name, shape, dt: ph.enter_context(nc.psum_tensor(name, shape, dt))
        bmrb = al("lb_bmrb", [128, D], BF16)
        p.op("gp", lambda e: e.dma_start(out=bmrb[:], in_=io["bmr"][:, :]),
             writes=["bmrb"], dma=True, stream="w")
        wb = [al("lb_wb0", [128, 32, 128], BF16), al("lb_wb1", [128, 32, 128], BF16)]
        fp = alp("lb_fp", [128, 512], F32)
        ff = al("lb_ff", [128, 512], F32)
        tpf = [alp("lb_tpf0", [128, 128], F32), alp("lb_tpf1", [128, 128], F32)]
        stg = [al("lb_st0", [128, 4, 128], F32), al("lb_st1", [128, 4, 128], F32)]
        x0p = [al("lb_x0p0", [128, 4, 128], F32), al("lb_x0p1", [128, 4, 128], F32)]
        a0p = [al("lb_a0p0", [128, 4, 128], F32), al("lb_a0p1", [128, 4, 128], F32)]
        for oc in range(32):
            b = oc % 2
            p.op("gp", lambda e, oc=oc, b=b: e.dma_start(
                out=wb[b][:], in_=io["w_s"][oc]),
                writes=[f"wb{b}"], dma=True, stream="w")
            for tc in range(2):
                sbf = (2 * oc + tc) % 2
                steps = [(wb[b][:, a, :], xnT[:, a, 512 * tc:512 * (tc + 1)])
                         for a in range(32)]
                steps.append((bmrb[:, 128 * oc:128 * (oc + 1)],
                              whT[:, 4 * tc:4 * (tc + 1), :]))
                e_mm_group(p, fp[:, 0:512], steps, "fp",
                           [f"wb{b}", "xnT", "bmrb", "whT"])
                p.op("dv", lambda e, oc=oc: e.tensor_scalar(
                    out=ff[:], in0=fp[:, 0:512], scalar1=bcol[:, oc:oc + 1],
                    scalar2=None, op0=ALU.add),
                    reads=["fp", "bcol"], writes=["ff"])
                p.op("sync", lambda e, oc=oc, tc=tc, sbf=sbf: e.dma_start(
                    out=x0p[sbf][:],
                    in_=io["x0o"][512 * tc:512 * (tc + 1), 128 * oc:128 * (oc + 1)]
                    .rearrange("(pc pp) o -> pp pc o", pp=128)),
                    writes=[f"x0p{sbf}"], dma=True, stream="xp")
                p.op("sync", lambda e, oc=oc, tc=tc, sbf=sbf: e.dma_start(
                    out=a0p[sbf][:],
                    in_=io["a0"][512 * tc:512 * (tc + 1), 128 * oc:128 * (oc + 1)]
                    .rearrange("(pc pp) o -> pp pc o", pp=128)),
                    writes=[f"a0p{sbf}"], dma=True, stream="ap")
                for pc in range(4):
                    tb = pc % 2
                    e_transpose(p, tpf[tb][:], ff[:, 128 * pc:128 * (pc + 1)],
                                identf[:], f"tpf{tb}", ["ff", "identf"])
                    p.op("dv", lambda e, pc=pc, tb=tb, sbf=sbf: e.tensor_copy(
                        out=stg[sbf][:, pc, :], in_=tpf[tb][:]),
                        reads=[f"tpf{tb}"], writes=[f"stg{sbf}"])
                p.op("dv", lambda e, sbf=sbf: e.tensor_add(
                    out=stg[sbf][:], in0=stg[sbf][:], in1=x0p[sbf][:]),
                    reads=[f"stg{sbf}", f"x0p{sbf}"], writes=[f"stg{sbf}"])
                p.op("dv", lambda e, sbf=sbf: e.tensor_add(
                    out=stg[sbf][:], in0=stg[sbf][:], in1=a0p[sbf][:]),
                    reads=[f"stg{sbf}", f"a0p{sbf}"], writes=[f"stg{sbf}"])
                p.op("gp", lambda e, oc=oc, tc=tc, sbf=sbf: e.dma_start(
                    out=io["x1"][512 * tc:512 * (tc + 1), 128 * oc:128 * (oc + 1)]
                    .rearrange("(pc pp) o -> pp pc o", pp=128),
                    in_=stg[sbf][:]),
                    reads=[f"stg{sbf}"], writes=["hbm_x1"], dma=True, stream="out")
        p.hard_fence()

    with ExitStack() as ph:
        al = lambda name, shape, dt: ph.enter_context(nc.sbuf_tensor(name, shape, dt))
        x1t = al("lb_x1t", [128, D], F32)
        xn1t = al("lb_xn1t", [128, D], F32)
        coltmp = al("lb_col2", [128, 3], F32)
        p.op("dv", lambda e: e.memset(coltmp[:, 2:3], EPS), writes=["epscol2"])
        for t in range(NT8):
            p.op("sync", lambda e, t=t: e.dma_start(
                out=x1t[:], in_=io["x1"][128 * t:128 * (t + 1), :]),
                reads=["hbm_x1"], writes=["x1t"], dma=True, stream="x")
            e_ln(p, x1t[:], xn1t[:], coltmp, "x1t", "xn1t", "coltmp2",
                 eps_col=coltmp[:, 2:3])
            p.op("gp", lambda e, t=t: e.dma_start(
                out=io["xn1"][128 * t:128 * (t + 1), :], in_=xn1t[:]),
                reads=["xn1t"], writes=["hbm_xn1"], dma=True, stream="out")
        p.hard_fence()
    p.finish(["hbm_x1", "hbm_xn1"])


def build_LB():
    nc = bass.Bass("TRN2", target_bir_lowering=False)
    io = {}
    def inp(name, shape):
        io[name] = nc.dram_tensor(name, list(shape), F32, kind="ExternalInput").ap()
    inp("x0o", (TOK, D)); inp("a0", (TOK, D))
    inp("w_s", (32, 128, 32, 128))
    inp("wqf_r", (128, 32, 256)); inp("wkf_r", (128, 32, 256))
    inp("af_r", (128, 32, 128)); inp("bmr", (128, D)); inp("bcol", (128, 32))
    inp("identf", (128, 128)); inp("identb", (128, 128))
    io["x1"] = nc.dram_tensor("x1", [TOK, D], F32, kind="ExternalOutput").ap()
    io["xn1"] = nc.dram_tensor("xn1", [TOK, D], F32, kind="ExternalOutput").ap()
    with ExitStack() as st:
        p = Prog(nc, st)
        emit_ffn0(nc, p, st, io)
    return nc


def host_prep_LB(inputs):
    w = inputs["L0_ffn_W"].astype(np.float32)
    bv = inputs["L0_ffn_b"].astype(np.float32)
    wqf = inputs["L0_ffn_Wq"].astype(np.float32)
    wkf = inputs["L0_ffn_Wk"].astype(np.float32)
    A = inputs["L0_ffn_A"].astype(np.float32)
    Bm = inputs["L0_ffn_Bm"].astype(np.float32)
    A2 = np.ascontiguousarray(A.transpose(1, 0, 2).reshape(128, D))
    Bmr = np.ascontiguousarray(Bm.transpose(2, 0, 1).reshape(128, D))
    shared = {
        "w_s": stage_stream(w),
        "wqf_r": stage_rhs(np.ascontiguousarray(wqf.T), 256),
        "wkf_r": stage_rhs(np.ascontiguousarray(wkf.T), 256),
        "af_r": stage_rhs(np.ascontiguousarray(A2.T), 128),
        "bmr": Bmr,
        "bcol": np.ascontiguousarray(bv.reshape(32, 128).T),
        "identf": np.eye(128, dtype=np.float32),
        "identb": np.eye(128, dtype=np.float32),
    }
    return shared



def _cmul(p, outre, outim, are, aim, bre, bim, t1, keys, sub=False):
    kor, koi, kar, kai, kbr, kbi, kt1 = keys
    p.op("dv", lambda e: e.tensor_mul(out=t1, in0=aim, in1=bim),
         reads=[kai, kbi], writes=[kt1])
    p.op("dv", lambda e: e.tensor_mul(out=outre, in0=are, in1=bre),
         reads=[kar, kbr], writes=[kor])
    p.op("dv", lambda e: e.tensor_sub(out=outre, in0=outre, in1=t1),
         reads=[kor, kt1], writes=[kor])
    p.op("dv", lambda e: e.tensor_mul(out=t1, in0=aim, in1=bre),
         reads=[kai, kbr], writes=[kt1])
    p.op("dv", lambda e: e.tensor_mul(out=outim, in0=are, in1=bim),
         reads=[kar, kbi], writes=[koi])
    p.op("dv", lambda e: e.tensor_add(out=outim, in0=outim, in1=t1),
         reads=[koi, kt1], writes=[koi])


def emit_s4(nc, p, st, io):
    sb = lambda name, shape, dt: st.enter_context(nc.sbuf_tensor(name, shape, dt))
    kpad = io["kpad"]
    scr = io["scr"]

    identf = sb("lc_idf", [128, 128], F32)
    identb = sb("lc_idb", [128, 128], BF16)
    jrev = sb("lc_jrev", [128, 128], F32)
    p.op("sync", lambda e: e.dma_start(out=identf[:], in_=io["identf"][:, :]),
         writes=["identf"], dma=True, stream="cst")
    p.op("gp", lambda e: e.dma_start(out=identb[:], in_=io["identb"][:, :]),
         writes=["identb"], dma=True, stream="cstb")
    p.op("sync", lambda e: e.dma_start(out=jrev[:], in_=io["jrev"][:, :]),
         writes=["jrev"], dma=True, stream="cst")
    dcol = sb("lc_dcol", [128, 1], F32)

    with ExitStack() as ph:
        al = lambda name, shape, dt: ph.enter_context(nc.sbuf_tensor(name, shape, dt))
        alp = lambda name, shape, dt: ph.enter_context(nc.psum_tensor(name, shape, dt))
        o64 = al("lc_o64", [1, 64], F32)
        o128 = al("lc_o128", [1, 128], F32)
        sc1 = al("lc_sc1", [1, 1], F32)
        scol = al("lc_scol", [64, 1], F32)
        small = {}
        for nm in ("lam_re", "lam_im", "p_re", "p_im", "q_re", "q_im",
                   "ct_re", "ct_im", "bp"):
            small[nm] = al("lc_" + nm, [64, 1], F32)
        vct_re = al("lc_vctre", [64, 64], F32)
        vct_im = al("lc_vctim", [64, 64], F32)
        for nm in list(small) + ["vct_re", "vct_im"]:
            t = small.get(nm) or {"vct_re": vct_re, "vct_im": vct_im}[nm]
            p.op("sync", lambda e, nm=nm, t=t: e.dma_start(out=t[:], in_=io[nm][:, :]),
                 writes=[nm], dma=True, stream="cst")
        p.op("sync", lambda e: e.dma_start(out=o64[:], in_=io["ones64"][:, :]),
             writes=["o64"], dma=True, stream="cst")
        p.op("sync", lambda e: e.dma_start(out=o128[:], in_=io["ones128"][:, :]),
             writes=["o128"], dma=True, stream="cst")
        p.op("sync", lambda e: e.dma_start(out=sc1[:], in_=io["lstep"][:, :]),
             writes=["sc1"], dma=True, stream="cst")
        p.op("sync", lambda e: e.dma_start(out=dcol[0:1, :], in_=io["dco"][:, :]),
             writes=["dcol"], dma=True, stream="cst")
        pbd = alp("lc_pbd", [128, 1], F32)
        e_mm_group(p, pbd[:], [(o128[:], dcol[0:1, 0:1])], "pbd", ["o128", "dcol"])
        p.op("dv", lambda e: e.tensor_copy(out=dcol[:], in_=pbd[:]),
             reads=["pbd"], writes=["dcol"])
        p.op("ac", lambda e: e.activation(out=sc1[:], in_=sc1[:], func=AF.Exp),
             reads=["sc1"], writes=["sc1"])
        p.op("dv", lambda e: e.reciprocal(out=sc1[:], in_=sc1[:]),
             reads=["sc1"], writes=["sc1"])
        p.op("ac", lambda e: e.mul(out=sc1[:], in_=sc1[:], mul=2.0),
             reads=["sc1"], writes=["sc1"])
        psc = alp("lc_psc", [64, 1], F32)
        e_mm_group(p, psc[:], [(o64[:], sc1[:])], "psc", ["o64", "sc1"])
        p.op("dv", lambda e: e.tensor_copy(out=scol[:], in_=psc[:]),
             reads=["psc"], writes=["scol"])
        bre = al("lc_bre", [64, 1], F32)
        bim = al("lc_bim", [64, 1], F32)
        pb2 = alp("lc_pb2", [64, 1], F32)
        e_mm_group(p, pb2[:], [(vct_re[:], small["bp"][:])], "pb2", ["vct_re", "bp"])
        p.op("dv", lambda e: e.tensor_copy(out=bre[:], in_=pb2[:]),
             reads=["pb2"], writes=["bre"])
        e_mm_group(p, pb2[:], [(vct_im[:], small["bp"][:])], "pb2", ["vct_im", "bp"])
        p.op("dv", lambda e: e.tensor_copy(out=bim[:], in_=pb2[:]),
             reads=["pb2"], writes=["bim"])
        vre = al("lc_vre", [64, 4], F32)
        vim = al("lc_vim", [64, 4], F32)
        nvre = al("lc_nvre", [64, 4], F32)
        tt1 = al("lc_tt1", [64, 1], F32)
        combos = [("ct_re", "ct_im", bre, bim, "bre", "bim"),
                  ("ct_re", "ct_im", small["p_re"], small["p_im"], "p_re", "p_im"),
                  ("q_re", "q_im", bre, bim, "bre", "bim"),
                  ("q_re", "q_im", small["p_re"], small["p_im"], "p_re", "p_im")]
        for j, (ur, ui, wr_t, wi_t, kwr, kwi) in enumerate(combos):
            urt, uit = small[ur], small[ui]
            p.op("dv", lambda e, urt=urt, wr_t=wr_t, j=j: e.tensor_mul(
                out=vre[:, j:j + 1], in0=urt[:], in1=wr_t[:]),
                reads=[ur, kwr], writes=["vre"])
            p.op("dv", lambda e, uit=uit, wi_t=wi_t: e.tensor_mul(
                out=tt1[:], in0=uit[:], in1=wi_t[:]),
                reads=[ui, kwi], writes=["tt1"])
            p.op("dv", lambda e, j=j: e.tensor_add(
                out=vre[:, j:j + 1], in0=vre[:, j:j + 1], in1=tt1[:]),
                reads=["vre", "tt1"], writes=["vre"])
            p.op("dv", lambda e, urt=urt, wi_t=wi_t, j=j: e.tensor_mul(
                out=vim[:, j:j + 1], in0=urt[:], in1=wi_t[:]),
                reads=[ur, kwi], writes=["vim"])
            p.op("dv", lambda e, uit=uit, wr_t=wr_t: e.tensor_mul(
                out=tt1[:], in0=uit[:], in1=wr_t[:]),
                reads=[ui, kwr], writes=["tt1"])
            p.op("dv", lambda e, j=j: e.tensor_sub(
                out=vim[:, j:j + 1], in0=vim[:, j:j + 1], in1=tt1[:]),
                reads=["vim", "tt1"], writes=["vim"])
        p.op("dv", lambda e: e.tensor_scalar_mul(out=nvre[:], in0=vre[:], scalar1=-1.0),
             reads=["vre"], writes=["nvre"])
        wrow_re = al("lc_wrowre", [1, D], F32)
        wrow_im = al("lc_wrowim", [1, D], F32)
        p.op("sync", lambda e: e.dma_start(out=wrow_re[:], in_=io["wre"][:, :]),
             writes=["wrow_re"], dma=True, stream="cst")
        p.op("sync", lambda e: e.dma_start(out=wrow_im[:], in_=io["wim"][:, :]),
             writes=["wrow_im"], dma=True, stream="cst")
        dre = al("lc_dre", [64, D], F32)
        dim = al("lc_dim", [64, D], F32)
        rec = al("lc_rec", [64, D], F32)
        pwb = alp("lc_pwb", [64, 512], F32)
        for c in range(8):
            sl = slice(512 * c, 512 * (c + 1))
            e_mm_group(p, pwb[:], [(o64[:], wrow_re[:, sl])], "pwb",
                       ["o64", "wrow_re"])
            p.op("dv", lambda e, sl=sl: e.tensor_scalar(
                out=dre[:, sl], in0=pwb[:], scalar1=scol[:, 0:1],
                scalar2=small["lam_re"][:, 0:1], op0=ALU.mult, op1=ALU.subtract),
                reads=["pwb", "scol", "lam_re"], writes=["dre"])
            e_mm_group(p, pwb[:], [(o64[:], wrow_im[:, sl])], "pwb",
                       ["o64", "wrow_im"])
            p.op("dv", lambda e, sl=sl: e.tensor_scalar(
                out=dim[:, sl], in0=pwb[:], scalar1=scol[:, 0:1],
                scalar2=small["lam_im"][:, 0:1], op0=ALU.mult, op1=ALU.subtract),
                reads=["pwb", "scol", "lam_im"], writes=["dim"])
        p.op("ac", lambda e: e.activation(out=rec[:], in_=dre[:], func=AF.Square),
             reads=["dre"], writes=["recA"])
        sq2 = al("lc_sq2", [64, D], F32)
        p.op("ac", lambda e: e.activation(out=sq2[:], in_=dim[:], func=AF.Square),
             reads=["dim"], writes=["sq2"])
        p.op("dv", lambda e: e.tensor_add(out=rec[:], in0=rec[:], in1=sq2[:]),
             reads=["recA", "sq2"], writes=["recB"])
        p.op("dv", lambda e: e.reciprocal(out=rec[:], in_=rec[:]),
             reads=["recB"], writes=["recC"])
        p.op("dv", lambda e: e.tensor_mul(out=dre[:], in0=dre[:], in1=rec[:]),
             reads=["dre", "recC"], writes=["idre"])
        p.op("dv", lambda e: e.tensor_mul(out=dim[:], in0=dim[:], in1=rec[:]),
             reads=["dim", "recC"], writes=["idimN"])
        kre = al("lc_kre", [4, D], F32)
        kim = al("lc_kim", [4, D], F32)
        pk = alp("lc_pk", [4, 512], F32)
        for c in range(8):
            sl = slice(512 * c, 512 * (c + 1))
            e_mm_group(p, pk[:], [(vre[:], dre[:, sl]), (vim[:], dim[:, sl])],
                       "pk", ["vre", "vim", "idre", "idimN"])
            p.op("dv", lambda e, sl=sl: e.tensor_copy(out=kre[:, sl], in_=pk[:]),
                 reads=["pk"], writes=["kre"])
            e_mm_group(p, pk[:], [(vim[:], dre[:, sl]), (nvre[:], dim[:, sl])],
                       "pk", ["vim", "nvre", "idre", "idimN"])
            p.op("dv", lambda e, sl=sl: e.tensor_copy(out=kim[:, sl], in_=pk[:]),
                 reads=["pk"], writes=["kim"])
        scr4 = io["scr4"]
        p.op("gp", lambda e: e.dma_start(out=scr4[0:4, :], in_=kre[:]),
             reads=["kre"], writes=["hbm_scr4a"], dma=True, stream="sc")
        p.op("gp", lambda e: e.dma_start(out=scr4[4:8, :], in_=kim[:]),
             reads=["kim"], writes=["hbm_scr4b"], dma=True, stream="sc")
        k64 = {}
        for j, nm in enumerate(("k00", "k01", "k10", "k11")):
            for ri, part in enumerate(("re", "im")):
                t = al(f"lc_{nm}{part}", [64, 64], F32)
                k64[nm + part] = t
                p.op("sync", lambda e, j=j, ri=ri, t=t: e.dma_start(
                    out=t[:], in_=scr4[4 * ri + j].rearrange("(pp f) -> pp f", f=64)),
                    reads=["hbm_scr4a", "hbm_scr4b"], writes=[nm + part],
                    dma=True, stream="k64")
        crow_re = al("lc_crowre", [64, 64], F32)
        crow_im = al("lc_crowim", [64, 64], F32)
        p.op("sync", lambda e: e.dma_start(out=crow_re[:], in_=io["cre"][:, :]),
             writes=["crow_re"], dma=True, stream="cst")
        p.op("sync", lambda e: e.dma_start(out=crow_im[:], in_=io["cim"][:, :]),
             writes=["crow_im"], dma=True, stream="cst")
        zre = al("lc_zre", [64, 64], F32)
        zim = al("lc_zim", [64, 64], F32)
        t64 = al("lc_t64", [64, 64], F32)
        den = al("lc_den", [64, 64], F32)
        _cmul(p, zre[:], zim[:], k64["k01re"][:], k64["k01im"][:],
              k64["k10re"][:], k64["k10im"][:], t64[:],
              ("zre", "zim", "k01re", "k01im", "k10re", "k10im", "t64"))
        p.op("dv", lambda e: e.tensor_scalar_add(out=k64["k11re"][:],
                                                 in0=k64["k11re"][:], scalar1=1.0),
             reads=["k11re"], writes=["k11re"])
        p.op("ac", lambda e: e.activation(out=den[:], in_=k64["k11re"][:],
                                          func=AF.Square),
             reads=["k11re"], writes=["den"])
        p.op("ac", lambda e: e.activation(out=t64[:], in_=k64["k11im"][:],
                                          func=AF.Square),
             reads=["k11im"], writes=["t64"])
        p.op("dv", lambda e: e.tensor_add(out=den[:], in0=den[:], in1=t64[:]),
             reads=["den", "t64"], writes=["den"])
        p.op("dv", lambda e: e.reciprocal(out=den[:], in_=den[:]),
             reads=["den"], writes=["den"])
        tre = al("lc_tre", [64, 64], F32)
        tim = al("lc_tim", [64, 64], F32)
        p.op("dv", lambda e: e.tensor_mul(out=tre[:], in0=zre[:], in1=k64["k11re"][:]),
             reads=["zre", "k11re"], writes=["tre"])
        p.op("dv", lambda e: e.tensor_mul(out=t64[:], in0=zim[:], in1=k64["k11im"][:]),
             reads=["zim", "k11im"], writes=["t64"])
        p.op("dv", lambda e: e.tensor_add(out=tre[:], in0=tre[:], in1=t64[:]),
             reads=["tre", "t64"], writes=["tre"])
        p.op("dv", lambda e: e.tensor_mul(out=tim[:], in0=zim[:], in1=k64["k11re"][:]),
             reads=["zim", "k11re"], writes=["tim"])
        p.op("dv", lambda e: e.tensor_mul(out=t64[:], in0=zre[:], in1=k64["k11im"][:]),
             reads=["zre", "k11im"], writes=["t64"])
        p.op("dv", lambda e: e.tensor_sub(out=tim[:], in0=tim[:], in1=t64[:]),
             reads=["tim", "t64"], writes=["tim"])
        p.op("dv", lambda e: e.tensor_mul(out=tre[:], in0=tre[:], in1=den[:]),
             reads=["tre", "den"], writes=["tre"])
        p.op("dv", lambda e: e.tensor_mul(out=tim[:], in0=tim[:], in1=den[:]),
             reads=["tim", "den"], writes=["tim"])
        p.op("dv", lambda e: e.tensor_sub(out=tre[:], in0=k64["k00re"][:], in1=tre[:]),
             reads=["k00re", "tre"], writes=["ure"])
        p.op("dv", lambda e: e.tensor_sub(out=tim[:], in0=k64["k00im"][:], in1=tim[:]),
             reads=["k00im", "tim"], writes=["uim"])
        are = al("lc_are", [64, 64], F32)
        aim = al("lc_aim", [64, 64], F32)
        _cmul(p, are[:], aim[:], crow_re[:], crow_im[:], tre[:], tim[:], t64[:],
              ("arere", "areim", "crow_re", "crow_im", "ure", "uim", "t64"))
        e64re = al("lc_e64re", [64, 64], F32)
        e64im = al("lc_e64im", [64, 64], F32)
        e64imn = al("lc_e64imn", [64, 64], F32)
        e64bre = al("lc_e64bre", [64, 64], F32)
        e64bimn = al("lc_e64bimn", [64, 64], F32)
        twre = al("lc_twre", [64, 64], F32)
        twim = al("lc_twim", [64, 64], F32)
        for nm, t in (("e64re", e64re), ("e64im", e64im), ("e64imn", e64imn),
                      ("e64bre", e64bre), ("e64bimn", e64bimn),
                      ("twre", twre), ("twim", twim)):
            p.op("sync", lambda e, nm=nm, t=t: e.dma_start(out=t[:], in_=io[nm][:, :]),
                 writes=[nm], dma=True, stream="cst")
        py = alp("lc_py", [64, 64], F32)
        yre = al("lc_yre", [64, 64], F32)
        yim = al("lc_yim", [64, 64], F32)
        e_mm_group(p, py[:], [(e64re[:], are[:]), (e64imn[:], aim[:])],
                   "py", ["e64re", "e64imn", "arere", "areim"])
        p.op("dv", lambda e: e.tensor_copy(out=yre[:], in_=py[:]),
             reads=["py"], writes=["yre"])
        e_mm_group(p, py[:], [(e64im[:], are[:]), (e64re[:], aim[:])],
                   "py", ["e64im", "e64re", "arere", "areim"])
        p.op("dv", lambda e: e.tensor_copy(out=yim[:], in_=py[:]),
             reads=["py"], writes=["yim"])
        _cmul(p, zre[:], zim[:], yre[:], yim[:], twre[:], twim[:], t64[:],
              ("zre2", "zim2", "yre", "yim", "twre", "twim", "t64"))
        pz = alp("lc_pz", [64, 64], F32)
        ztre = al("lc_ztre", [64, 64], F32)
        ztim = al("lc_ztim", [64, 64], F32)
        e_transpose(p, pz[:], zre[:], identf[0:64, 0:64], "pz", ["zre2", "identf"])
        p.op("dv", lambda e: e.tensor_copy(out=ztre[:], in_=pz[:]),
             reads=["pz"], writes=["ztre"])
        e_transpose(p, pz[:], zim[:], identf[0:64, 0:64], "pz", ["zim2", "identf"])
        p.op("dv", lambda e: e.tensor_copy(out=ztim[:], in_=pz[:]),
             reads=["pz"], writes=["ztim"])
        e_mm_group(p, py[:], [(e64bre[:], ztre[:]), (e64bimn[:], ztim[:])],
                   "py", ["e64bre", "e64bimn", "ztre", "ztim"])
        kmat = al("lc_kmat", [64, 64], F32)
        p.op("dv", lambda e: e.tensor_copy(out=kmat[:], in_=py[:]),
             reads=["py"], writes=["kmat"])
        zt = al("lc_zt", [1, 256], F32)
        p.op("dv", lambda e: e.memset(zt[:], 0.0), writes=["zt"])
        p.op("gp", lambda e: e.dma_start(out=kpad[0:127].rearrange("(a f) -> a f", a=1),
                                         in_=zt[:, 0:127]),
             reads=["zt"], writes=["hbm_kpad"], dma=True, stream="kp")
        p.op("gp", lambda e: e.dma_start(
            out=kpad[4223:4352].rearrange("(a f) -> a f", a=1), in_=zt[:, 0:129]),
            reads=["zt"], writes=["hbm_kpad"], dma=True, stream="kp")
        p.op("gp", lambda e: e.dma_start(
            out=kpad[127:4223].rearrange("(pp f) -> pp f", f=64), in_=kmat[:]),
            reads=["kmat"], writes=["hbm_kpad"], dma=True, stream="kp")
        p.hard_fence()

    with ExitStack() as ph:
        al = lambda name, shape, dt: ph.enter_context(nc.sbuf_tensor(name, shape, dt))
        alp = lambda name, shape, dt: ph.enter_context(nc.psum_tensor(name, shape, dt))
        bd = al("lc_bd", [128, 32, 128], BF16)
        gd = [al("lc_gd0", [128, 128], F32), al("lc_gd1", [128, 128], F32)]
        pbd2 = [alp("lc_pbd0", [128, 128], F32), alp("lc_pbd1", [128, 128], F32)]
        for d_ in range(32):
            b = d_ % 2
            p.op("sync", lambda e, d_=d_, b=b: e.dma_start(
                out=gd[b][:],
                in_=bass.AP(tensor=kpad.tensor, offset=kpad.offset + 128 * d_,
                            ap=[[1, 128], [1, 128]])),
                reads=["hbm_kpad"], writes=[f"gd{b}"], dma=True, stream="kg")
            e_mm_group(p, pbd2[b][:], [(jrev[:], gd[b][:])], f"pbd{b}",
                       ["jrev", f"gd{b}"])
            p.op("dv", lambda e, d_=d_, b=b: e.tensor_copy(out=bd[:, d_, :],
                                                           in_=pbd2[b][:]),
                 reads=[f"pbd{b}"], writes=["bd"])
        xT = al("lc_xT", [128, 32, TOK], BF16)
        xb = al("lc_xb", [128, D], BF16)
        tpp = [alp("lc_tp0", [128, 128], BF16), alp("lc_tp1", [128, 128], BF16)]
        for t in range(TOK // 128):
            p.op("gp", lambda e, t=t: e.dma_start(
                out=xb[:], in_=io["xn1"][128 * t:128 * (t + 1), :]),
                writes=["xb"], dma=True, stream="x")
            for a in range(32):
                tb = a % 2
                e_transpose(p, tpp[tb][:], xb[:, 128 * a:128 * (a + 1)],
                            identb[:], f"tp{tb}", ["xb", "identb"])
                p.op("dv", lambda e, a=a, t=t, tb=tb: e.tensor_copy(
                    out=xT[:, a, 128 * t:128 * (t + 1)], in_=tpp[tb][:]),
                    reads=[f"tp{tb}"], writes=["xT"])
        yp = alp("lc_yp", [128, 512], F32)
        yf = al("lc_yf", [128, 512], F32)
        tpf = [alp("lc_tpf0", [128, 128], F32), alp("lc_tpf1", [128, 128], F32)]
        stg = [al("lc_st0", [128, 4, 128], F32), al("lc_st1", [128, 4, 128], F32)]
        xnp = [al("lc_xnp0", [128, 4, 128], F32), al("lc_xnp1", [128, 4, 128], F32)]
        for oc in range(32):
            for tc in range(2):
                sbf = (2 * oc + tc) % 2
                e_mm_group(p, yp[:, 0:512],
                           [(bd[:, oc - i, :], xT[:, i, 512 * tc:512 * (tc + 1)])
                            for i in range(oc + 1)],
                           "yp", ["bd", "xT"])
                p.op("ac", lambda e: e.copy(out=yf[:], in_=yp[:, 0:512]),
                     reads=["yp"], writes=["yf"])
                p.op("sync", lambda e, oc=oc, tc=tc, sbf=sbf: e.dma_start(
                    out=xnp[sbf][:],
                    in_=io["xn1"][512 * tc:512 * (tc + 1), 128 * oc:128 * (oc + 1)]
                    .rearrange("(pc pp) o -> pp pc o", pp=128)),
                    writes=[f"xnp{sbf}"], dma=True, stream="xp")
                for pc in range(4):
                    tb = pc % 2
                    e_transpose(p, tpf[tb][:], yf[:, 128 * pc:128 * (pc + 1)],
                                identf[:], f"tpf{tb}", ["yf", "identf"])
                    p.op("dv", lambda e, pc=pc, tb=tb, sbf=sbf: e.tensor_copy(
                        out=stg[sbf][:, pc, :], in_=tpf[tb][:]),
                        reads=[f"tpf{tb}"], writes=[f"stg{sbf}"])
                p.op("dv", lambda e, sbf=sbf: e.scalar_tensor_tensor(
                    out=stg[sbf][:], in0=xnp[sbf][:], scalar=dcol[:, 0:1],
                    in1=stg[sbf][:], op0=ALU.mult, op1=ALU.add),
                    reads=[f"stg{sbf}", f"xnp{sbf}", "dcol"], writes=[f"stg{sbf}"])
                p.op("gp", lambda e, oc=oc, tc=tc, sbf=sbf: e.dma_start(
                    out=io["a1"][512 * tc:512 * (tc + 1), 128 * oc:128 * (oc + 1)]
                    .rearrange("(pc pp) o -> pp pc o", pp=128),
                    in_=stg[sbf][:]),
                    reads=[f"stg{sbf}"], writes=["hbm_a1"], dma=True, stream="out")
        p.hard_fence()
    p.finish(["hbm_a1"])


def build_LC():
    nc = bass.Bass("TRN2", target_bir_lowering=False)
    io = {}
    def inp(name, shape):
        io[name] = nc.dram_tensor(name, list(shape), F32, kind="ExternalInput").ap()
    inp("xn1", (TOK, D))
    for nm in ("lam_re", "lam_im", "p_re", "p_im", "q_re", "q_im",
               "ct_re", "ct_im", "bp"):
        inp(nm, (64, 1))
    inp("vct_re", (64, 64)); inp("vct_im", (64, 64))
    inp("dco", (1, 1)); inp("lstep", (1, 1))
    inp("wre", (1, D)); inp("wim", (1, D))
    inp("cre", (64, 64)); inp("cim", (64, 64))
    for nm in ("e64re", "e64im", "e64imn", "e64bre", "e64bimn", "twre", "twim"):
        inp(nm, (64, 64))
    inp("ones64", (1, 64)); inp("ones128", (1, 128))
    inp("identf", (128, 128)); inp("identb", (128, 128)); inp("jrev", (128, 128))
    io["kpad"] = nc.dram_tensor("kpad", [4352], F32, kind="Internal").ap()
    io["scr4"] = nc.dram_tensor("scr4", [8, D], F32, kind="Internal").ap()
    io["scr"] = None
    io["a1"] = nc.dram_tensor("a1", [TOK, D], F32, kind="ExternalOutput").ap()
    with ExitStack() as st:
        p = Prog(nc, st)
        emit_s4(nc, p, st, io)
    return nc


def host_prep_LC(inputs):
    def ri(t):
        return np.ascontiguousarray(t[..., 0]), np.ascontiguousarray(t[..., 1])
    lam_re, lam_im = ri(inputs["L1_attn_Lambda_ri"].astype(np.float32))
    p_re, p_im = ri(inputs["L1_attn_p_ri"].astype(np.float32))
    q_re, q_im = ri(inputs["L1_attn_q_ri"].astype(np.float32))
    vc_re, vc_im = ri(inputs["L1_attn_Vc_ri"].astype(np.float32))
    ct_re, ct_im = ri(inputs["L1_attn_Ct_ri"].astype(np.float32).reshape(64, 2)
                      .reshape(1, 64, 2))
    ct_re = ct_re.reshape(64, 1); ct_im = ct_im.reshape(64, 1)
    L = D
    karr = np.arange(L)
    Om = np.exp(-2j * np.pi * (karr / L)).astype(np.complex64)
    w = ((1.0 - Om) / (1.0 + Om)).astype(np.complex64)
    cc = (2.0 / (1.0 + Om)).astype(np.complex64)
    n1 = np.arange(64)
    E64 = np.exp(2j * np.pi * np.outer(n1, n1) / 64.0)
    Tw = np.exp(2j * np.pi * np.outer(n1, n1) / 4096.0)
    shared = {
        "lam_re": lam_re.reshape(64, 1), "lam_im": lam_im.reshape(64, 1),
        "p_re": p_re.reshape(64, 1), "p_im": p_im.reshape(64, 1),
        "q_re": q_re.reshape(64, 1), "q_im": q_im.reshape(64, 1),
        "ct_re": ct_re, "ct_im": ct_im,
        "bp": inputs["L1_attn_Bp"].astype(np.float32).reshape(64, 1),
        "vct_re": np.ascontiguousarray(vc_re.T),
        "vct_im": np.ascontiguousarray(vc_im.T),
        "dco": inputs["L1_attn_D"].astype(np.float32).reshape(1, 1),
        "lstep": inputs["L1_attn_log_step"].astype(np.float32).reshape(1, 1),
        "wre": np.ascontiguousarray(w.real.astype(np.float32)).reshape(1, L),
        "wim": np.ascontiguousarray(w.imag.astype(np.float32)).reshape(1, L),
        "cre": np.ascontiguousarray(cc.real.astype(np.float32)).reshape(64, 64),
        "cim": np.ascontiguousarray(cc.imag.astype(np.float32)).reshape(64, 64),
        "e64re": E64.real.astype(np.float32).copy(),
        "e64im": E64.imag.astype(np.float32).copy(),
        "e64imn": (-E64.imag).astype(np.float32).copy(),
        "e64bre": (E64.real / 4096.0).astype(np.float32).copy(),
        "e64bimn": (-E64.imag / 4096.0).astype(np.float32).copy(),
        "twre": Tw.real.astype(np.float32).copy(),
        "twim": Tw.imag.astype(np.float32).copy(),
        "ones64": np.ones((1, 64), np.float32),
        "ones128": np.ones((1, 128), np.float32),
        "identf": np.eye(128, dtype=np.float32),
        "identb": np.eye(128, dtype=np.float32),
        "jrev": np.eye(128, dtype=np.float32)[::-1].copy(),
    }
    return shared



def emit_ffn1(nc, p, st, io):
    sb = lambda name, shape, dt: st.enter_context(nc.sbuf_tensor(name, shape, dt))
    JC = 112
    NJQ = 4
    JPQ = JC // NJQ

    identf = sb("ld_idf", [128, 128], F32)
    identb = sb("ld_idb", [128, 128], BF16)
    ones128f = sb("ld_o128", [1, 128], F32)
    p.op("sync", lambda e: e.dma_start(out=identf[:], in_=io["identf"][:, :]),
         writes=["identf"], dma=True, stream="cst")
    p.op("gp", lambda e: e.dma_start(out=identb[:], in_=io["identb"][:, :]),
         writes=["identb"], dma=True, stream="cstb")
    p.op("sync", lambda e: e.dma_start(out=ones128f[:], in_=io["ones128"][:, :]),
         writes=["o128"], dma=True, stream="cst")
    c1tb = sb("ld_c1tb", [128, 32, 16], BF16)
    c2tb = sb("ld_c2tb", [16, 14336], BF16)
    p.op("gp", lambda e: e.dma_start(out=c1tb[:], in_=io["c1t_r"][:, :, :]),
         writes=["c1tb"], dma=True, stream="cstb")
    p.op("gp", lambda e: e.dma_start(out=c2tb[:], in_=io["c2t"][:, :]),
         writes=["c2tb"], dma=True, stream="cstb")

    xTh = sb("ld_xTh", [128, 32, 512], BF16)
    hmT = sb("ld_hmT", [128, JPQ, 512], BF16)
    oacc = sb("ld_oacc", [128, 32, 512], F32)
    ztb = sb("ld_ztb", [16, 512], BF16)
    bmaxT = sb("ld_bmaxT", [64, 512], F32)
    xb = sb("ld_xb", [128, D], BF16)
    w1b = [sb("ld_w1b0", [128, 32, 128], BF16), sb("ld_w1b1", [128, 32, 128], BF16)]
    w2b = [sb("ld_w2b0", [128, 8, 128], BF16), sb("ld_w2b1", [128, 8, 128], BF16)]
    brow = [sb("ld_brow0", [1, 512], F32), sb("ld_brow1", [1, 512], F32)]
    bbs = sb("ld_bbs", [128, 512], F32)
    bbadj = sb("ld_bbadj", [128, 512], F32)
    iv = sb("ld_iv", [128, 512], F32)
    hr = sb("ld_hr", [128, 512], BF16)
    mk = sb("ld_mk", [128, 512], BF16)
    bmax = sb("ld_bmax", [128, 64], F32)
    stg = [sb("ld_st0", [128, 4, 128], F32)]
    x1p = [sb("ld_x1p0", [128, 4, 128], F32)]
    a1p = [sb("ld_a1p0", [128, 4, 128], F32)]
    bmaxd = io["bmaxd"]

    for th in range(2):
        t0 = 512 * th
        with ExitStack() as ph:
            alp = lambda name, shape, dt: ph.enter_context(nc.psum_tensor(name, shape, dt))
            tpp = [alp(f"ld_tp0_{th}", [128, 128], BF16), alp(f"ld_tp1_{th}", [128, 128], BF16)]
            for t in range(4):
                p.op("gp", lambda e, th=th, t=t: e.dma_start(
                    out=xb[:], in_=io["xn1"][512 * th + 128 * t:512 * th + 128 * (t + 1), :]),
                    writes=["xb"], dma=True, stream="x")
                for a in range(32):
                    tb = a % 2
                    e_transpose(p, tpp[tb][:], xb[:, 128 * a:128 * (a + 1)],
                                identb[:], f"tp{tb}", ["xb", "identb"])
                    p.op("dv", lambda e, a=a, t=t, tb=tb: e.tensor_copy(
                        out=xTh[:, a, 128 * t:128 * (t + 1)], in_=tpp[tb][:]),
                        reads=[f"tp{tb}"], writes=["xTh"])
            p.hard_fence()
        with ExitStack() as ph:
            alp = lambda name, shape, dt: ph.enter_context(nc.psum_tensor(name, shape, dt))
            zp = alp(f"ld_zp_{th}", [16, 512], F32)
            e_mm_group(p, zp[:], [(c1tb[:, a, :], xTh[:, a, :]) for a in range(32)],
                       "zp", ["c1tb", "xTh"])
            p.op("ac", lambda e: e.copy(out=ztb[:], in_=zp[:]),
                 reads=["zp"], writes=["ztb"])
            p.hard_fence()
        with ExitStack() as ph:
            alp = lambda name, shape, dt: ph.enter_context(nc.psum_tensor(name, shape, dt))
            ip = alp(f"ld_ip_{th}", [128, 512], F32)
            pt = alp(f"ld_pt_{th}", [64, 128], F32)
            p.op("dv", lambda e: e.memset(bmax[:, 56:64], 0.0), writes=["bmax"])
            for tt in range(4):
                for j5 in range(28):
                    e_mm_group(p, ip[:], [(ztb[:, 128 * tt:128 * (tt + 1)],
                                           c2tb[:, 512 * j5:512 * (j5 + 1)])],
                               "ip", ["ztb", "c2tb"])
                    for half in range(2):
                        p.op("dv", lambda e, j5=j5, half=half: e.reduce_max(
                            out=bmax[:, 2 * j5 + half:2 * j5 + half + 1],
                            in_=ip[:, 256 * half:256 * (half + 1)], axis=AX.X),
                            reads=["ip"], writes=["bmax"])
                e_transpose(p, pt[:], bmax[:, 0:64], identf[:], "pt",
                            ["bmax", "identf"])
                p.op("dv", lambda e, tt=tt: e.tensor_copy(
                    out=bmaxT[:, 128 * tt:128 * (tt + 1)], in_=pt[:]),
                    reads=["pt"], writes=["bmaxT"])
            p.op("gp", lambda e: e.dma_start(out=bmaxd[:, :], in_=bmaxT[:]),
                 reads=["bmaxT"], writes=["hbm_bmaxd"], dma=True, stream="bm")
            p.hard_fence()
        for jq in range(NJQ):
            with ExitStack() as ph:
                alp = lambda name, shape, dt: ph.enter_context(nc.psum_tensor(name, shape, dt))
                hp = alp(f"ld_hp_{th}_{jq}", [128, 512], F32)
                ip2 = alp(f"ld_ip2_{th}_{jq}", [128, 512], F32)
                bb = alp(f"ld_bb_{th}_{jq}", [128, 512], F32)
                for jc in range(JPQ):
                    jg = JPQ * jq + jc
                    wbi = jc % 2
                    p.op("gp", lambda e, jg=jg, wbi=wbi: e.dma_start(
                        out=w1b[wbi][:], in_=io["w1_s"][jg]),
                        writes=[f"w1b{wbi}"], dma=True, stream="w1")
                    e_mm_group(p, hp[:], [(w1b[wbi][:, a, :], xTh[:, a, :])
                                          for a in range(32)],
                               "hp", [f"w1b{wbi}", "xTh"])
                    e_mm_group(p, ip2[:], [(c2tb[:, 128 * jg:128 * (jg + 1)], ztb[:])],
                               "ip2", ["c2tb", "ztb"])
                    if jg % 2 == 0:
                        bri = (jg // 2) % 2
                        p.op("sync", lambda e, jg=jg, bri=bri: e.dma_start(
                            out=brow[bri][:], in_=bmaxd[jg // 2:jg // 2 + 1, :]),
                            reads=["hbm_bmaxd"], writes=[f"brow{bri}"],
                            dma=True, stream="br")
                        e_mm_group(p, bb[:], [(ones128f[:], brow[bri][:])],
                                   "bb", ["o128", f"brow{bri}"])
                        p.op("ac", lambda e: e.copy(out=bbs[:], in_=bb[:]),
                             reads=["bb"], writes=["bbs"])
                        p.op("dv", lambda e: e.tensor_scalar_mul(
                            out=bbadj[:], in0=bbs[:], scalar1=-1.0),
                            reads=["bbs"], writes=["bbadj"])
                        p.op("dv", lambda e: e.tensor_max(
                            out=bbadj[:], in0=bbadj[:], in1=bbs[:]),
                            reads=["bbadj", "bbs"], writes=["bbadj"])
                        p.op("dv", lambda e: e.scalar_tensor_tensor(
                            out=bbadj[:], in0=bbadj[:], scalar=-1e-5, in1=bbs[:],
                            op0=ALU.mult, op1=ALU.add),
                            reads=["bbadj", "bbs"], writes=["bbadj"])
                    p.op("ac", lambda e: e.copy(out=iv[:], in_=ip2[:]),
                         reads=["ip2"], writes=["iv"])
                    p.op("dv", lambda e: e.tensor_tensor(
                        out=mk[:], in0=iv[:], in1=bbadj[:], op=ALU.is_ge),
                        reads=["iv", "bbadj"], writes=["mk"])
                    p.op("ac", lambda e: e.activation(out=hr[:], in_=hp[:],
                                                      func=AF.Relu),
                        reads=["hp"], writes=["hr"])
                    p.op("dv", lambda e, jc=jc: e.tensor_mul(
                        out=hmT[:, jc, :], in0=hr[:], in1=mk[:]),
                        reads=["hr", "mk"], writes=["hmT"])
                p.hard_fence()
            with ExitStack() as ph:
                alp = lambda name, shape, dt: ph.enter_context(nc.psum_tensor(name, shape, dt))
                po = [alp(f"ld_po{m}_{th}_{jq}", [128, 512], F32) for m in range(8)]
                for og in range(4):
                    for jc in range(JPQ):
                        jg = JPQ * jq + jc
                        wbi = jc % 2
                        p.op("gp", lambda e, og=og, jg=jg, wbi=wbi: e.dma_start(
                            out=w2b[wbi][:], in_=io["w2t_s"][og, jg]),
                            writes=[f"w2b{wbi}"], dma=True, stream="w2")

                        def build(e, jc=jc, wbi=wbi):
                            inst = None
                            for m in range(8):
                                inst = e.matmul(po[m][:], w2b[wbi][:, m, :],
                                                hmT[:, jc, :],
                                                start=(jc == 0), stop=(jc == JPQ - 1))
                            return inst
                        p.op("pe", build, reads=[f"w2b{wbi}", "hmT"],
                             writes=[f"po{m}" for m in range(8)])
                    for m in range(8):
                        ot = 8 * og + m
                        if jq == 0:
                            p.op("dv", lambda e, m=m, ot=ot: e.tensor_copy(
                                out=oacc[:, ot, :], in_=po[m][:]),
                                reads=[f"po{m}"], writes=["oacc"])
                        else:
                            p.op("dv", lambda e, m=m, ot=ot: e.tensor_add(
                                out=oacc[:, ot, :], in0=oacc[:, ot, :], in1=po[m][:]),
                                reads=[f"po{m}", "oacc"], writes=["oacc"])
                p.hard_fence()
        with ExitStack() as ph:
            alp = lambda name, shape, dt: ph.enter_context(nc.psum_tensor(name, shape, dt))
            tpf = [alp(f"ld_tpf0_{th}", [128, 128], F32), alp(f"ld_tpf1_{th}", [128, 128], F32)]
            for ot in range(32):
                sbf = 0
                p.op("sync", lambda e, th=th, ot=ot, sbf=sbf: e.dma_start(
                    out=x1p[sbf][:],
                    in_=io["x1"][512 * th:512 * (th + 1), 128 * ot:128 * (ot + 1)]
                    .rearrange("(pc pp) o -> pp pc o", pp=128)),
                    writes=[f"x1p{sbf}"], dma=True, stream="xp"),
                p.op("sync", lambda e, th=th, ot=ot, sbf=sbf: e.dma_start(
                    out=a1p[sbf][:],
                    in_=io["a1"][512 * th:512 * (th + 1), 128 * ot:128 * (ot + 1)]
                    .rearrange("(pc pp) o -> pp pc o", pp=128)),
                    writes=[f"a1p{sbf}"], dma=True, stream="ap")
                for pc in range(4):
                    tb = pc % 2
                    e_transpose(p, tpf[tb][:], oacc[:, ot, 128 * pc:128 * (pc + 1)],
                                identf[:], f"tpf{tb}", ["oacc", "identf"])
                    p.op("dv", lambda e, pc=pc, tb=tb, sbf=sbf: e.tensor_copy(
                        out=stg[sbf][:, pc, :], in_=tpf[tb][:]),
                        reads=[f"tpf{tb}"], writes=[f"stg{sbf}"])
                p.op("dv", lambda e, sbf=sbf: e.tensor_add(
                    out=stg[sbf][:], in0=stg[sbf][:], in1=x1p[sbf][:]),
                    reads=[f"stg{sbf}", f"x1p{sbf}"], writes=[f"stg{sbf}"])
                p.op("dv", lambda e, sbf=sbf: e.tensor_add(
                    out=stg[sbf][:], in0=stg[sbf][:], in1=a1p[sbf][:]),
                    reads=[f"stg{sbf}", f"a1p{sbf}"], writes=[f"stg{sbf}"])
                p.op("gp", lambda e, th=th, ot=ot, sbf=sbf: e.dma_start(
                    out=io["out"][512 * th:512 * (th + 1), 128 * ot:128 * (ot + 1)]
                    .rearrange("(pc pp) o -> pp pc o", pp=128),
                    in_=stg[sbf][:]),
                    reads=[f"stg{sbf}"], writes=["hbm_out"], dma=True, stream="out")
            p.hard_fence()
    p.finish(["hbm_out"])


def build_LD():
    nc = bass.Bass("TRN2", target_bir_lowering=False)
    io = {}
    def inp(name, shape):
        io[name] = nc.dram_tensor(name, list(shape), F32, kind="ExternalInput").ap()
    inp("xn1", (TOK, D)); inp("x1", (TOK, D)); inp("a1", (TOK, D))
    inp("c1t_r", (128, 32, 16)); inp("c2t", (16, 14336))
    inp("w1_s", (112, 128, 32, 128))
    inp("w2t_s", (4, 112, 128, 8, 128))
    inp("identf", (128, 128)); inp("identb", (128, 128)); inp("ones128", (1, 128))
    io["bmaxd"] = nc.dram_tensor("bmaxd", [64, 512], F32, kind="Internal").ap()
    io["out"] = nc.dram_tensor("out", [TOK, D], F32, kind="ExternalOutput").ap()
    with ExitStack() as st:
        p = Prog(nc, st)
        emit_ffn1(nc, p, st, io)
    return nc


def host_prep_LD(inputs):
    c1 = inputs["L1_ffn_c1_w"].astype(np.float32)
    c2 = inputs["L1_ffn_c2_w"].astype(np.float32)
    w1 = inputs["L1_ffn_w1"].astype(np.float32)
    w2t = inputs["L1_ffn_w2t"].astype(np.float32)
    shared = {
        "c1t_r": np.ascontiguousarray(c1.T.reshape(32, 128, 16).transpose(1, 0, 2)),
        "c2t": np.ascontiguousarray(c2.T),
        "w1_s": stage_stream(w1),
        "w2t_s": np.ascontiguousarray(
            w2t.reshape(112, 128, 4, 8, 128).transpose(2, 0, 1, 3, 4)),
        "identf": np.eye(128, dtype=np.float32),
        "identb": np.eye(128, dtype=np.float32),
        "ones128": np.ones((1, 128), np.float32),
    }
    return shared



def kernel_multilaunch(**inputs):
    x = np.ascontiguousarray(np.asarray(inputs["L0_attn_x"], np.float32)
                             .reshape(S, D))
    shA, pcA = host_prep_LA(inputs)
    shB = host_prep_LB(inputs)
    shC = host_prep_LC(inputs)
    shD = host_prep_LD(inputs)
    core_ids = list(range(NCORE))

    ncA = build_LA()
    in_maps = [dict(shA, **pcA[c]) for c in range(NCORE)]
    resA = run_bass_kernel_spmd(ncA, in_maps, core_ids=core_ids)
    a0 = [np.ascontiguousarray(np.asarray(resA.results[c]["a0"], np.float32))
          for c in range(NCORE)]

    ncB = build_LB()
    in_maps = [dict(shB, x0o=np.ascontiguousarray(x[TOK * c:TOK * (c + 1)]),
                    a0=a0[c]) for c in range(NCORE)]
    resB = run_bass_kernel_spmd(ncB, in_maps, core_ids=core_ids)
    x1 = [np.ascontiguousarray(np.asarray(resB.results[c]["x1"], np.float32))
          for c in range(NCORE)]
    xn1 = [np.ascontiguousarray(np.asarray(resB.results[c]["xn1"], np.float32))
           for c in range(NCORE)]

    ncC = build_LC()
    in_maps = [dict(shC, xn1=xn1[c]) for c in range(NCORE)]
    resC = run_bass_kernel_spmd(ncC, in_maps, core_ids=core_ids)
    a1 = [np.ascontiguousarray(np.asarray(resC.results[c]["a1"], np.float32))
          for c in range(NCORE)]

    ncD = build_LD()
    in_maps = [dict(shD, xn1=xn1[c], x1=x1[c], a1=a1[c]) for c in range(NCORE)]
    resD = run_bass_kernel_spmd(ncD, in_maps, core_ids=core_ids)
    out = np.concatenate([np.asarray(resD.results[c]["out"], np.float32)
                          for c in range(NCORE)], axis=0)
    return np.ascontiguousarray(out.reshape(1, S, D))



def build_fused():
    nc = bass.Bass("TRN2", target_bir_lowering=False)
    io = {}
    def inp(name, shape):
        io[name] = nc.dram_tensor(name, list(shape), F32, kind="ExternalInput").ap()
    inp("x0h", (HALO, D)); inp("kmask", (128, HALO // 128))
    inp("cosq", (128, TOK)); inp("sinq", (128, TOK))
    inp("cosk", (128, HALO)); inp("sink", (128, HALO))
    inp("alph", (128, 1024))
    inp("wq_s", (32, 128, 32, 128)); inp("wk_s", (1, 128, 32, 128))
    inp("wv_r", (128, 32, 128)); inp("wo_s", (32, 128, 32, 128))
    inp("onesb", (128, 128))
    inp("x0o", (TOK, D)); inp("w_s", (32, 128, 32, 128))
    inp("wqf_r", (128, 32, 256)); inp("wkf_r", (128, 32, 256))
    inp("af_r", (128, 32, 128)); inp("bmr", (128, D)); inp("bcol", (128, 32))
    for nm in ("lam_re", "lam_im", "p_re", "p_im", "q_re", "q_im",
               "ct_re", "ct_im", "bp"):
        inp(nm, (64, 1))
    inp("vct_re", (64, 64)); inp("vct_im", (64, 64))
    inp("dco", (1, 1)); inp("lstep", (1, 1))
    inp("wre", (1, D)); inp("wim", (1, D))
    inp("cre", (64, 64)); inp("cim", (64, 64))
    for nm in ("e64re", "e64im", "e64imn", "e64bre", "e64bimn", "twre", "twim"):
        inp(nm, (64, 64))
    inp("ones64", (1, 64)); inp("ones128", (1, 128)); inp("jrev", (128, 128))
    inp("c1t_r", (128, 32, 16)); inp("c2t", (16, 14336))
    inp("w1_s", (112, 128, 32, 128)); inp("w2t_s", (4, 112, 128, 8, 128))
    inp("identf", (128, 128)); inp("identb", (128, 128))
    for nm in ("a0", "x1", "xn1", "a1"):
        io[nm] = nc.dram_tensor(nm, [TOK, D], F32, kind="Internal").ap()
    io["kpad"] = nc.dram_tensor("kpad", [4352], F32, kind="Internal").ap()
    io["scr4"] = nc.dram_tensor("scr4", [8, D], F32, kind="Internal").ap()
    io["scr"] = None
    io["bmaxd"] = nc.dram_tensor("bmaxd", [64, 512], F32, kind="Internal").ap()
    io["out"] = nc.dram_tensor("out", [TOK, D], F32, kind="ExternalOutput").ap()
    with ExitStack() as st:
        p = Prog(nc, st)
        with ExitStack() as s1:
            emit_attn0(nc, p, s1, io)
            p.hard_fence()
        with ExitStack() as s2:
            emit_ffn0(nc, p, s2, io)
            p.hard_fence()
        with ExitStack() as s3:
            emit_s4(nc, p, s3, io)
            p.hard_fence()
        with ExitStack() as s4:
            emit_ffn1(nc, p, s4, io)
            p.hard_fence()
        p.finish(["hbm_out"])
    return nc


def kernel_fused(**inputs):
    x = np.ascontiguousarray(np.asarray(inputs["L0_attn_x"], np.float32)
                             .reshape(S, D))
    shA, pcA = host_prep_LA(inputs)
    shB = host_prep_LB(inputs)
    shC = host_prep_LC(inputs)
    shD = host_prep_LD(inputs)
    nc = build_fused()
    in_maps = []
    for c in range(NCORE):
        m = dict(shA)
        m.update(pcA[c])
        m.update(shB)
        m.update(shC)
        m.update(shD)
        m["x0o"] = np.ascontiguousarray(x[TOK * c:TOK * (c + 1)])
        in_maps.append(m)
    res = run_bass_kernel_spmd(nc, in_maps, core_ids=list(range(NCORE)))
    out = np.concatenate([np.asarray(res.results[c]["out"], np.float32)
                          for c in range(NCORE)], axis=0)
    return np.ascontiguousarray(out.reshape(1, S, D))


def kernel(**inputs):
    return kernel_fused(**inputs)


# version 04 (final)
# speedup vs baseline: 1.0984x; 1.0984x over previous
import numpy as np
import concourse.bass as bass
import concourse.mybir as mybir
from concourse.bass_utils import run_bass_kernel_spmd
from contextlib import ExitStack

F32 = mybir.dt.float32
BF16 = mybir.dt.bfloat16
AF = mybir.ActivationFunctionType
ALU = mybir.AluOpType
AX = mybir.AxisListType

NCORE = 8
S = 8192
D = 4096
TOK = S // NCORE
HALO = TOK + 512
NH = 32
DH = 128
WIN = 512
SCALE = 1.0 / float(np.sqrt(DH))
EPS = 1e-5


class Prog:

    ENGS = ("sync", "gp", "pe", "dv", "ac")

    def __init__(self, nc, stack):
        self.nc = nc
        self.stack = stack
        self.sem = {}
        self.cnt = {}
        self.waited = {e: {} for e in self.ENGS}
        for e in ("pe", "dv", "ac"):
            self.sem[e] = stack.enter_context(nc.semaphore(f"S_{e}"))
            self.cnt[e] = 0
        self.eng = {"sync": nc.sync, "gp": nc.gpsimd, "pe": nc.tensor,
                    "dv": nc.vector, "ac": nc.scalar}
        self.state = {}

    def _stream_sem(self, key):
        if key not in self.sem:
            self.sem[key] = self.stack.enter_context(
                self.nc.semaphore(f"S_{key.replace(':', '_')}"))
            self.cnt[key] = 0
        return self.sem[key]

    def _wait(self, e, semkey, val):
        if val > self.waited[e].get(semkey, 0):
            self.eng[e].wait_ge(self.sem[semkey], val)
            self.waited[e][semkey] = val

    def op(self, e, build, reads=(), writes=(), dma=False, stream=None):
        need = {}

        def add(dep):
            if dep is None:
                return
            dk, dc = dep
            if dc > need.get(dk, 0):
                need[dk] = dc

        for k in reads:
            st = self.state.get(k)
            if st:
                add(st[0])
        for k in writes:
            st = self.state.get(k)
            if st:
                add(st[0])
                for rk, rc in st[1].items():
                    add((rk, rc))
        if dma:
            semkey = f"{e}:{stream or 'main'}"
            self._stream_sem(semkey)
            add((semkey, self.cnt[semkey]))
        else:
            semkey = e
        for dk, dc in sorted(need.items()):
            self._wait(e, dk, dc)
        inst = build(self.eng[e])
        inc = 16 if dma else 1
        inst.then_inc(self.sem[semkey], inc)
        self.cnt[semkey] += inc
        me = (semkey, self.cnt[semkey])
        for k in reads:
            st = self.state.setdefault(k, [None, {}])
            st[1][semkey] = self.cnt[semkey]
        for k in writes:
            self.state[k] = [me, {}]
        return inst

    def hard_fence(self):
        for e in self.ENGS:
            for sk, c in list(self.cnt.items()):
                if c > 0:
                    self._wait(e, sk, c)

    def finish(self, out_keys):
        need = {}
        for k in out_keys:
            st = self.state.get(k)
            if st and st[0]:
                dk, dc = st[0]
                need[dk] = max(need.get(dk, 0), dc)
        for dk, dc in sorted(need.items()):
            self._wait("sync", dk, dc)



def e_ln(p, x_f32, xn_f32, tmp_col, key_x, key_xn, key_tmp, n=D, eps_col=None):
    p.op("dv", lambda e: e.reduce_sum(out=tmp_col[:, 0:1], in_=x_f32[:], axis=AX.X),
         reads=[key_x], writes=[key_tmp])
    p.op("dv", lambda e: e.tensor_scalar_mul(out=tmp_col[:, 0:1], in0=tmp_col[:, 0:1],
                                             scalar1=1.0 / n),
         reads=[key_tmp], writes=[key_tmp])
    p.op("dv", lambda e: e.tensor_scalar(out=xn_f32[:], in0=x_f32[:],
                                         scalar1=tmp_col[:, 0:1], scalar2=None,
                                         op0=ALU.subtract),
         reads=[key_x, key_tmp], writes=[key_xn])
    p.op("ac", lambda e: e.activation(out=x_f32[:], in_=xn_f32[:], func=AF.Square),
         reads=[key_xn], writes=[key_x])
    p.op("dv", lambda e: e.reduce_sum(out=tmp_col[:, 1:2], in_=x_f32[:], axis=AX.X),
         reads=[key_x], writes=[key_tmp])
    p.op("ac", lambda e: e.activation(out=tmp_col[:, 1:2], in_=tmp_col[:, 1:2],
                                      func=AF.Sqrt, scale=1.0 / n,
                                      bias=eps_col),
         reads=[key_tmp, "epscol"], writes=[key_tmp])
    p.op("dv", lambda e: e.reciprocal(out=tmp_col[:, 1:2], in_=tmp_col[:, 1:2]),
         reads=[key_tmp], writes=[key_tmp])
    p.op("dv", lambda e: e.tensor_scalar(out=xn_f32[:], in0=xn_f32[:],
                                         scalar1=tmp_col[:, 1:2], scalar2=None,
                                         op0=ALU.mult),
         reads=[key_xn, key_tmp], writes=[key_xn])


def e_mm_group(p, psum_ap, steps, key_psum, read_keys):
    n = len(steps)

    def build(e):
        inst = None
        for i, (lh, rh) in enumerate(steps):
            inst = e.matmul(psum_ap, lh, rh, start=(i == 0), stop=(i == n - 1))
        return inst

    return p.op("pe", build, reads=list(read_keys), writes=[key_psum])


def e_transpose(p, out_psum, in_sbuf, ident, key_out, read_keys):
    return p.op("pe", lambda e: e.transpose(out_psum, in_sbuf, ident),
                reads=list(read_keys), writes=[key_out])



def stage_stream(W):
    O, K = W.shape
    return np.ascontiguousarray(
        W.reshape(O // 128, 128, K // 128, 128).transpose(0, 3, 2, 1))


def stage_rhs(WT, m):
    K, M = WT.shape
    assert M == m
    return np.ascontiguousarray(WT.reshape(K // 128, 128, M).transpose(1, 0, 2))


def rope_cache_T(S_, Dh):
    inv_freq = 1.0 / (10000.0 ** (np.arange(0, Dh, 2, dtype=np.float32) / Dh))
    t = np.arange(S_, dtype=np.float32)
    f = np.outer(t, inv_freq)
    cos = np.zeros((S_, Dh), dtype=np.float32)
    sin = np.zeros((S_, Dh), dtype=np.float32)
    cos[:, 0::2] = np.cos(f)
    cos[:, 1::2] = np.cos(f)
    sin[:, 0::2] = np.sin(f)
    sin[:, 1::2] = np.sin(f)
    return np.ascontiguousarray(cos.T), np.ascontiguousarray(sin.T)



def e_rope(p, src, cos, sin, rt, t1, out_bf, keys, n, off=0):
    ksrc, kcos, ksin, krt, kt1, kout = keys
    p.op("dv", lambda e: e.tensor_scalar_mul(out=rt[0:64, 0:n],
                                             in0=src[64:128, off:off + n],
                                             scalar1=-1.0),
         reads=[ksrc], writes=[krt])
    p.op("ac", lambda e: e.copy(out=rt[64:128, 0:n], in_=src[0:64, off:off + n]),
         reads=[ksrc], writes=[krt])
    p.op("dv", lambda e: e.tensor_mul(out=t1[:, 0:n], in0=src[:, off:off + n],
                                      in1=cos[:, off:off + n]),
         reads=[ksrc, kcos], writes=[kt1])
    p.op("dv", lambda e: e.tensor_mul(out=rt[:, 0:n], in0=rt[:, 0:n],
                                      in1=sin[:, off:off + n]),
         reads=[krt, ksin], writes=[krt])
    p.op("dv", lambda e: e.tensor_add(out=out_bf[:, off:off + n], in0=t1[:, 0:n],
                                      in1=rt[:, 0:n]),
         reads=[kt1, krt], writes=[kout])


def emit_attn0(nc, p, st, io):
    sb = lambda name, shape, dt: st.enter_context(nc.sbuf_tensor(name, shape, dt))
    NT = HALO // 128

    identf = sb("la_idf", [128, 128], F32)
    identb = sb("la_idb", [128, 128], BF16)
    onesb = sb("la_ones", [128, 128], BF16)
    kmask = sb("la_kmask", [128, NT], F32)
    for nm, t in (("identf", identf), ("kmask", kmask)):
        p.op("sync", lambda e, nm=nm, t=t: e.dma_start(out=t[:], in_=io[nm][:, :]),
             writes=[nm], dma=True, stream="cst")
    for nm, t in (("identb", identb), ("onesb", onesb)):
        p.op("gp", lambda e, nm=nm, t=t: e.dma_start(out=t[:], in_=io[nm][:, :]),
             writes=[nm], dma=True, stream="cstb")

    krT = sb("la_krT", [128, HALO], BF16)
    v_s = sb("la_v", [128, NT, 128], BF16)
    blend = sb("la_blend", [128, NH, TOK], BF16)
    xnT = sb("la_xnT", [128, 32, HALO], BF16)

    with ExitStack() as ph:
        al = lambda name, shape, dt: ph.enter_context(nc.sbuf_tensor(name, shape, dt))
        alp = lambda name, shape, dt: ph.enter_context(nc.psum_tensor(name, shape, dt))
        x0b = [al("la_x00", [128, D], F32)]
        xnb = [al("la_xn0", [128, D], F32)]
        xbfb = [al("la_xb0", [128, D], BF16)]
        coltmp = al("la_col", [128, 3], F32)
        p.op("dv", lambda e: e.memset(coltmp[:, 2:3], EPS), writes=["epscol"])
        tpp = [alp("la_tp0", [128, 128], BF16), alp("la_tp1", [128, 128], BF16)]
        for t in range(NT):
            b = 0
            p.op("sync", lambda e, t=t, b=b: e.dma_start(
                out=x0b[b][:], in_=io["x0h"][128 * t:128 * (t + 1), :]),
                writes=[f"x0b{b}"], dma=True, stream="x")
            e_ln(p, x0b[b][:], xnb[b][:], coltmp, f"x0b{b}", f"xnb{b}", "coltmp",
                 eps_col=coltmp[:, 2:3])
            p.op("ac", lambda e, b=b: e.copy(out=xbfb[b][:], in_=xnb[b][:]),
                 reads=[f"xnb{b}"], writes=[f"xbf{b}"])
            for a in range(32):
                tb = a % 2
                e_transpose(p, tpp[tb][:], xbfb[b][:, 128 * a:128 * (a + 1)],
                            identb[:], f"tp{tb}", [f"xbf{b}", "identb"])
                if a % 2:
                    p.op("dv", lambda e, a=a, t=t, tb=tb: e.tensor_copy(
                        out=xnT[:, a, 128 * t:128 * (t + 1)], in_=tpp[tb][:]),
                        reads=[f"tp{tb}"], writes=["xnT"])
                else:
                    p.op("ac", lambda e, a=a, t=t, tb=tb: e.copy(
                        out=xnT[:, a, 128 * t:128 * (t + 1)], in_=tpp[tb][:]),
                        reads=[f"tp{tb}"], writes=["xnT"])
        p.hard_fence()

    with ExitStack() as ph:
        al = lambda name, shape, dt: ph.enter_context(nc.sbuf_tensor(name, shape, dt))
        alp = lambda name, shape, dt: ph.enter_context(nc.psum_tensor(name, shape, dt))
        cosk = al("la_cosk", [128, HALO], F32)
        sink = al("la_sink", [128, HALO], F32)
        wkb = al("la_wk0b", [128, 32, 128], BF16)
        wvb = al("la_wv0b", [128, 32, 128], BF16)
        kps = alp("la_kps", [128, 512], F32)
        ktf = al("la_ktf", [128, HALO], F32)
        rt_t = al("la_rt", [128, 512], F32)
        t1_t = al("la_t1", [128, 512], F32)
        vps = alp("la_vps", [128, 128], F32)
        p.op("sync", lambda e: e.dma_start(out=cosk[:], in_=io["cosk"][:, :]),
             writes=["cosk"], dma=True, stream="cst")
        p.op("sync", lambda e: e.dma_start(out=sink[:], in_=io["sink"][:, :]),
             writes=["sink"], dma=True, stream="cst")
        p.op("gp", lambda e: e.dma_start(out=wkb[:], in_=io["wk_s"][0]),
             writes=["wkb"], dma=True, stream="w")
        p.op("gp", lambda e: e.dma_start(out=wvb[:], in_=io["wv_r"][:, :, :]),
             writes=["wvb"], dma=True, stream="w")
        for c in range(3):
            e_mm_group(p, kps[:, 0:512],
                       [(wkb[:, a, :], xnT[:, a, 512 * c:512 * (c + 1)])
                        for a in range(32)],
                       "kps", ["wkb", "xnT"])
            p.op("ac", lambda e, c=c: e.copy(out=ktf[:, 512 * c:512 * (c + 1)],
                                             in_=kps[:, 0:512]),
                 reads=["kps"], writes=["ktf"])
        for c in range(3):
            e_rope(p, ktf, cosk, sink, rt_t, t1_t, krT,
                   ("ktf", "cosk", "sink", "rt_t", "t1_t", "krT"), 512, off=512 * c)
        for t in range(NT):
            e_mm_group(p, vps[:],
                       [(xnT[:, a, 128 * t:128 * (t + 1)], wvb[:, a, :])
                        for a in range(32)],
                       "vps", ["xnT", "wvb"])
            p.op("ac", lambda e, t=t: e.copy(out=v_s[:, t, :], in_=vps[:]),
                 reads=["vps"], writes=["v_s"])
        p.hard_fence()

    with ExitStack() as ph:
        al = lambda name, shape, dt: ph.enter_context(nc.sbuf_tensor(name, shape, dt))
        alp = lambda name, shape, dt: ph.enter_context(nc.psum_tensor(name, shape, dt))
        cosq = al("la_cosq", [128, TOK], F32)
        sinq = al("la_sinq", [128, TOK], F32)
        alph = al("la_alph", [128, 1024], F32)
        wqb = [al("la_wqb0", [128, 32, 128], BF16)]
        qps = alp("la_qps", [128, 512], F32)
        qf = al("la_qf", [128, TOK], F32)
        qrT = al("la_qrT", [128, TOK], BF16)
        rt_t = al("la_rt2", [128, 512], F32)
        t1_t = al("la_t12", [128, 512], F32)
        scp = [alp("la_sc0", [128, 512], F32), alp("la_sc1", [128, 512], F32)]
        sig = al("la_sig", [128, 4, 512], BF16)
        rsp_ = [alp("la_rsp0", [128, 512], F32), alp("la_rsp1", [128, 512], F32)]
        wop_ = [alp("la_wop0", [128, 512], F32), alp("la_wop1", [128, 512], F32)]
        rrec = al("la_rrec", [128, 512], F32)
        wout = [al("la_wo0", [128, 512], F32), al("la_wo1", [128, 512], F32)]
        for nm, t in (("cosq", cosq), ("sinq", sinq), ("alph", alph)):
            p.op("sync", lambda e, nm=nm, t=t: e.dma_start(out=t[:], in_=io[nm][:, :]),
                 writes=[nm], dma=True, stream="cst")
        for h in range(NH):
            wb = 0
            p.op("gp", lambda e, h=h, wb=wb: e.dma_start(out=wqb[wb][:], in_=io["wq_s"][h]),
                 writes=[f"wqb{wb}"], dma=True, stream="w")
            for c in range(2):
                e_mm_group(p, qps[:, 0:512],
                           [(wqb[wb][:, a, :], xnT[:, a, 256 + 512 * c:256 + 512 * (c + 1)])
                            for a in range(32)],
                           "qps", [f"wqb{wb}", "xnT"])
                p.op("ac", lambda e, c=c: e.copy(out=qf[:, 512 * c:512 * (c + 1)],
                                                 in_=qps[:, 0:512]),
                     reads=["qps"], writes=["qf"])
            for c in range(2):
                e_rope(p, qf, cosq, sinq, rt_t, t1_t, qrT,
                       ("qf", "cosq", "sinq", "rt_t", "t1_t", "qrT"), 512, off=512 * c)
            for r in range(5):
                q0 = max(0, 256 * (r - 1))
                qlen = 256 if r in (0, 4) else 512
                rw = r % 2
                for kc in range(4):
                    b = kc % 2
                    e_mm_group(p, scp[b][:, 0:qlen],
                               [(krT[:, 256 * r + 128 * kc:256 * r + 128 * (kc + 1)],
                                 qrT[:, q0:q0 + qlen])],
                               f"scp{b}", ["krT", "qrT"])
                    p.op("ac", lambda e, r=r, kc=kc, b=b, qlen=qlen: e.activation(
                        out=sig[:, kc, 0:qlen], in_=scp[b][:, 0:qlen],
                        func=AF.Sigmoid,
                        bias=kmask[:, 2 * r + kc:2 * r + kc + 1], scale=SCALE),
                        reads=[f"scp{b}", "kmask"], writes=[f"sig{kc}"])
                rsp = rsp_[rw]
                wop = wop_[rw]
                e_mm_group(p, rsp[:, 0:qlen],
                           [(onesb[:], sig[:, kc, 0:qlen]) for kc in range(4)],
                           f"rsp{rw}", ["onesb", "sig0", "sig1", "sig2", "sig3"])
                e_mm_group(p, wop[:, 0:qlen],
                           [(v_s[:, 2 * r + kc, :], sig[:, kc, 0:qlen])
                            for kc in range(4)],
                           f"wop{rw}", ["v_s", "sig0", "sig1", "sig2", "sig3"])
                p.op("dv", lambda e, rsp=rsp, qlen=qlen: e.tensor_scalar_max(
                    out=rrec[:, 0:qlen], in0=rsp[:, 0:qlen], scalar1=1e-30),
                    reads=[f"rsp{rw}"], writes=["rrec"])
                p.op("dv", lambda e, qlen=qlen: e.reciprocal(
                    out=rrec[:, 0:qlen], in_=rrec[:, 0:qlen]),
                    reads=["rrec"], writes=["rrec"])
                p.op("dv", lambda e, rw=rw, wop=wop, qlen=qlen: e.tensor_mul(
                    out=wout[rw][:, 0:qlen], in0=wop[:, 0:qlen], in1=rrec[:, 0:qlen]),
                    reads=[f"wop{rw}", "rrec"], writes=[f"wout{rw}"])
                if r >= 1:
                    j = r - 1
                    offp = 0 if j == 0 else 256
                    pw = (r - 1) % 2
                    p.op("dv", lambda e, rw=rw, pw=pw, offp=offp: e.tensor_sub(
                        out=t1_t[:, 0:256], in0=wout[rw][:, 0:256],
                        in1=wout[pw][:, offp:offp + 256]),
                        reads=[f"wout{rw}", f"wout{pw}"], writes=["t1_t"])
                    p.op("dv", lambda e, j=j: e.tensor_mul(
                        out=t1_t[:, 0:256], in0=t1_t[:, 0:256],
                        in1=alph[:, 256 * j:256 * (j + 1)]),
                        reads=["t1_t", "alph"], writes=["t1_t"])
                    p.op("dv", lambda e, h=h, j=j, pw=pw, offp=offp: e.tensor_add(
                        out=blend[:, h, 256 * j:256 * (j + 1)], in0=t1_t[:, 0:256],
                        in1=wout[pw][:, offp:offp + 256]),
                        reads=["t1_t", f"wout{pw}"], writes=["blend"])
        p.hard_fence()

    with ExitStack() as ph:
        al = lambda name, shape, dt: ph.enter_context(nc.sbuf_tensor(name, shape, dt))
        alp = lambda name, shape, dt: ph.enter_context(nc.psum_tensor(name, shape, dt))
        wob = [al("la_woAb", [128, 32, 128], BF16), al("la_woBb", [128, 32, 128], BF16)]
        aop = alp("la_aop", [128, 512], F32)
        aof = al("la_aof", [128, 512], F32)
        tpf = [alp("la_tpf0", [128, 128], F32), alp("la_tpf1", [128, 128], F32)]
        stage = al("la_stage", [128, 4, 128], F32)
        for oc in range(32):
            b = oc % 2
            p.op("gp", lambda e, oc=oc, b=b: e.dma_start(
                out=wob[b][:], in_=io["wo_s"][oc]),
                writes=[f"wob{b}"], dma=True, stream=f"wo{b}")
            for qc in range(2):
                e_mm_group(p, aop[:, 0:512],
                           [(wob[b][:, a, :], blend[:, a, 512 * qc:512 * (qc + 1)])
                            for a in range(32)],
                           "aop", [f"wob{b}", "blend"])
                p.op("ac", lambda e: e.copy(out=aof[:], in_=aop[:, 0:512]),
                     reads=["aop"], writes=["aof"])
                for pc in range(4):
                    tb = pc % 2
                    e_transpose(p, tpf[tb][:], aof[:, 128 * pc:128 * (pc + 1)],
                                identf[:], f"tpf{tb}", ["aof", "identf"])
                    p.op("dv", lambda e, pc=pc, tb=tb: e.tensor_copy(
                        out=stage[:, pc, :], in_=tpf[tb][:]),
                        reads=[f"tpf{tb}"], writes=["stage"])
                p.op("gp", lambda e, oc=oc, qc=qc: e.dma_start(
                    out=io["a0"][512 * qc:512 * (qc + 1), 128 * oc:128 * (oc + 1)]
                    .rearrange("(pc pp) o -> pp pc o", pp=128),
                    in_=stage[:]),
                    reads=["stage"], writes=["hbm_a0"], dma=True, stream="out")
        p.hard_fence()
    p.finish(["hbm_a0"])



def build_LA():
    nc = bass.Bass("TRN2", target_bir_lowering=False)
    io = {}
    def inp(name, shape):
        io[name] = nc.dram_tensor(name, list(shape), F32, kind="ExternalInput").ap()
    inp("x0h", (HALO, D))
    inp("kmask", (128, HALO // 128))
    inp("cosq", (128, TOK)); inp("sinq", (128, TOK))
    inp("cosk", (128, HALO)); inp("sink", (128, HALO))
    inp("alph", (128, 1024)); inp("omal", (128, 1024))
    inp("wq_s", (32, 128, 32, 128))
    inp("wk_s", (1, 128, 32, 128))
    inp("wv_r", (128, 32, 128))
    inp("wo_s", (32, 128, 32, 128))
    inp("identf", (128, 128)); inp("onesb", (128, 128)); inp("identb", (128, 128))
    io["a0"] = nc.dram_tensor("a0", [TOK, D], F32, kind="ExternalOutput").ap()
    with ExitStack() as st:
        p = Prog(nc, st)
        emit_attn0(nc, p, st, io)
    return nc


def host_prep_LA(inputs):
    x = np.ascontiguousarray(inputs["L0_attn_x"].reshape(S, D).astype(np.float32))
    wq = inputs["L0_attn_Wq"].astype(np.float32)
    wk = inputs["L0_attn_Wk"].astype(np.float32)
    wv = inputs["L0_attn_Wv"].astype(np.float32)
    wo = inputs["L0_attn_Wo"].astype(np.float32)
    cosT, sinT = rope_cache_T(S, DH)
    cosP = np.pad(cosT, ((0, 0), (256, 256)))
    sinP = np.pad(sinT, ((0, 0), (256, 256)))
    xP = np.pad(x, ((256, 256), (0, 0)))
    shared = {
        "wq_s": stage_stream(wq),
        "wk_s": stage_stream(wk).reshape(1, 128, 32, 128),
        "wv_r": stage_rhs(np.ascontiguousarray(wv.T), 128),
        "wo_s": stage_stream(wo),
        "identf": np.eye(128, dtype=np.float32),
        "identb": np.eye(128, dtype=np.float32),
        "onesb": np.ones((128, 128), dtype=np.float32),
    }
    alpha = (np.arange(256, dtype=np.float32) / 255.0)
    percore = []
    for c in range(NCORE):
        h0 = 1024 * c
        kmask = np.zeros((128, HALO // 128), dtype=np.float32)
        for t in range(HALO // 128):
            rows = h0 - 256 + 128 * t + np.arange(128)
            kmask[:, t] = np.where((rows >= 0) & (rows < S), 0.0, -30000.0)
        al = np.tile(alpha, (128, 4)).reshape(128, 1024)
        om = 1.0 - al
        if c == 0:
            al[:, 0:256] = 1.0
            om[:, 0:256] = 0.0
        percore.append({
            "x0h": np.ascontiguousarray(xP[h0:h0 + HALO]),
            "kmask": kmask,
            "cosq": np.ascontiguousarray(cosP[:, h0 + 256:h0 + 256 + TOK]),
            "sinq": np.ascontiguousarray(sinP[:, h0 + 256:h0 + 256 + TOK]),
            "cosk": np.ascontiguousarray(cosP[:, h0:h0 + HALO]),
            "sink": np.ascontiguousarray(sinP[:, h0:h0 + HALO]),
            "alph": np.ascontiguousarray(al),
            "omal": np.ascontiguousarray(om),
        })
    return shared, percore



def emit_ffn0(nc, p, st, io):
    sb = lambda name, shape, dt: st.enter_context(nc.sbuf_tensor(name, shape, dt))
    NT8 = TOK // 128

    identb = sb("lb_idb", [128, 128], BF16)
    identf = sb("lb_idf", [128, 128], F32)
    p.op("gp", lambda e: e.dma_start(out=identb[:], in_=io["identb"][:, :]),
         writes=["identb"], dma=True, stream="cstb")
    p.op("sync", lambda e: e.dma_start(out=identf[:], in_=io["identf"][:, :]),
         writes=["identf"], dma=True, stream="cst")
    xnT = sb("lb_xnT", [128, 32, TOK], BF16)
    whT = sb("lb_whT", [128, NT8, 128], BF16)
    bcol = sb("lb_bcol", [128, 32], F32)
    p.op("sync", lambda e: e.dma_start(out=bcol[:], in_=io["bcol"][:, :]),
         writes=["bcol"], dma=True, stream="cst")

    with ExitStack() as ph:
        al = lambda name, shape, dt: ph.enter_context(nc.sbuf_tensor(name, shape, dt))
        alp = lambda name, shape, dt: ph.enter_context(nc.psum_tensor(name, shape, dt))
        x0 = al("lb_x0", [128, D], F32)
        xn = al("lb_xn", [128, D], F32)
        xb = al("lb_xb", [128, D], BF16)
        coltmp = al("lb_col", [128, 3], F32)
        p.op("dv", lambda e: e.memset(coltmp[:, 2:3], EPS), writes=["epscol"])
        tpp = [alp("lb_tp0", [128, 128], BF16), alp("lb_tp1", [128, 128], BF16)]
        for t in range(NT8):
            p.op("sync", lambda e, t=t: e.dma_start(
                out=x0[:], in_=io["x0o"][128 * t:128 * (t + 1), :]),
                writes=["x0b0"], dma=True, stream="x")
            e_ln(p, x0[:], xn[:], coltmp, "x0b0", "xnb0", "coltmp",
                 eps_col=coltmp[:, 2:3])
            p.op("ac", lambda e: e.copy(out=xb[:], in_=xn[:]),
                 reads=["xnb0"], writes=["xbf0"])
            for a in range(32):
                tb = a % 2
                e_transpose(p, tpp[tb][:], xb[:, 128 * a:128 * (a + 1)],
                            identb[:], f"tp{tb}", ["xbf0", "identb"])
                if a % 2:
                    p.op("dv", lambda e, a=a, t=t, tb=tb: e.tensor_copy(
                        out=xnT[:, a, 128 * t:128 * (t + 1)], in_=tpp[tb][:]),
                        reads=[f"tp{tb}"], writes=["xnT"])
                else:
                    p.op("ac", lambda e, a=a, t=t, tb=tb: e.copy(
                        out=xnT[:, a, 128 * t:128 * (t + 1)], in_=tpp[tb][:]),
                        reads=[f"tp{tb}"], writes=["xnT"])
        p.hard_fence()

    with ExitStack() as ph:
        al = lambda name, shape, dt: ph.enter_context(nc.sbuf_tensor(name, shape, dt))
        alp = lambda name, shape, dt: ph.enter_context(nc.psum_tensor(name, shape, dt))
        wqfb = al("lb_wqfb", [128, 32, 256], BF16)
        wkfb = al("lb_wkfb", [128, 32, 256], BF16)
        afb = al("lb_afb", [128, 32, 128], BF16)
        p.op("gp", lambda e: e.dma_start(out=wqfb[:], in_=io["wqf_r"][:, :, :]),
             writes=["wqfb"], dma=True, stream="w")
        p.op("gp", lambda e: e.dma_start(out=wkfb[:], in_=io["wkf_r"][:, :, :]),
             writes=["wkfb"], dma=True, stream="w")
        p.op("gp", lambda e: e.dma_start(out=afb[:], in_=io["af_r"][:, :, :]),
             writes=["afb"], dma=True, stream="w")
        qp = alp("lb_qp", [128, 256], F32)
        kp = alp("lb_kp", [128, 256], F32)
        hp = alp("lb_hp", [128, 128], F32)
        tpb = alp("lb_tpb", [128, 128], BF16)
        qf = al("lb_qf", [128, 256], F32)
        kf = al("lb_kf", [128, 256], F32)
        s8 = al("lb_s8", [128, 8], F32)
        scol = al("lb_scol", [128, 2], F32)
        hf = al("lb_hf", [128, 128], F32)
        wexp = al("lb_wexp", [128, 128], F32)
        whb = al("lb_whb", [128, 128], BF16)
        for t in range(NT8):
            e_mm_group(p, qp[:], [(xnT[:, a, 128 * t:128 * (t + 1)], wqfb[:, a, :])
                                  for a in range(32)], "qp", ["xnT", "wqfb"])
            p.op("ac", lambda e: e.copy(out=qf[:], in_=qp[:]),
                 reads=["qp"], writes=["qf"])
            e_mm_group(p, kp[:], [(xnT[:, a, 128 * t:128 * (t + 1)], wkfb[:, a, :])
                                  for a in range(32)], "kp", ["xnT", "wkfb"])
            p.op("ac", lambda e: e.copy(out=kf[:], in_=kp[:]),
                 reads=["kp"], writes=["kf"])
            p.op("dv", lambda e: e.tensor_mul(out=qf[:], in0=qf[:], in1=kf[:]),
                 reads=["qf", "kf"], writes=["qf"])
            for e8 in range(8):
                p.op("dv", lambda e, e8=e8: e.reduce_sum(
                    out=s8[:, e8:e8 + 1], in_=qf[:, 32 * e8:32 * (e8 + 1)], axis=AX.X),
                    reads=["qf"], writes=["s8"])
            p.op("dv", lambda e: e.tensor_scalar_mul(
                out=s8[:], in0=s8[:], scalar1=1.0 / float(np.sqrt(32.0))),
                reads=["s8"], writes=["s8"])
            p.op("dv", lambda e: e.reduce_max(out=scol[:, 0:1], in_=s8[:], axis=AX.X),
                 reads=["s8"], writes=["scol"])
            p.op("ac", lambda e: e.mul(out=scol[:, 0:1], in_=scol[:, 0:1], mul=-1.0),
                 reads=["scol"], writes=["scol"])
            p.op("ac", lambda e: e.activation(out=s8[:], in_=s8[:], func=AF.Exp,
                                              bias=scol[:, 0:1]),
                 reads=["s8", "scol"], writes=["s8"])
            p.op("dv", lambda e: e.reduce_sum(out=scol[:, 1:2], in_=s8[:], axis=AX.X),
                 reads=["s8"], writes=["scol"])
            p.op("dv", lambda e: e.reciprocal(out=scol[:, 1:2], in_=scol[:, 1:2]),
                 reads=["scol"], writes=["scol"])
            p.op("dv", lambda e: e.tensor_scalar(out=s8[:], in0=s8[:],
                                                 scalar1=scol[:, 1:2], scalar2=None,
                                                 op0=ALU.mult),
                 reads=["s8", "scol"], writes=["s8"])
            e_mm_group(p, hp[:], [(xnT[:, a, 128 * t:128 * (t + 1)], afb[:, a, :])
                                  for a in range(32)], "hp", ["xnT", "afb"])
            p.op("ac", lambda e: e.copy(out=hf[:], in_=hp[:]),
                 reads=["hp"], writes=["hf"])
            for r_ in range(16):
                p.op("ac", lambda e, r_=r_: e.copy(out=wexp[:, 8 * r_:8 * (r_ + 1)],
                                                   in_=s8[:]),
                     reads=["s8"], writes=["wexp"])
            p.op("dv", lambda e: e.tensor_mul(out=whb[:], in0=hf[:], in1=wexp[:]),
                 reads=["hf", "wexp"], writes=["whb"])
            e_transpose(p, tpb[:], whb[:], identb[:], "tpb", ["whb", "identb"])
            p.op("dv", lambda e, t=t: e.tensor_copy(out=whT[:, t, :], in_=tpb[:]),
                 reads=["tpb"], writes=["whT"])
        p.hard_fence()

    with ExitStack() as ph:
        al = lambda name, shape, dt: ph.enter_context(nc.sbuf_tensor(name, shape, dt))
        alp = lambda name, shape, dt: ph.enter_context(nc.psum_tensor(name, shape, dt))
        bmrb = al("lb_bmrb", [128, D], BF16)
        p.op("gp", lambda e: e.dma_start(out=bmrb[:], in_=io["bmr"][:, :]),
             writes=["bmrb"], dma=True, stream="w")
        wb = [al("lb_wb0", [128, 32, 128], BF16), al("lb_wb1", [128, 32, 128], BF16)]
        fp = alp("lb_fp", [128, 512], F32)
        ff = al("lb_ff", [128, 512], F32)
        tpf = [alp("lb_tpf0", [128, 128], F32), alp("lb_tpf1", [128, 128], F32)]
        stg = [al("lb_st0", [128, 4, 128], F32), al("lb_st1", [128, 4, 128], F32)]
        x0p = [al("lb_x0p0", [128, 4, 128], F32), al("lb_x0p1", [128, 4, 128], F32)]
        a0p = [al("lb_a0p0", [128, 4, 128], F32), al("lb_a0p1", [128, 4, 128], F32)]
        for oc in range(32):
            b = oc % 2
            p.op("gp", lambda e, oc=oc, b=b: e.dma_start(
                out=wb[b][:], in_=io["w_s"][oc]),
                writes=[f"wb{b}"], dma=True, stream=f"wb{b}")
            for tc in range(2):
                sbf = (2 * oc + tc) % 2
                steps = [(wb[b][:, a, :], xnT[:, a, 512 * tc:512 * (tc + 1)])
                         for a in range(32)]
                steps.append((bmrb[:, 128 * oc:128 * (oc + 1)],
                              whT[:, 4 * tc:4 * (tc + 1), :]))
                e_mm_group(p, fp[:, 0:512], steps, "fp",
                           [f"wb{b}", "xnT", "bmrb", "whT"])
                p.op("dv", lambda e, oc=oc: e.tensor_scalar(
                    out=ff[:], in0=fp[:, 0:512], scalar1=bcol[:, oc:oc + 1],
                    scalar2=None, op0=ALU.add),
                    reads=["fp", "bcol"], writes=["ff"])
                p.op("sync", lambda e, oc=oc, tc=tc, sbf=sbf: e.dma_start(
                    out=x0p[sbf][:],
                    in_=io["x0o"][512 * tc:512 * (tc + 1), 128 * oc:128 * (oc + 1)]
                    .rearrange("(pc pp) o -> pp pc o", pp=128)),
                    writes=[f"x0p{sbf}"], dma=True, stream="xp")
                p.op("sync", lambda e, oc=oc, tc=tc, sbf=sbf: e.dma_start(
                    out=a0p[sbf][:],
                    in_=io["a0"][512 * tc:512 * (tc + 1), 128 * oc:128 * (oc + 1)]
                    .rearrange("(pc pp) o -> pp pc o", pp=128)),
                    writes=[f"a0p{sbf}"], dma=True, stream="ap")
                for pc in range(4):
                    tb = pc % 2
                    e_transpose(p, tpf[tb][:], ff[:, 128 * pc:128 * (pc + 1)],
                                identf[:], f"tpf{tb}", ["ff", "identf"])
                    p.op("dv", lambda e, pc=pc, tb=tb, sbf=sbf: e.tensor_copy(
                        out=stg[sbf][:, pc, :], in_=tpf[tb][:]),
                        reads=[f"tpf{tb}"], writes=[f"stg{sbf}"])
                p.op("dv", lambda e, sbf=sbf: e.tensor_add(
                    out=stg[sbf][:], in0=stg[sbf][:], in1=x0p[sbf][:]),
                    reads=[f"stg{sbf}", f"x0p{sbf}"], writes=[f"stg{sbf}"])
                p.op("dv", lambda e, sbf=sbf: e.tensor_add(
                    out=stg[sbf][:], in0=stg[sbf][:], in1=a0p[sbf][:]),
                    reads=[f"stg{sbf}", f"a0p{sbf}"], writes=[f"stg{sbf}"])
                p.op("gp", lambda e, oc=oc, tc=tc, sbf=sbf: e.dma_start(
                    out=io["x1"][512 * tc:512 * (tc + 1), 128 * oc:128 * (oc + 1)]
                    .rearrange("(pc pp) o -> pp pc o", pp=128),
                    in_=stg[sbf][:]),
                    reads=[f"stg{sbf}"], writes=["hbm_x1"], dma=True, stream="out")
        p.hard_fence()

    with ExitStack() as ph:
        al = lambda name, shape, dt: ph.enter_context(nc.sbuf_tensor(name, shape, dt))
        x1t = al("lb_x1t", [128, D], F32)
        xn1t = al("lb_xn1t", [128, D], F32)
        coltmp = al("lb_col2", [128, 3], F32)
        p.op("dv", lambda e: e.memset(coltmp[:, 2:3], EPS), writes=["epscol2"])
        for t in range(NT8):
            p.op("sync", lambda e, t=t: e.dma_start(
                out=x1t[:], in_=io["x1"][128 * t:128 * (t + 1), :]),
                reads=["hbm_x1"], writes=["x1t"], dma=True, stream="x")
            e_ln(p, x1t[:], xn1t[:], coltmp, "x1t", "xn1t", "coltmp2",
                 eps_col=coltmp[:, 2:3])
            p.op("gp", lambda e, t=t: e.dma_start(
                out=io["xn1"][128 * t:128 * (t + 1), :], in_=xn1t[:]),
                reads=["xn1t"], writes=["hbm_xn1"], dma=True, stream="out")
        p.hard_fence()
    p.finish(["hbm_x1", "hbm_xn1"])


def build_LB():
    nc = bass.Bass("TRN2", target_bir_lowering=False)
    io = {}
    def inp(name, shape):
        io[name] = nc.dram_tensor(name, list(shape), F32, kind="ExternalInput").ap()
    inp("x0o", (TOK, D)); inp("a0", (TOK, D))
    inp("w_s", (32, 128, 32, 128))
    inp("wqf_r", (128, 32, 256)); inp("wkf_r", (128, 32, 256))
    inp("af_r", (128, 32, 128)); inp("bmr", (128, D)); inp("bcol", (128, 32))
    inp("identf", (128, 128)); inp("identb", (128, 128))
    io["x1"] = nc.dram_tensor("x1", [TOK, D], F32, kind="ExternalOutput").ap()
    io["xn1"] = nc.dram_tensor("xn1", [TOK, D], F32, kind="ExternalOutput").ap()
    with ExitStack() as st:
        p = Prog(nc, st)
        emit_ffn0(nc, p, st, io)
    return nc


def host_prep_LB(inputs):
    w = inputs["L0_ffn_W"].astype(np.float32)
    bv = inputs["L0_ffn_b"].astype(np.float32)
    wqf = inputs["L0_ffn_Wq"].astype(np.float32)
    wkf = inputs["L0_ffn_Wk"].astype(np.float32)
    A = inputs["L0_ffn_A"].astype(np.float32)
    Bm = inputs["L0_ffn_Bm"].astype(np.float32)
    A2 = np.ascontiguousarray(A.transpose(1, 0, 2).reshape(128, D))
    Bmr = np.ascontiguousarray(Bm.transpose(2, 0, 1).reshape(128, D))
    shared = {
        "w_s": stage_stream(w),
        "wqf_r": stage_rhs(np.ascontiguousarray(wqf.T), 256),
        "wkf_r": stage_rhs(np.ascontiguousarray(wkf.T), 256),
        "af_r": stage_rhs(np.ascontiguousarray(A2.T), 128),
        "bmr": Bmr,
        "bcol": np.ascontiguousarray(bv.reshape(32, 128).T),
        "identf": np.eye(128, dtype=np.float32),
        "identb": np.eye(128, dtype=np.float32),
    }
    return shared



def _cmul(p, outre, outim, are, aim, bre, bim, t1, keys, sub=False):
    kor, koi, kar, kai, kbr, kbi, kt1 = keys
    p.op("dv", lambda e: e.tensor_mul(out=t1, in0=aim, in1=bim),
         reads=[kai, kbi], writes=[kt1])
    p.op("dv", lambda e: e.tensor_mul(out=outre, in0=are, in1=bre),
         reads=[kar, kbr], writes=[kor])
    p.op("dv", lambda e: e.tensor_sub(out=outre, in0=outre, in1=t1),
         reads=[kor, kt1], writes=[kor])
    p.op("dv", lambda e: e.tensor_mul(out=t1, in0=aim, in1=bre),
         reads=[kai, kbr], writes=[kt1])
    p.op("dv", lambda e: e.tensor_mul(out=outim, in0=are, in1=bim),
         reads=[kar, kbi], writes=[koi])
    p.op("dv", lambda e: e.tensor_add(out=outim, in0=outim, in1=t1),
         reads=[koi, kt1], writes=[koi])


def emit_s4(nc, p, st, io):
    sb = lambda name, shape, dt: st.enter_context(nc.sbuf_tensor(name, shape, dt))
    kpad = io["kpad"]
    scr = io["scr"]

    identf = sb("lc_idf", [128, 128], F32)
    identb = sb("lc_idb", [128, 128], BF16)
    jrev = sb("lc_jrev", [128, 128], F32)
    p.op("sync", lambda e: e.dma_start(out=identf[:], in_=io["identf"][:, :]),
         writes=["identf"], dma=True, stream="cst")
    p.op("gp", lambda e: e.dma_start(out=identb[:], in_=io["identb"][:, :]),
         writes=["identb"], dma=True, stream="cstb")
    p.op("sync", lambda e: e.dma_start(out=jrev[:], in_=io["jrev"][:, :]),
         writes=["jrev"], dma=True, stream="cst")
    dcol = sb("lc_dcol", [128, 1], F32)

    with ExitStack() as ph:
        al = lambda name, shape, dt: ph.enter_context(nc.sbuf_tensor(name, shape, dt))
        alp = lambda name, shape, dt: ph.enter_context(nc.psum_tensor(name, shape, dt))
        o64 = al("lc_o64", [1, 64], F32)
        o128 = al("lc_o128", [1, 128], F32)
        sc1 = al("lc_sc1", [1, 1], F32)
        scol = al("lc_scol", [64, 1], F32)
        small = {}
        for nm in ("lam_re", "lam_im", "p_re", "p_im", "q_re", "q_im",
                   "ct_re", "ct_im", "bp"):
            small[nm] = al("lc_" + nm, [64, 1], F32)
        vct_re = al("lc_vctre", [64, 64], F32)
        vct_im = al("lc_vctim", [64, 64], F32)
        for nm in list(small) + ["vct_re", "vct_im"]:
            t = small.get(nm) or {"vct_re": vct_re, "vct_im": vct_im}[nm]
            p.op("sync", lambda e, nm=nm, t=t: e.dma_start(out=t[:], in_=io[nm][:, :]),
                 writes=[nm], dma=True, stream="cst")
        p.op("sync", lambda e: e.dma_start(out=o64[:], in_=io["ones64"][:, :]),
             writes=["o64"], dma=True, stream="cst")
        p.op("sync", lambda e: e.dma_start(out=o128[:], in_=io["ones128"][:, :]),
             writes=["o128"], dma=True, stream="cst")
        p.op("sync", lambda e: e.dma_start(out=sc1[:], in_=io["lstep"][:, :]),
             writes=["sc1"], dma=True, stream="cst")
        p.op("sync", lambda e: e.dma_start(out=dcol[0:1, :], in_=io["dco"][:, :]),
             writes=["dcol"], dma=True, stream="cst")
        pbd = alp("lc_pbd", [128, 1], F32)
        e_mm_group(p, pbd[:], [(o128[:], dcol[0:1, 0:1])], "pbd", ["o128", "dcol"])
        p.op("dv", lambda e: e.tensor_copy(out=dcol[:], in_=pbd[:]),
             reads=["pbd"], writes=["dcol"])
        p.op("ac", lambda e: e.activation(out=sc1[:], in_=sc1[:], func=AF.Exp),
             reads=["sc1"], writes=["sc1"])
        p.op("dv", lambda e: e.reciprocal(out=sc1[:], in_=sc1[:]),
             reads=["sc1"], writes=["sc1"])
        p.op("ac", lambda e: e.mul(out=sc1[:], in_=sc1[:], mul=2.0),
             reads=["sc1"], writes=["sc1"])
        psc = alp("lc_psc", [64, 1], F32)
        e_mm_group(p, psc[:], [(o64[:], sc1[:])], "psc", ["o64", "sc1"])
        p.op("dv", lambda e: e.tensor_copy(out=scol[:], in_=psc[:]),
             reads=["psc"], writes=["scol"])
        bre = al("lc_bre", [64, 1], F32)
        bim = al("lc_bim", [64, 1], F32)
        pb2 = alp("lc_pb2", [64, 1], F32)
        e_mm_group(p, pb2[:], [(vct_re[:], small["bp"][:])], "pb2", ["vct_re", "bp"])
        p.op("dv", lambda e: e.tensor_copy(out=bre[:], in_=pb2[:]),
             reads=["pb2"], writes=["bre"])
        e_mm_group(p, pb2[:], [(vct_im[:], small["bp"][:])], "pb2", ["vct_im", "bp"])
        p.op("dv", lambda e: e.tensor_copy(out=bim[:], in_=pb2[:]),
             reads=["pb2"], writes=["bim"])
        vre = al("lc_vre", [64, 4], F32)
        vim = al("lc_vim", [64, 4], F32)
        nvre = al("lc_nvre", [64, 4], F32)
        tt1 = al("lc_tt1", [64, 1], F32)
        combos = [("ct_re", "ct_im", bre, bim, "bre", "bim"),
                  ("ct_re", "ct_im", small["p_re"], small["p_im"], "p_re", "p_im"),
                  ("q_re", "q_im", bre, bim, "bre", "bim"),
                  ("q_re", "q_im", small["p_re"], small["p_im"], "p_re", "p_im")]
        for j, (ur, ui, wr_t, wi_t, kwr, kwi) in enumerate(combos):
            urt, uit = small[ur], small[ui]
            p.op("dv", lambda e, urt=urt, wr_t=wr_t, j=j: e.tensor_mul(
                out=vre[:, j:j + 1], in0=urt[:], in1=wr_t[:]),
                reads=[ur, kwr], writes=["vre"])
            p.op("dv", lambda e, uit=uit, wi_t=wi_t: e.tensor_mul(
                out=tt1[:], in0=uit[:], in1=wi_t[:]),
                reads=[ui, kwi], writes=["tt1"])
            p.op("dv", lambda e, j=j: e.tensor_add(
                out=vre[:, j:j + 1], in0=vre[:, j:j + 1], in1=tt1[:]),
                reads=["vre", "tt1"], writes=["vre"])
            p.op("dv", lambda e, urt=urt, wi_t=wi_t, j=j: e.tensor_mul(
                out=vim[:, j:j + 1], in0=urt[:], in1=wi_t[:]),
                reads=[ur, kwi], writes=["vim"])
            p.op("dv", lambda e, uit=uit, wr_t=wr_t: e.tensor_mul(
                out=tt1[:], in0=uit[:], in1=wr_t[:]),
                reads=[ui, kwr], writes=["tt1"])
            p.op("dv", lambda e, j=j: e.tensor_sub(
                out=vim[:, j:j + 1], in0=vim[:, j:j + 1], in1=tt1[:]),
                reads=["vim", "tt1"], writes=["vim"])
        p.op("dv", lambda e: e.tensor_scalar_mul(out=nvre[:], in0=vre[:], scalar1=-1.0),
             reads=["vre"], writes=["nvre"])
        wrow_re = al("lc_wrowre", [1, D], F32)
        wrow_im = al("lc_wrowim", [1, D], F32)
        p.op("sync", lambda e: e.dma_start(out=wrow_re[:], in_=io["wre"][:, :]),
             writes=["wrow_re"], dma=True, stream="cst")
        p.op("sync", lambda e: e.dma_start(out=wrow_im[:], in_=io["wim"][:, :]),
             writes=["wrow_im"], dma=True, stream="cst")
        dre = al("lc_dre", [64, D], F32)
        dim = al("lc_dim", [64, D], F32)
        rec = al("lc_rec", [64, D], F32)
        pwb = alp("lc_pwb", [64, 512], F32)
        for c in range(8):
            sl = slice(512 * c, 512 * (c + 1))
            e_mm_group(p, pwb[:], [(o64[:], wrow_re[:, sl])], "pwb",
                       ["o64", "wrow_re"])
            p.op("dv", lambda e, sl=sl: e.tensor_scalar(
                out=dre[:, sl], in0=pwb[:], scalar1=scol[:, 0:1],
                scalar2=small["lam_re"][:, 0:1], op0=ALU.mult, op1=ALU.subtract),
                reads=["pwb", "scol", "lam_re"], writes=["dre"])
            e_mm_group(p, pwb[:], [(o64[:], wrow_im[:, sl])], "pwb",
                       ["o64", "wrow_im"])
            p.op("dv", lambda e, sl=sl: e.tensor_scalar(
                out=dim[:, sl], in0=pwb[:], scalar1=scol[:, 0:1],
                scalar2=small["lam_im"][:, 0:1], op0=ALU.mult, op1=ALU.subtract),
                reads=["pwb", "scol", "lam_im"], writes=["dim"])
        p.op("ac", lambda e: e.activation(out=rec[:], in_=dre[:], func=AF.Square),
             reads=["dre"], writes=["recA"])
        sq2 = al("lc_sq2", [64, D], F32)
        p.op("ac", lambda e: e.activation(out=sq2[:], in_=dim[:], func=AF.Square),
             reads=["dim"], writes=["sq2"])
        p.op("dv", lambda e: e.tensor_add(out=rec[:], in0=rec[:], in1=sq2[:]),
             reads=["recA", "sq2"], writes=["recB"])
        p.op("dv", lambda e: e.reciprocal(out=rec[:], in_=rec[:]),
             reads=["recB"], writes=["recC"])
        p.op("dv", lambda e: e.tensor_mul(out=dre[:], in0=dre[:], in1=rec[:]),
             reads=["dre", "recC"], writes=["idre"])
        p.op("dv", lambda e: e.tensor_mul(out=dim[:], in0=dim[:], in1=rec[:]),
             reads=["dim", "recC"], writes=["idimN"])
        kre = al("lc_kre", [4, D], F32)
        kim = al("lc_kim", [4, D], F32)
        pk = alp("lc_pk", [4, 512], F32)
        for c in range(8):
            sl = slice(512 * c, 512 * (c + 1))
            e_mm_group(p, pk[:], [(vre[:], dre[:, sl]), (vim[:], dim[:, sl])],
                       "pk", ["vre", "vim", "idre", "idimN"])
            p.op("dv", lambda e, sl=sl: e.tensor_copy(out=kre[:, sl], in_=pk[:]),
                 reads=["pk"], writes=["kre"])
            e_mm_group(p, pk[:], [(vim[:], dre[:, sl]), (nvre[:], dim[:, sl])],
                       "pk", ["vim", "nvre", "idre", "idimN"])
            p.op("dv", lambda e, sl=sl: e.tensor_copy(out=kim[:, sl], in_=pk[:]),
                 reads=["pk"], writes=["kim"])
        scr4 = io["scr4"]
        p.op("gp", lambda e: e.dma_start(out=scr4[0:4, :], in_=kre[:]),
             reads=["kre"], writes=["hbm_scr4a"], dma=True, stream="sc")
        p.op("gp", lambda e: e.dma_start(out=scr4[4:8, :], in_=kim[:]),
             reads=["kim"], writes=["hbm_scr4b"], dma=True, stream="sc")
        k64 = {}
        for j, nm in enumerate(("k00", "k01", "k10", "k11")):
            for ri, part in enumerate(("re", "im")):
                t = al(f"lc_{nm}{part}", [64, 64], F32)
                k64[nm + part] = t
                p.op("sync", lambda e, j=j, ri=ri, t=t: e.dma_start(
                    out=t[:], in_=scr4[4 * ri + j].rearrange("(pp f) -> pp f", f=64)),
                    reads=["hbm_scr4a", "hbm_scr4b"], writes=[nm + part],
                    dma=True, stream="k64")
        crow_re = al("lc_crowre", [64, 64], F32)
        crow_im = al("lc_crowim", [64, 64], F32)
        p.op("sync", lambda e: e.dma_start(out=crow_re[:], in_=io["cre"][:, :]),
             writes=["crow_re"], dma=True, stream="cst")
        p.op("sync", lambda e: e.dma_start(out=crow_im[:], in_=io["cim"][:, :]),
             writes=["crow_im"], dma=True, stream="cst")
        zre = al("lc_zre", [64, 64], F32)
        zim = al("lc_zim", [64, 64], F32)
        t64 = al("lc_t64", [64, 64], F32)
        den = al("lc_den", [64, 64], F32)
        _cmul(p, zre[:], zim[:], k64["k01re"][:], k64["k01im"][:],
              k64["k10re"][:], k64["k10im"][:], t64[:],
              ("zre", "zim", "k01re", "k01im", "k10re", "k10im", "t64"))
        p.op("dv", lambda e: e.tensor_scalar_add(out=k64["k11re"][:],
                                                 in0=k64["k11re"][:], scalar1=1.0),
             reads=["k11re"], writes=["k11re"])
        p.op("ac", lambda e: e.activation(out=den[:], in_=k64["k11re"][:],
                                          func=AF.Square),
             reads=["k11re"], writes=["den"])
        p.op("ac", lambda e: e.activation(out=t64[:], in_=k64["k11im"][:],
                                          func=AF.Square),
             reads=["k11im"], writes=["t64"])
        p.op("dv", lambda e: e.tensor_add(out=den[:], in0=den[:], in1=t64[:]),
             reads=["den", "t64"], writes=["den"])
        p.op("dv", lambda e: e.reciprocal(out=den[:], in_=den[:]),
             reads=["den"], writes=["den"])
        tre = al("lc_tre", [64, 64], F32)
        tim = al("lc_tim", [64, 64], F32)
        p.op("dv", lambda e: e.tensor_mul(out=tre[:], in0=zre[:], in1=k64["k11re"][:]),
             reads=["zre", "k11re"], writes=["tre"])
        p.op("dv", lambda e: e.tensor_mul(out=t64[:], in0=zim[:], in1=k64["k11im"][:]),
             reads=["zim", "k11im"], writes=["t64"])
        p.op("dv", lambda e: e.tensor_add(out=tre[:], in0=tre[:], in1=t64[:]),
             reads=["tre", "t64"], writes=["tre"])
        p.op("dv", lambda e: e.tensor_mul(out=tim[:], in0=zim[:], in1=k64["k11re"][:]),
             reads=["zim", "k11re"], writes=["tim"])
        p.op("dv", lambda e: e.tensor_mul(out=t64[:], in0=zre[:], in1=k64["k11im"][:]),
             reads=["zre", "k11im"], writes=["t64"])
        p.op("dv", lambda e: e.tensor_sub(out=tim[:], in0=tim[:], in1=t64[:]),
             reads=["tim", "t64"], writes=["tim"])
        p.op("dv", lambda e: e.tensor_mul(out=tre[:], in0=tre[:], in1=den[:]),
             reads=["tre", "den"], writes=["tre"])
        p.op("dv", lambda e: e.tensor_mul(out=tim[:], in0=tim[:], in1=den[:]),
             reads=["tim", "den"], writes=["tim"])
        p.op("dv", lambda e: e.tensor_sub(out=tre[:], in0=k64["k00re"][:], in1=tre[:]),
             reads=["k00re", "tre"], writes=["ure"])
        p.op("dv", lambda e: e.tensor_sub(out=tim[:], in0=k64["k00im"][:], in1=tim[:]),
             reads=["k00im", "tim"], writes=["uim"])
        are = al("lc_are", [64, 64], F32)
        aim = al("lc_aim", [64, 64], F32)
        _cmul(p, are[:], aim[:], crow_re[:], crow_im[:], tre[:], tim[:], t64[:],
              ("arere", "areim", "crow_re", "crow_im", "ure", "uim", "t64"))
        e64re = al("lc_e64re", [64, 64], F32)
        e64im = al("lc_e64im", [64, 64], F32)
        e64imn = al("lc_e64imn", [64, 64], F32)
        e64bre = al("lc_e64bre", [64, 64], F32)
        e64bimn = al("lc_e64bimn", [64, 64], F32)
        twre = al("lc_twre", [64, 64], F32)
        twim = al("lc_twim", [64, 64], F32)
        for nm, t in (("e64re", e64re), ("e64im", e64im), ("e64imn", e64imn),
                      ("e64bre", e64bre), ("e64bimn", e64bimn),
                      ("twre", twre), ("twim", twim)):
            p.op("sync", lambda e, nm=nm, t=t: e.dma_start(out=t[:], in_=io[nm][:, :]),
                 writes=[nm], dma=True, stream="cst")
        py = alp("lc_py", [64, 64], F32)
        yre = al("lc_yre", [64, 64], F32)
        yim = al("lc_yim", [64, 64], F32)
        e_mm_group(p, py[:], [(e64re[:], are[:]), (e64imn[:], aim[:])],
                   "py", ["e64re", "e64imn", "arere", "areim"])
        p.op("dv", lambda e: e.tensor_copy(out=yre[:], in_=py[:]),
             reads=["py"], writes=["yre"])
        e_mm_group(p, py[:], [(e64im[:], are[:]), (e64re[:], aim[:])],
                   "py", ["e64im", "e64re", "arere", "areim"])
        p.op("dv", lambda e: e.tensor_copy(out=yim[:], in_=py[:]),
             reads=["py"], writes=["yim"])
        _cmul(p, zre[:], zim[:], yre[:], yim[:], twre[:], twim[:], t64[:],
              ("zre2", "zim2", "yre", "yim", "twre", "twim", "t64"))
        pz = alp("lc_pz", [64, 64], F32)
        ztre = al("lc_ztre", [64, 64], F32)
        ztim = al("lc_ztim", [64, 64], F32)
        e_transpose(p, pz[:], zre[:], identf[0:64, 0:64], "pz", ["zre2", "identf"])
        p.op("dv", lambda e: e.tensor_copy(out=ztre[:], in_=pz[:]),
             reads=["pz"], writes=["ztre"])
        e_transpose(p, pz[:], zim[:], identf[0:64, 0:64], "pz", ["zim2", "identf"])
        p.op("dv", lambda e: e.tensor_copy(out=ztim[:], in_=pz[:]),
             reads=["pz"], writes=["ztim"])
        e_mm_group(p, py[:], [(e64bre[:], ztre[:]), (e64bimn[:], ztim[:])],
                   "py", ["e64bre", "e64bimn", "ztre", "ztim"])
        kmat = al("lc_kmat", [64, 64], F32)
        p.op("dv", lambda e: e.tensor_copy(out=kmat[:], in_=py[:]),
             reads=["py"], writes=["kmat"])
        zt = al("lc_zt", [1, 256], F32)
        p.op("dv", lambda e: e.memset(zt[:], 0.0), writes=["zt"])
        p.op("gp", lambda e: e.dma_start(out=kpad[0:127].rearrange("(a f) -> a f", a=1),
                                         in_=zt[:, 0:127]),
             reads=["zt"], writes=["hbm_kpad"], dma=True, stream="kp")
        p.op("gp", lambda e: e.dma_start(
            out=kpad[4223:4352].rearrange("(a f) -> a f", a=1), in_=zt[:, 0:129]),
            reads=["zt"], writes=["hbm_kpad"], dma=True, stream="kp")
        p.op("gp", lambda e: e.dma_start(
            out=kpad[127:4223].rearrange("(pp f) -> pp f", f=64), in_=kmat[:]),
            reads=["kmat"], writes=["hbm_kpad"], dma=True, stream="kp")
        p.hard_fence()

    with ExitStack() as ph:
        al = lambda name, shape, dt: ph.enter_context(nc.sbuf_tensor(name, shape, dt))
        alp = lambda name, shape, dt: ph.enter_context(nc.psum_tensor(name, shape, dt))
        bd = al("lc_bd", [128, 32, 128], BF16)
        gd = [al("lc_gd0", [128, 128], F32), al("lc_gd1", [128, 128], F32)]
        pbd2 = [alp("lc_pbd0", [128, 128], F32), alp("lc_pbd1", [128, 128], F32)]
        for d_ in range(32):
            b = d_ % 2
            p.op("sync", lambda e, d_=d_, b=b: e.dma_start(
                out=gd[b][:],
                in_=bass.AP(tensor=kpad.tensor, offset=kpad.offset + 128 * d_,
                            ap=[[1, 128], [1, 128]])),
                reads=["hbm_kpad"], writes=[f"gd{b}"], dma=True, stream="kg")
            e_mm_group(p, pbd2[b][:], [(jrev[:], gd[b][:])], f"pbd{b}",
                       ["jrev", f"gd{b}"])
            p.op("dv", lambda e, d_=d_, b=b: e.tensor_copy(out=bd[:, d_, :],
                                                           in_=pbd2[b][:]),
                 reads=[f"pbd{b}"], writes=["bd"])
        xT = al("lc_xT", [128, 32, TOK], BF16)
        xb = al("lc_xb", [128, D], BF16)
        tpp = [alp("lc_tp0", [128, 128], BF16), alp("lc_tp1", [128, 128], BF16)]
        for t in range(TOK // 128):
            p.op("gp", lambda e, t=t: e.dma_start(
                out=xb[:], in_=io["xn1"][128 * t:128 * (t + 1), :]),
                writes=["xb"], dma=True, stream="x")
            for a in range(32):
                tb = a % 2
                e_transpose(p, tpp[tb][:], xb[:, 128 * a:128 * (a + 1)],
                            identb[:], f"tp{tb}", ["xb", "identb"])
                if a % 2:
                    p.op("dv", lambda e, a=a, t=t, tb=tb: e.tensor_copy(
                        out=xT[:, a, 128 * t:128 * (t + 1)], in_=tpp[tb][:]),
                        reads=[f"tp{tb}"], writes=["xT"])
                else:
                    p.op("ac", lambda e, a=a, t=t, tb=tb: e.copy(
                        out=xT[:, a, 128 * t:128 * (t + 1)], in_=tpp[tb][:]),
                        reads=[f"tp{tb}"], writes=["xT"])
        yp = alp("lc_yp", [128, 512], F32)
        yf = al("lc_yf", [128, 512], F32)
        tpf = [alp("lc_tpf0", [128, 128], F32), alp("lc_tpf1", [128, 128], F32)]
        stg = [al("lc_st0", [128, 4, 128], F32), al("lc_st1", [128, 4, 128], F32)]
        xnp = [al("lc_xnp0", [128, 4, 128], F32), al("lc_xnp1", [128, 4, 128], F32)]
        for oc in range(32):
            for tc in range(2):
                sbf = (2 * oc + tc) % 2
                e_mm_group(p, yp[:, 0:512],
                           [(bd[:, oc - i, :], xT[:, i, 512 * tc:512 * (tc + 1)])
                            for i in range(oc + 1)],
                           "yp", ["bd", "xT"])
                p.op("ac", lambda e: e.copy(out=yf[:], in_=yp[:, 0:512]),
                     reads=["yp"], writes=["yf"])
                p.op("sync", lambda e, oc=oc, tc=tc, sbf=sbf: e.dma_start(
                    out=xnp[sbf][:],
                    in_=io["xn1"][512 * tc:512 * (tc + 1), 128 * oc:128 * (oc + 1)]
                    .rearrange("(pc pp) o -> pp pc o", pp=128)),
                    writes=[f"xnp{sbf}"], dma=True, stream="xp")
                for pc in range(4):
                    tb = pc % 2
                    e_transpose(p, tpf[tb][:], yf[:, 128 * pc:128 * (pc + 1)],
                                identf[:], f"tpf{tb}", ["yf", "identf"])
                    p.op("dv", lambda e, pc=pc, tb=tb, sbf=sbf: e.tensor_copy(
                        out=stg[sbf][:, pc, :], in_=tpf[tb][:]),
                        reads=[f"tpf{tb}"], writes=[f"stg{sbf}"])
                p.op("dv", lambda e, sbf=sbf: e.scalar_tensor_tensor(
                    out=stg[sbf][:], in0=xnp[sbf][:], scalar=dcol[:, 0:1],
                    in1=stg[sbf][:], op0=ALU.mult, op1=ALU.add),
                    reads=[f"stg{sbf}", f"xnp{sbf}", "dcol"], writes=[f"stg{sbf}"])
                p.op("gp", lambda e, oc=oc, tc=tc, sbf=sbf: e.dma_start(
                    out=io["a1"][512 * tc:512 * (tc + 1), 128 * oc:128 * (oc + 1)]
                    .rearrange("(pc pp) o -> pp pc o", pp=128),
                    in_=stg[sbf][:]),
                    reads=[f"stg{sbf}"], writes=["hbm_a1"], dma=True, stream="out")
        p.hard_fence()
    p.finish(["hbm_a1"])


def build_LC():
    nc = bass.Bass("TRN2", target_bir_lowering=False)
    io = {}
    def inp(name, shape):
        io[name] = nc.dram_tensor(name, list(shape), F32, kind="ExternalInput").ap()
    inp("xn1", (TOK, D))
    for nm in ("lam_re", "lam_im", "p_re", "p_im", "q_re", "q_im",
               "ct_re", "ct_im", "bp"):
        inp(nm, (64, 1))
    inp("vct_re", (64, 64)); inp("vct_im", (64, 64))
    inp("dco", (1, 1)); inp("lstep", (1, 1))
    inp("wre", (1, D)); inp("wim", (1, D))
    inp("cre", (64, 64)); inp("cim", (64, 64))
    for nm in ("e64re", "e64im", "e64imn", "e64bre", "e64bimn", "twre", "twim"):
        inp(nm, (64, 64))
    inp("ones64", (1, 64)); inp("ones128", (1, 128))
    inp("identf", (128, 128)); inp("identb", (128, 128)); inp("jrev", (128, 128))
    io["kpad"] = nc.dram_tensor("kpad", [4352], F32, kind="Internal").ap()
    io["scr4"] = nc.dram_tensor("scr4", [8, D], F32, kind="Internal").ap()
    io["scr"] = None
    io["a1"] = nc.dram_tensor("a1", [TOK, D], F32, kind="ExternalOutput").ap()
    with ExitStack() as st:
        p = Prog(nc, st)
        emit_s4(nc, p, st, io)
    return nc


def host_prep_LC(inputs):
    def ri(t):
        return np.ascontiguousarray(t[..., 0]), np.ascontiguousarray(t[..., 1])
    lam_re, lam_im = ri(inputs["L1_attn_Lambda_ri"].astype(np.float32))
    p_re, p_im = ri(inputs["L1_attn_p_ri"].astype(np.float32))
    q_re, q_im = ri(inputs["L1_attn_q_ri"].astype(np.float32))
    vc_re, vc_im = ri(inputs["L1_attn_Vc_ri"].astype(np.float32))
    ct_re, ct_im = ri(inputs["L1_attn_Ct_ri"].astype(np.float32).reshape(64, 2)
                      .reshape(1, 64, 2))
    ct_re = ct_re.reshape(64, 1); ct_im = ct_im.reshape(64, 1)
    L = D
    karr = np.arange(L)
    Om = np.exp(-2j * np.pi * (karr / L)).astype(np.complex64)
    w = ((1.0 - Om) / (1.0 + Om)).astype(np.complex64)
    cc = (2.0 / (1.0 + Om)).astype(np.complex64)
    n1 = np.arange(64)
    E64 = np.exp(2j * np.pi * np.outer(n1, n1) / 64.0)
    Tw = np.exp(2j * np.pi * np.outer(n1, n1) / 4096.0)
    shared = {
        "lam_re": lam_re.reshape(64, 1), "lam_im": lam_im.reshape(64, 1),
        "p_re": p_re.reshape(64, 1), "p_im": p_im.reshape(64, 1),
        "q_re": q_re.reshape(64, 1), "q_im": q_im.reshape(64, 1),
        "ct_re": ct_re, "ct_im": ct_im,
        "bp": inputs["L1_attn_Bp"].astype(np.float32).reshape(64, 1),
        "vct_re": np.ascontiguousarray(vc_re.T),
        "vct_im": np.ascontiguousarray(vc_im.T),
        "dco": inputs["L1_attn_D"].astype(np.float32).reshape(1, 1),
        "lstep": inputs["L1_attn_log_step"].astype(np.float32).reshape(1, 1),
        "wre": np.ascontiguousarray(w.real.astype(np.float32)).reshape(1, L),
        "wim": np.ascontiguousarray(w.imag.astype(np.float32)).reshape(1, L),
        "cre": np.ascontiguousarray(cc.real.astype(np.float32)).reshape(64, 64),
        "cim": np.ascontiguousarray(cc.imag.astype(np.float32)).reshape(64, 64),
        "e64re": E64.real.astype(np.float32).copy(),
        "e64im": E64.imag.astype(np.float32).copy(),
        "e64imn": (-E64.imag).astype(np.float32).copy(),
        "e64bre": (E64.real / 4096.0).astype(np.float32).copy(),
        "e64bimn": (-E64.imag / 4096.0).astype(np.float32).copy(),
        "twre": Tw.real.astype(np.float32).copy(),
        "twim": Tw.imag.astype(np.float32).copy(),
        "ones64": np.ones((1, 64), np.float32),
        "ones128": np.ones((1, 128), np.float32),
        "identf": np.eye(128, dtype=np.float32),
        "identb": np.eye(128, dtype=np.float32),
        "jrev": np.eye(128, dtype=np.float32)[::-1].copy(),
    }
    return shared



def emit_ffn1(nc, p, st, io):
    sb = lambda name, shape, dt: st.enter_context(nc.sbuf_tensor(name, shape, dt))
    JC = 112
    NJQ = 4
    JPQ = JC // NJQ

    identf = sb("ld_idf", [128, 128], F32)
    identb = sb("ld_idb", [128, 128], BF16)
    ones128f = sb("ld_o128", [1, 128], F32)
    p.op("sync", lambda e: e.dma_start(out=identf[:], in_=io["identf"][:, :]),
         writes=["identf"], dma=True, stream="cst")
    p.op("gp", lambda e: e.dma_start(out=identb[:], in_=io["identb"][:, :]),
         writes=["identb"], dma=True, stream="cstb")
    p.op("sync", lambda e: e.dma_start(out=ones128f[:], in_=io["ones128"][:, :]),
         writes=["o128"], dma=True, stream="cst")
    c1tb = sb("ld_c1tb", [128, 32, 16], BF16)
    c2tb = sb("ld_c2tb", [16, 14336], BF16)
    p.op("gp", lambda e: e.dma_start(out=c1tb[:], in_=io["c1t_r"][:, :, :]),
         writes=["c1tb"], dma=True, stream="cstb")
    p.op("gp", lambda e: e.dma_start(out=c2tb[:], in_=io["c2t"][:, :]),
         writes=["c2tb"], dma=True, stream="cstb")

    xTh = sb("ld_xTh", [128, 32, 512], BF16)
    hmT = sb("ld_hmT", [128, JPQ, 512], BF16)
    oacc = sb("ld_oacc", [128, 32, 512], F32)
    ztb = sb("ld_ztb", [16, 512], BF16)
    bmaxT = sb("ld_bmaxT", [64, 512], F32)
    xb = sb("ld_xb", [128, D], BF16)
    w1b = [sb("ld_w1b0", [128, 32, 128], BF16), sb("ld_w1b1", [128, 32, 128], BF16)]
    w2b = [sb("ld_w2b0", [128, 8, 128], BF16), sb("ld_w2b1", [128, 8, 128], BF16)]
    brow = [sb("ld_brow0", [1, 512], F32), sb("ld_brow1", [1, 512], F32)]
    bbs = sb("ld_bbs", [128, 512], F32)
    bbadj = sb("ld_bbadj", [128, 512], F32)
    iv = sb("ld_iv", [128, 512], F32)
    hr = sb("ld_hr", [128, 512], BF16)
    mk = sb("ld_mk", [128, 512], BF16)
    bmax = sb("ld_bmax", [128, 64], F32)
    stg = [sb("ld_st0", [128, 4, 128], F32)]
    x1p = [sb("ld_x1p0", [128, 4, 128], F32)]
    a1p = [sb("ld_a1p0", [128, 4, 128], F32)]
    bmaxd = io["bmaxd"]

    for th in range(2):
        t0 = 512 * th
        with ExitStack() as ph:
            alp = lambda name, shape, dt: ph.enter_context(nc.psum_tensor(name, shape, dt))
            tpp = [alp(f"ld_tp0_{th}", [128, 128], BF16), alp(f"ld_tp1_{th}", [128, 128], BF16)]
            for t in range(4):
                p.op("gp", lambda e, th=th, t=t: e.dma_start(
                    out=xb[:], in_=io["xn1"][512 * th + 128 * t:512 * th + 128 * (t + 1), :]),
                    writes=["xb"], dma=True, stream="x")
                for a in range(32):
                    tb = a % 2
                    e_transpose(p, tpp[tb][:], xb[:, 128 * a:128 * (a + 1)],
                                identb[:], f"tp{tb}", ["xb", "identb"])
                    if a % 2:
                        p.op("dv", lambda e, a=a, t=t, tb=tb: e.tensor_copy(
                            out=xTh[:, a, 128 * t:128 * (t + 1)], in_=tpp[tb][:]),
                            reads=[f"tp{tb}"], writes=["xTh"])
                    else:
                        p.op("ac", lambda e, a=a, t=t, tb=tb: e.copy(
                            out=xTh[:, a, 128 * t:128 * (t + 1)], in_=tpp[tb][:]),
                            reads=[f"tp{tb}"], writes=["xTh"])
            p.hard_fence()
        with ExitStack() as ph:
            alp = lambda name, shape, dt: ph.enter_context(nc.psum_tensor(name, shape, dt))
            zp = alp(f"ld_zp_{th}", [16, 512], F32)
            e_mm_group(p, zp[:], [(c1tb[:, a, :], xTh[:, a, :]) for a in range(32)],
                       "zp", ["c1tb", "xTh"])
            p.op("ac", lambda e: e.copy(out=ztb[:], in_=zp[:]),
                 reads=["zp"], writes=["ztb"])
            p.hard_fence()
        with ExitStack() as ph:
            alp = lambda name, shape, dt: ph.enter_context(nc.psum_tensor(name, shape, dt))
            ip = alp(f"ld_ip_{th}", [128, 512], F32)
            pt = alp(f"ld_pt_{th}", [64, 128], F32)
            p.op("dv", lambda e: e.memset(bmax[:, 56:64], 0.0), writes=["bmax"])
            for tt in range(4):
                for j5 in range(28):
                    e_mm_group(p, ip[:], [(ztb[:, 128 * tt:128 * (tt + 1)],
                                           c2tb[:, 512 * j5:512 * (j5 + 1)])],
                               "ip", ["ztb", "c2tb"])
                    for half in range(2):
                        p.op("dv", lambda e, j5=j5, half=half: e.reduce_max(
                            out=bmax[:, 2 * j5 + half:2 * j5 + half + 1],
                            in_=ip[:, 256 * half:256 * (half + 1)], axis=AX.X),
                            reads=["ip"], writes=["bmax"])
                e_transpose(p, pt[:], bmax[:, 0:64], identf[:], "pt",
                            ["bmax", "identf"])
                p.op("dv", lambda e, tt=tt: e.tensor_copy(
                    out=bmaxT[:, 128 * tt:128 * (tt + 1)], in_=pt[:]),
                    reads=["pt"], writes=["bmaxT"])
            p.op("gp", lambda e: e.dma_start(out=bmaxd[:, :], in_=bmaxT[:]),
                 reads=["bmaxT"], writes=["hbm_bmaxd"], dma=True, stream="bm")
            p.hard_fence()
        for jq in range(NJQ):
            with ExitStack() as ph:
                alp = lambda name, shape, dt: ph.enter_context(nc.psum_tensor(name, shape, dt))
                hp_ = [alp(f"ld_hp_{th}_{jq}_0", [128, 512], F32),
                       alp(f"ld_hp_{th}_{jq}_1", [128, 512], F32)]
                ip2_ = [alp(f"ld_ip2_{th}_{jq}_0", [128, 512], F32),
                        alp(f"ld_ip2_{th}_{jq}_1", [128, 512], F32)]
                bb = alp(f"ld_bb_{th}_{jq}", [128, 512], F32)
                for jc in range(JPQ):
                    jg = JPQ * jq + jc
                    wbi = jc % 2
                    p.op("gp", lambda e, jg=jg, wbi=wbi: e.dma_start(
                        out=w1b[wbi][:], in_=io["w1_s"][jg]),
                        writes=[f"w1b{wbi}"], dma=True, stream=f"w1{wbi}")
                    hp = hp_[jc % 2]
                    ip2 = ip2_[jc % 2]
                    e_mm_group(p, hp[:], [(w1b[wbi][:, a, :], xTh[:, a, :])
                                          for a in range(32)],
                               f"hp{jc % 2}", [f"w1b{wbi}", "xTh"])
                    e_mm_group(p, ip2[:], [(c2tb[:, 128 * jg:128 * (jg + 1)], ztb[:])],
                               f"ip2{jc % 2}", ["c2tb", "ztb"])
                    if jg % 2 == 0:
                        bri = (jg // 2) % 2
                        p.op("sync", lambda e, jg=jg, bri=bri: e.dma_start(
                            out=brow[bri][:], in_=bmaxd[jg // 2:jg // 2 + 1, :]),
                            reads=["hbm_bmaxd"], writes=[f"brow{bri}"],
                            dma=True, stream="br")
                        e_mm_group(p, bb[:], [(ones128f[:], brow[bri][:])],
                                   "bb", ["o128", f"brow{bri}"])
                        p.op("ac", lambda e: e.copy(out=bbs[:], in_=bb[:]),
                             reads=["bb"], writes=["bbs"])
                        p.op("dv", lambda e: e.tensor_scalar_mul(
                            out=bbadj[:], in0=bbs[:], scalar1=-1.0),
                            reads=["bbs"], writes=["bbadj"])
                        p.op("dv", lambda e: e.tensor_max(
                            out=bbadj[:], in0=bbadj[:], in1=bbs[:]),
                            reads=["bbadj", "bbs"], writes=["bbadj"])
                        p.op("dv", lambda e: e.scalar_tensor_tensor(
                            out=bbadj[:], in0=bbadj[:], scalar=-1e-5, in1=bbs[:],
                            op0=ALU.mult, op1=ALU.add),
                            reads=["bbadj", "bbs"], writes=["bbadj"])
                    p.op("ac", lambda e, ip2=ip2: e.copy(out=iv[:], in_=ip2[:]),
                         reads=[f"ip2{jc % 2}"], writes=["iv"])
                    p.op("dv", lambda e: e.tensor_tensor(
                        out=mk[:], in0=iv[:], in1=bbadj[:], op=ALU.is_ge),
                        reads=["iv", "bbadj"], writes=["mk"])
                    p.op("ac", lambda e, hp=hp: e.activation(out=hr[:], in_=hp[:],
                                                      func=AF.Relu),
                        reads=[f"hp{jc % 2}"], writes=["hr"])
                    p.op("dv", lambda e, jc=jc: e.tensor_mul(
                        out=hmT[:, jc, :], in0=hr[:], in1=mk[:]),
                        reads=["hr", "mk"], writes=["hmT"])
                p.hard_fence()
            with ExitStack() as ph:
                alp = lambda name, shape, dt: ph.enter_context(nc.psum_tensor(name, shape, dt))
                po = [alp(f"ld_po{m}_{th}_{jq}", [128, 512], F32) for m in range(8)]
                for og in range(4):
                    for jc in range(JPQ):
                        jg = JPQ * jq + jc
                        wbi = jc % 2
                        p.op("gp", lambda e, og=og, jg=jg, wbi=wbi: e.dma_start(
                            out=w2b[wbi][:], in_=io["w2t_s"][og, jg]),
                            writes=[f"w2b{wbi}"], dma=True, stream=f"w2{wbi}")

                        def build(e, jc=jc, wbi=wbi):
                            inst = None
                            for m in range(8):
                                inst = e.matmul(po[m][:], w2b[wbi][:, m, :],
                                                hmT[:, jc, :],
                                                start=(jc == 0), stop=(jc == JPQ - 1))
                            return inst
                        p.op("pe", build, reads=[f"w2b{wbi}", "hmT"],
                             writes=[f"po{m}" for m in range(8)])
                    for m in range(8):
                        ot = 8 * og + m
                        if jq == 0:
                            p.op("dv", lambda e, m=m, ot=ot: e.tensor_copy(
                                out=oacc[:, ot, :], in_=po[m][:]),
                                reads=[f"po{m}"], writes=["oacc"])
                        else:
                            p.op("dv", lambda e, m=m, ot=ot: e.tensor_add(
                                out=oacc[:, ot, :], in0=oacc[:, ot, :], in1=po[m][:]),
                                reads=[f"po{m}", "oacc"], writes=["oacc"])
                p.hard_fence()
        with ExitStack() as ph:
            alp = lambda name, shape, dt: ph.enter_context(nc.psum_tensor(name, shape, dt))
            tpf = [alp(f"ld_tpf0_{th}", [128, 128], F32), alp(f"ld_tpf1_{th}", [128, 128], F32)]
            for ot in range(32):
                sbf = 0
                p.op("sync", lambda e, th=th, ot=ot, sbf=sbf: e.dma_start(
                    out=x1p[sbf][:],
                    in_=io["x1"][512 * th:512 * (th + 1), 128 * ot:128 * (ot + 1)]
                    .rearrange("(pc pp) o -> pp pc o", pp=128)),
                    writes=[f"x1p{sbf}"], dma=True, stream="xp"),
                p.op("sync", lambda e, th=th, ot=ot, sbf=sbf: e.dma_start(
                    out=a1p[sbf][:],
                    in_=io["a1"][512 * th:512 * (th + 1), 128 * ot:128 * (ot + 1)]
                    .rearrange("(pc pp) o -> pp pc o", pp=128)),
                    writes=[f"a1p{sbf}"], dma=True, stream="ap")
                for pc in range(4):
                    tb = pc % 2
                    e_transpose(p, tpf[tb][:], oacc[:, ot, 128 * pc:128 * (pc + 1)],
                                identf[:], f"tpf{tb}", ["oacc", "identf"])
                    p.op("dv", lambda e, pc=pc, tb=tb, sbf=sbf: e.tensor_copy(
                        out=stg[sbf][:, pc, :], in_=tpf[tb][:]),
                        reads=[f"tpf{tb}"], writes=[f"stg{sbf}"])
                p.op("dv", lambda e, sbf=sbf: e.tensor_add(
                    out=stg[sbf][:], in0=stg[sbf][:], in1=x1p[sbf][:]),
                    reads=[f"stg{sbf}", f"x1p{sbf}"], writes=[f"stg{sbf}"])
                p.op("dv", lambda e, sbf=sbf: e.tensor_add(
                    out=stg[sbf][:], in0=stg[sbf][:], in1=a1p[sbf][:]),
                    reads=[f"stg{sbf}", f"a1p{sbf}"], writes=[f"stg{sbf}"])
                p.op("gp", lambda e, th=th, ot=ot, sbf=sbf: e.dma_start(
                    out=io["out"][512 * th:512 * (th + 1), 128 * ot:128 * (ot + 1)]
                    .rearrange("(pc pp) o -> pp pc o", pp=128),
                    in_=stg[sbf][:]),
                    reads=[f"stg{sbf}"], writes=["hbm_out"], dma=True, stream="out")
            p.hard_fence()
    p.finish(["hbm_out"])


def build_LD():
    nc = bass.Bass("TRN2", target_bir_lowering=False)
    io = {}
    def inp(name, shape):
        io[name] = nc.dram_tensor(name, list(shape), F32, kind="ExternalInput").ap()
    inp("xn1", (TOK, D)); inp("x1", (TOK, D)); inp("a1", (TOK, D))
    inp("c1t_r", (128, 32, 16)); inp("c2t", (16, 14336))
    inp("w1_s", (112, 128, 32, 128))
    inp("w2t_s", (4, 112, 128, 8, 128))
    inp("identf", (128, 128)); inp("identb", (128, 128)); inp("ones128", (1, 128))
    io["bmaxd"] = nc.dram_tensor("bmaxd", [64, 512], F32, kind="Internal").ap()
    io["out"] = nc.dram_tensor("out", [TOK, D], F32, kind="ExternalOutput").ap()
    with ExitStack() as st:
        p = Prog(nc, st)
        emit_ffn1(nc, p, st, io)
    return nc


def host_prep_LD(inputs):
    c1 = inputs["L1_ffn_c1_w"].astype(np.float32)
    c2 = inputs["L1_ffn_c2_w"].astype(np.float32)
    w1 = inputs["L1_ffn_w1"].astype(np.float32)
    w2t = inputs["L1_ffn_w2t"].astype(np.float32)
    shared = {
        "c1t_r": np.ascontiguousarray(c1.T.reshape(32, 128, 16).transpose(1, 0, 2)),
        "c2t": np.ascontiguousarray(c2.T),
        "w1_s": stage_stream(w1),
        "w2t_s": np.ascontiguousarray(
            w2t.reshape(112, 128, 4, 8, 128).transpose(2, 0, 1, 3, 4)),
        "identf": np.eye(128, dtype=np.float32),
        "identb": np.eye(128, dtype=np.float32),
        "ones128": np.ones((1, 128), np.float32),
    }
    return shared



def kernel_multilaunch(**inputs):
    x = np.ascontiguousarray(np.asarray(inputs["L0_attn_x"], np.float32)
                             .reshape(S, D))
    shA, pcA = host_prep_LA(inputs)
    shB = host_prep_LB(inputs)
    shC = host_prep_LC(inputs)
    shD = host_prep_LD(inputs)
    core_ids = list(range(NCORE))

    ncA = build_LA()
    in_maps = [dict(shA, **pcA[c]) for c in range(NCORE)]
    resA = run_bass_kernel_spmd(ncA, in_maps, core_ids=core_ids)
    a0 = [np.ascontiguousarray(np.asarray(resA.results[c]["a0"], np.float32))
          for c in range(NCORE)]

    ncB = build_LB()
    in_maps = [dict(shB, x0o=np.ascontiguousarray(x[TOK * c:TOK * (c + 1)]),
                    a0=a0[c]) for c in range(NCORE)]
    resB = run_bass_kernel_spmd(ncB, in_maps, core_ids=core_ids)
    x1 = [np.ascontiguousarray(np.asarray(resB.results[c]["x1"], np.float32))
          for c in range(NCORE)]
    xn1 = [np.ascontiguousarray(np.asarray(resB.results[c]["xn1"], np.float32))
           for c in range(NCORE)]

    ncC = build_LC()
    in_maps = [dict(shC, xn1=xn1[c]) for c in range(NCORE)]
    resC = run_bass_kernel_spmd(ncC, in_maps, core_ids=core_ids)
    a1 = [np.ascontiguousarray(np.asarray(resC.results[c]["a1"], np.float32))
          for c in range(NCORE)]

    ncD = build_LD()
    in_maps = [dict(shD, xn1=xn1[c], x1=x1[c], a1=a1[c]) for c in range(NCORE)]
    resD = run_bass_kernel_spmd(ncD, in_maps, core_ids=core_ids)
    out = np.concatenate([np.asarray(resD.results[c]["out"], np.float32)
                          for c in range(NCORE)], axis=0)
    return np.ascontiguousarray(out.reshape(1, S, D))



def build_fused():
    nc = bass.Bass("TRN2", target_bir_lowering=False)
    io = {}
    def inp(name, shape):
        io[name] = nc.dram_tensor(name, list(shape), F32, kind="ExternalInput").ap()
    inp("x0h", (HALO, D)); inp("kmask", (128, HALO // 128))
    inp("cosq", (128, TOK)); inp("sinq", (128, TOK))
    inp("cosk", (128, HALO)); inp("sink", (128, HALO))
    inp("alph", (128, 1024))
    inp("wq_s", (32, 128, 32, 128)); inp("wk_s", (1, 128, 32, 128))
    inp("wv_r", (128, 32, 128)); inp("wo_s", (32, 128, 32, 128))
    inp("onesb", (128, 128))
    inp("x0o", (TOK, D)); inp("w_s", (32, 128, 32, 128))
    inp("wqf_r", (128, 32, 256)); inp("wkf_r", (128, 32, 256))
    inp("af_r", (128, 32, 128)); inp("bmr", (128, D)); inp("bcol", (128, 32))
    for nm in ("lam_re", "lam_im", "p_re", "p_im", "q_re", "q_im",
               "ct_re", "ct_im", "bp"):
        inp(nm, (64, 1))
    inp("vct_re", (64, 64)); inp("vct_im", (64, 64))
    inp("dco", (1, 1)); inp("lstep", (1, 1))
    inp("wre", (1, D)); inp("wim", (1, D))
    inp("cre", (64, 64)); inp("cim", (64, 64))
    for nm in ("e64re", "e64im", "e64imn", "e64bre", "e64bimn", "twre", "twim"):
        inp(nm, (64, 64))
    inp("ones64", (1, 64)); inp("ones128", (1, 128)); inp("jrev", (128, 128))
    inp("c1t_r", (128, 32, 16)); inp("c2t", (16, 14336))
    inp("w1_s", (112, 128, 32, 128)); inp("w2t_s", (4, 112, 128, 8, 128))
    inp("identf", (128, 128)); inp("identb", (128, 128))
    for nm in ("a0", "x1", "xn1", "a1"):
        io[nm] = nc.dram_tensor(nm, [TOK, D], F32, kind="Internal").ap()
    io["kpad"] = nc.dram_tensor("kpad", [4352], F32, kind="Internal").ap()
    io["scr4"] = nc.dram_tensor("scr4", [8, D], F32, kind="Internal").ap()
    io["scr"] = None
    io["bmaxd"] = nc.dram_tensor("bmaxd", [64, 512], F32, kind="Internal").ap()
    io["out"] = nc.dram_tensor("out", [TOK, D], F32, kind="ExternalOutput").ap()
    with ExitStack() as st:
        p = Prog(nc, st)
        with ExitStack() as s1:
            emit_attn0(nc, p, s1, io)
            p.hard_fence()
        with ExitStack() as s2:
            emit_ffn0(nc, p, s2, io)
            p.hard_fence()
        with ExitStack() as s3:
            emit_s4(nc, p, s3, io)
            p.hard_fence()
        with ExitStack() as s4:
            emit_ffn1(nc, p, s4, io)
            p.hard_fence()
        p.finish(["hbm_out"])
    return nc


def kernel_fused(**inputs):
    x = np.ascontiguousarray(np.asarray(inputs["L0_attn_x"], np.float32)
                             .reshape(S, D))
    shA, pcA = host_prep_LA(inputs)
    shB = host_prep_LB(inputs)
    shC = host_prep_LC(inputs)
    shD = host_prep_LD(inputs)
    nc = build_fused()
    in_maps = []
    for c in range(NCORE):
        m = dict(shA)
        m.update(pcA[c])
        m.update(shB)
        m.update(shC)
        m.update(shD)
        m["x0o"] = np.ascontiguousarray(x[TOK * c:TOK * (c + 1)])
        in_maps.append(m)
    res = run_bass_kernel_spmd(nc, in_maps, core_ids=list(range(NCORE)))
    out = np.concatenate([np.asarray(res.results[c]["out"], np.float32)
                          for c in range(NCORE)], axis=0)
    return np.ascontiguousarray(out.reshape(1, S, D))


def kernel(**inputs):
    return kernel_fused(**inputs)


# version 05
# speedup vs baseline: 1.1172x; 1.0171x over previous
import numpy as np
import concourse.bass as bass
import concourse.mybir as mybir
from concourse.bass_utils import run_bass_kernel_spmd
from contextlib import ExitStack

F32 = mybir.dt.float32
BF16 = mybir.dt.bfloat16
AF = mybir.ActivationFunctionType
ALU = mybir.AluOpType
AX = mybir.AxisListType

NCORE = 8
S = 8192
D = 4096
TOK = S // NCORE
HALO = TOK + 512
NH = 32
DH = 128
WIN = 512
SCALE = 1.0 / float(np.sqrt(DH))
EPS = 1e-5


class Prog:

    ENGS = ("sync", "gp", "pe", "dv", "ac")

    def __init__(self, nc, stack):
        self.nc = nc
        self.stack = stack
        self.sem = {}
        self.cnt = {}
        self.waited = {e: {} for e in self.ENGS}
        for e in ("pe", "dv", "ac"):
            self.sem[e] = stack.enter_context(nc.semaphore(f"S_{e}"))
            self.cnt[e] = 0
        self.eng = {"sync": nc.sync, "gp": nc.gpsimd, "pe": nc.tensor,
                    "dv": nc.vector, "ac": nc.scalar}
        self.state = {}

    def _stream_sem(self, key):
        if key not in self.sem:
            self.sem[key] = self.stack.enter_context(
                self.nc.semaphore(f"S_{key.replace(':', '_')}"))
            self.cnt[key] = 0
        return self.sem[key]

    def _wait(self, e, semkey, val):
        if val > self.waited[e].get(semkey, 0):
            self.eng[e].wait_ge(self.sem[semkey], val)
            self.waited[e][semkey] = val

    def op(self, e, build, reads=(), writes=(), dma=False, stream=None):
        need = {}

        def add(dep):
            if dep is None:
                return
            dk, dc = dep
            if dc > need.get(dk, 0):
                need[dk] = dc

        for k in reads:
            st = self.state.get(k)
            if st:
                add(st[0])
        for k in writes:
            st = self.state.get(k)
            if st:
                add(st[0])
                for rk, rc in st[1].items():
                    add((rk, rc))
        if dma:
            semkey = f"{e}:{stream or 'main'}"
            self._stream_sem(semkey)
            add((semkey, self.cnt[semkey]))
        else:
            semkey = e
        for dk, dc in sorted(need.items()):
            self._wait(e, dk, dc)
        inst = build(self.eng[e])
        inc = 16 if dma else 1
        inst.then_inc(self.sem[semkey], inc)
        self.cnt[semkey] += inc
        me = (semkey, self.cnt[semkey])
        for k in reads:
            st = self.state.setdefault(k, [None, {}])
            st[1][semkey] = self.cnt[semkey]
        for k in writes:
            self.state[k] = [me, {}]
        return inst

    def hard_fence(self):
        for e in self.ENGS:
            for sk, c in list(self.cnt.items()):
                if c > 0:
                    self._wait(e, sk, c)

    def finish(self, out_keys):
        need = {}
        for k in out_keys:
            st = self.state.get(k)
            if st and st[0]:
                dk, dc = st[0]
                need[dk] = max(need.get(dk, 0), dc)
        for dk, dc in sorted(need.items()):
            self._wait("sync", dk, dc)



def e_ln(p, x_f32, xn_f32, tmp_col, key_x, key_xn, key_tmp, n=D, eps_col=None):
    p.op("dv", lambda e: e.reduce_sum(out=tmp_col[:, 0:1], in_=x_f32[:], axis=AX.X),
         reads=[key_x], writes=[key_tmp])
    p.op("dv", lambda e: e.tensor_scalar_mul(out=tmp_col[:, 0:1], in0=tmp_col[:, 0:1],
                                             scalar1=1.0 / n),
         reads=[key_tmp], writes=[key_tmp])
    p.op("dv", lambda e: e.tensor_scalar(out=xn_f32[:], in0=x_f32[:],
                                         scalar1=tmp_col[:, 0:1], scalar2=None,
                                         op0=ALU.subtract),
         reads=[key_x, key_tmp], writes=[key_xn])
    p.op("ac", lambda e: e.activation(out=x_f32[:], in_=xn_f32[:], func=AF.Square),
         reads=[key_xn], writes=[key_x])
    p.op("dv", lambda e: e.reduce_sum(out=tmp_col[:, 1:2], in_=x_f32[:], axis=AX.X),
         reads=[key_x], writes=[key_tmp])
    p.op("ac", lambda e: e.activation(out=tmp_col[:, 1:2], in_=tmp_col[:, 1:2],
                                      func=AF.Sqrt, scale=1.0 / n,
                                      bias=eps_col),
         reads=[key_tmp, "epscol"], writes=[key_tmp])
    p.op("dv", lambda e: e.reciprocal(out=tmp_col[:, 1:2], in_=tmp_col[:, 1:2]),
         reads=[key_tmp], writes=[key_tmp])
    p.op("dv", lambda e: e.tensor_scalar(out=xn_f32[:], in0=xn_f32[:],
                                         scalar1=tmp_col[:, 1:2], scalar2=None,
                                         op0=ALU.mult),
         reads=[key_xn, key_tmp], writes=[key_xn])


def e_mm_group(p, psum_ap, steps, key_psum, read_keys):
    n = len(steps)

    def build(e):
        inst = None
        for i, (lh, rh) in enumerate(steps):
            inst = e.matmul(psum_ap, lh, rh, start=(i == 0), stop=(i == n - 1))
        return inst

    return p.op("pe", build, reads=list(read_keys), writes=[key_psum])


def e_transpose(p, out_psum, in_sbuf, ident, key_out, read_keys):
    return p.op("pe", lambda e: e.transpose(out_psum, in_sbuf, ident),
                reads=list(read_keys), writes=[key_out])



def stage_stream(W):
    O, K = W.shape
    return np.ascontiguousarray(
        W.reshape(O // 128, 128, K // 128, 128).transpose(0, 3, 2, 1))


def stage_rhs(WT, m):
    K, M = WT.shape
    assert M == m
    return np.ascontiguousarray(WT.reshape(K // 128, 128, M).transpose(1, 0, 2))


def rope_cache_T(S_, Dh):
    inv_freq = 1.0 / (10000.0 ** (np.arange(0, Dh, 2, dtype=np.float32) / Dh))
    t = np.arange(S_, dtype=np.float32)
    f = np.outer(t, inv_freq)
    cos = np.zeros((S_, Dh), dtype=np.float32)
    sin = np.zeros((S_, Dh), dtype=np.float32)
    cos[:, 0::2] = np.cos(f)
    cos[:, 1::2] = np.cos(f)
    sin[:, 0::2] = np.sin(f)
    sin[:, 1::2] = np.sin(f)
    return np.ascontiguousarray(cos.T), np.ascontiguousarray(sin.T)



def e_rope(p, src, cos, sin, rt, t1, out_bf, keys, n, off=0):
    ksrc, kcos, ksin, krt, kt1, kout = keys
    p.op("dv", lambda e: e.tensor_scalar_mul(out=rt[0:64, 0:n],
                                             in0=src[64:128, off:off + n],
                                             scalar1=-1.0),
         reads=[ksrc], writes=[krt])
    p.op("ac", lambda e: e.copy(out=rt[64:128, 0:n], in_=src[0:64, off:off + n]),
         reads=[ksrc], writes=[krt])
    p.op("dv", lambda e: e.tensor_mul(out=t1[:, 0:n], in0=src[:, off:off + n],
                                      in1=cos[:, off:off + n]),
         reads=[ksrc, kcos], writes=[kt1])
    p.op("dv", lambda e: e.tensor_mul(out=rt[:, 0:n], in0=rt[:, 0:n],
                                      in1=sin[:, off:off + n]),
         reads=[krt, ksin], writes=[krt])
    p.op("dv", lambda e: e.tensor_add(out=out_bf[:, off:off + n], in0=t1[:, 0:n],
                                      in1=rt[:, 0:n]),
         reads=[kt1, krt], writes=[kout])


def emit_attn0(nc, p, st, io):
    sb = lambda name, shape, dt: st.enter_context(nc.sbuf_tensor(name, shape, dt))
    NT = HALO // 128

    identf = sb("la_idf", [128, 128], F32)
    identb = sb("la_idb", [128, 128], BF16)
    onesb = sb("la_ones", [128, 128], BF16)
    kmask = sb("la_kmask", [128, NT], F32)
    for nm, t in (("identf", identf), ("kmask", kmask)):
        p.op("sync", lambda e, nm=nm, t=t: e.dma_start(out=t[:], in_=io[nm][:, :]),
             writes=[nm], dma=True, stream="cst")
    for nm, t in (("identb", identb), ("onesb", onesb)):
        p.op("gp", lambda e, nm=nm, t=t: e.dma_start(out=t[:], in_=io[nm][:, :]),
             writes=[nm], dma=True, stream="cstb")

    krT = sb("la_krT", [128, HALO], BF16)
    v_s = sb("la_v", [128, NT, 128], BF16)
    blend = sb("la_blend", [128, NH, TOK], BF16)
    xnT = sb("la_xnT", [128, 32, HALO], BF16)

    with ExitStack() as ph:
        al = lambda name, shape, dt: ph.enter_context(nc.sbuf_tensor(name, shape, dt))
        alp = lambda name, shape, dt: ph.enter_context(nc.psum_tensor(name, shape, dt))
        x0b = [al("la_x00", [128, D], F32)]
        xnb = [al("la_xn0", [128, D], F32)]
        xbfb = [al("la_xb0", [128, D], BF16)]
        coltmp = al("la_col", [128, 3], F32)
        p.op("dv", lambda e: e.memset(coltmp[:, 2:3], EPS), writes=["epscol"])
        tpp = [alp("la_tp0", [128, 128], BF16), alp("la_tp1", [128, 128], BF16)]
        for t in range(NT):
            b = 0
            p.op("sync", lambda e, t=t, b=b: e.dma_start(
                out=x0b[b][:], in_=io["x0h"][128 * t:128 * (t + 1), :]),
                writes=[f"x0b{b}"], dma=True, stream="x")
            e_ln(p, x0b[b][:], xnb[b][:], coltmp, f"x0b{b}", f"xnb{b}", "coltmp",
                 eps_col=coltmp[:, 2:3])
            p.op("ac", lambda e, b=b: e.copy(out=xbfb[b][:], in_=xnb[b][:]),
                 reads=[f"xnb{b}"], writes=[f"xbf{b}"])
            for a in range(32):
                tb = a % 2
                e_transpose(p, tpp[tb][:], xbfb[b][:, 128 * a:128 * (a + 1)],
                            identb[:], f"tp{tb}", [f"xbf{b}", "identb"])
                if a % 2:
                    p.op("dv", lambda e, a=a, t=t, tb=tb: e.tensor_copy(
                        out=xnT[:, a, 128 * t:128 * (t + 1)], in_=tpp[tb][:]),
                        reads=[f"tp{tb}"], writes=["xnT"])
                else:
                    p.op("ac", lambda e, a=a, t=t, tb=tb: e.copy(
                        out=xnT[:, a, 128 * t:128 * (t + 1)], in_=tpp[tb][:]),
                        reads=[f"tp{tb}"], writes=["xnT"])
        p.hard_fence()

    with ExitStack() as ph:
        al = lambda name, shape, dt: ph.enter_context(nc.sbuf_tensor(name, shape, dt))
        alp = lambda name, shape, dt: ph.enter_context(nc.psum_tensor(name, shape, dt))
        cosk = al("la_cosk", [128, HALO], F32)
        sink = al("la_sink", [128, HALO], F32)
        wkb = al("la_wk0b", [128, 32, 128], BF16)
        wvb = al("la_wv0b", [128, 32, 128], BF16)
        kps_ = [alp("la_kps0", [128, 512], F32), alp("la_kps1", [128, 512], F32)]
        ktf = al("la_ktf", [128, HALO], F32)
        rt_t = al("la_rt", [128, 512], F32)
        t1_t = al("la_t1", [128, 512], F32)
        vps_ = [alp("la_vps0", [128, 128], F32), alp("la_vps1", [128, 128], F32)]
        p.op("sync", lambda e: e.dma_start(out=cosk[:], in_=io["cosk"][:, :]),
             writes=["cosk"], dma=True, stream="cst")
        p.op("sync", lambda e: e.dma_start(out=sink[:], in_=io["sink"][:, :]),
             writes=["sink"], dma=True, stream="cst")
        p.op("gp", lambda e: e.dma_start(out=wkb[:], in_=io["wk_s"][0]),
             writes=["wkb"], dma=True, stream="w")
        p.op("gp", lambda e: e.dma_start(out=wvb[:], in_=io["wv_r"][:, :, :]),
             writes=["wvb"], dma=True, stream="w")
        for c in range(3):
            kps = kps_[c % 2]
            e_mm_group(p, kps[:, 0:512],
                       [(wkb[:, a, :], xnT[:, a, 512 * c:512 * (c + 1)])
                        for a in range(32)],
                       f"kps{c % 2}", ["wkb", "xnT"])
            p.op("ac", lambda e, c=c, kps=kps: e.copy(
                out=ktf[:, 512 * c:512 * (c + 1)], in_=kps[:, 0:512]),
                 reads=[f"kps{c % 2}"], writes=["ktf"])
        for c in range(3):
            e_rope(p, ktf, cosk, sink, rt_t, t1_t, krT,
                   ("ktf", "cosk", "sink", "rt_t", "t1_t", "krT"), 512, off=512 * c)
        for t in range(NT):
            vps = vps_[t % 2]
            e_mm_group(p, vps[:],
                       [(xnT[:, a, 128 * t:128 * (t + 1)], wvb[:, a, :])
                        for a in range(32)],
                       f"vps{t % 2}", ["xnT", "wvb"])
            p.op("ac", lambda e, t=t, vps=vps: e.copy(out=v_s[:, t, :], in_=vps[:]),
                 reads=[f"vps{t % 2}"], writes=["v_s"])
        p.hard_fence()

    with ExitStack() as ph:
        al = lambda name, shape, dt: ph.enter_context(nc.sbuf_tensor(name, shape, dt))
        alp = lambda name, shape, dt: ph.enter_context(nc.psum_tensor(name, shape, dt))
        cosq = al("la_cosq", [128, TOK], F32)
        sinq = al("la_sinq", [128, TOK], F32)
        alph = al("la_alph", [128, 1024], F32)
        wqb = [al("la_wqb0", [128, 32, 128], BF16)]
        qps_ = [alp("la_qps0", [128, 512], F32), alp("la_qps1", [128, 512], F32)]
        qf = al("la_qf", [128, TOK], F32)
        qrT = al("la_qrT", [128, TOK], BF16)
        rt_t = al("la_rt2", [128, 512], F32)
        t1_t = al("la_t12", [128, 512], F32)
        scp = [alp("la_sc0", [128, 512], F32), alp("la_sc1", [128, 512], F32)]
        sig = al("la_sig", [128, 4, 512], BF16)
        rsp_ = [alp("la_rsp0", [128, 512], F32), alp("la_rsp1", [128, 512], F32)]
        wop_ = [alp("la_wop0", [128, 512], F32), alp("la_wop1", [128, 512], F32)]
        rrec = al("la_rrec", [128, 512], F32)
        wout = [al("la_wo0", [128, 512], F32), al("la_wo1", [128, 512], F32)]
        for nm, t in (("cosq", cosq), ("sinq", sinq), ("alph", alph)):
            p.op("sync", lambda e, nm=nm, t=t: e.dma_start(out=t[:], in_=io[nm][:, :]),
                 writes=[nm], dma=True, stream="cst")
        for h in range(NH):
            wb = 0
            p.op("gp", lambda e, h=h, wb=wb: e.dma_start(out=wqb[wb][:], in_=io["wq_s"][h]),
                 writes=[f"wqb{wb}"], dma=True, stream="w")
            for c in range(2):
                qps = qps_[c]
                e_mm_group(p, qps[:, 0:512],
                           [(wqb[wb][:, a, :], xnT[:, a, 256 + 512 * c:256 + 512 * (c + 1)])
                            for a in range(32)],
                           f"qps{c}", [f"wqb{wb}", "xnT"])
                p.op("ac", lambda e, c=c, qps=qps: e.copy(
                    out=qf[:, 512 * c:512 * (c + 1)], in_=qps[:, 0:512]),
                     reads=[f"qps{c}"], writes=["qf"])
            for c in range(2):
                e_rope(p, qf, cosq, sinq, rt_t, t1_t, qrT,
                       ("qf", "cosq", "sinq", "rt_t", "t1_t", "qrT"), 512, off=512 * c)
            for r in range(5):
                q0 = max(0, 256 * (r - 1))
                qlen = 256 if r in (0, 4) else 512
                rw = r % 2
                for kc in range(4):
                    b = kc % 2
                    e_mm_group(p, scp[b][:, 0:qlen],
                               [(krT[:, 256 * r + 128 * kc:256 * r + 128 * (kc + 1)],
                                 qrT[:, q0:q0 + qlen])],
                               f"scp{b}", ["krT", "qrT"])
                    p.op("ac", lambda e, r=r, kc=kc, b=b, qlen=qlen: e.activation(
                        out=sig[:, kc, 0:qlen], in_=scp[b][:, 0:qlen],
                        func=AF.Sigmoid,
                        bias=kmask[:, 2 * r + kc:2 * r + kc + 1], scale=SCALE),
                        reads=[f"scp{b}", "kmask"], writes=[f"sig{kc}"])
                rsp = rsp_[rw]
                wop = wop_[rw]
                e_mm_group(p, rsp[:, 0:qlen],
                           [(onesb[:], sig[:, kc, 0:qlen]) for kc in range(4)],
                           f"rsp{rw}", ["onesb", "sig0", "sig1", "sig2", "sig3"])
                e_mm_group(p, wop[:, 0:qlen],
                           [(v_s[:, 2 * r + kc, :], sig[:, kc, 0:qlen])
                            for kc in range(4)],
                           f"wop{rw}", ["v_s", "sig0", "sig1", "sig2", "sig3"])
                p.op("dv", lambda e, rsp=rsp, qlen=qlen: e.tensor_scalar_max(
                    out=rrec[:, 0:qlen], in0=rsp[:, 0:qlen], scalar1=1e-30),
                    reads=[f"rsp{rw}"], writes=["rrec"])
                p.op("dv", lambda e, qlen=qlen: e.reciprocal(
                    out=rrec[:, 0:qlen], in_=rrec[:, 0:qlen]),
                    reads=["rrec"], writes=["rrec"])
                p.op("dv", lambda e, rw=rw, wop=wop, qlen=qlen: e.tensor_mul(
                    out=wout[rw][:, 0:qlen], in0=wop[:, 0:qlen], in1=rrec[:, 0:qlen]),
                    reads=[f"wop{rw}", "rrec"], writes=[f"wout{rw}"])
                if r >= 1:
                    j = r - 1
                    offp = 0 if j == 0 else 256
                    pw = (r - 1) % 2
                    p.op("dv", lambda e, rw=rw, pw=pw, offp=offp: e.tensor_sub(
                        out=t1_t[:, 0:256], in0=wout[rw][:, 0:256],
                        in1=wout[pw][:, offp:offp + 256]),
                        reads=[f"wout{rw}", f"wout{pw}"], writes=["t1_t"])
                    p.op("dv", lambda e, j=j: e.tensor_mul(
                        out=t1_t[:, 0:256], in0=t1_t[:, 0:256],
                        in1=alph[:, 256 * j:256 * (j + 1)]),
                        reads=["t1_t", "alph"], writes=["t1_t"])
                    p.op("dv", lambda e, h=h, j=j, pw=pw, offp=offp: e.tensor_add(
                        out=blend[:, h, 256 * j:256 * (j + 1)], in0=t1_t[:, 0:256],
                        in1=wout[pw][:, offp:offp + 256]),
                        reads=["t1_t", f"wout{pw}"], writes=["blend"])
        p.hard_fence()

    with ExitStack() as ph:
        al = lambda name, shape, dt: ph.enter_context(nc.sbuf_tensor(name, shape, dt))
        alp = lambda name, shape, dt: ph.enter_context(nc.psum_tensor(name, shape, dt))
        wob = [al("la_woAb", [128, 32, 128], BF16), al("la_woBb", [128, 32, 128], BF16)]
        aop_ = [alp("la_aop0", [128, 512], F32), alp("la_aop1", [128, 512], F32)]
        aof_ = [al("la_aof0", [128, 512], F32), al("la_aof1", [128, 512], F32)]
        tpf = [alp("la_tpf0", [128, 128], F32), alp("la_tpf1", [128, 128], F32)]
        stage = al("la_stage", [128, 4, 128], F32)
        for oc in range(32):
            b = oc % 2
            p.op("gp", lambda e, oc=oc, b=b: e.dma_start(
                out=wob[b][:], in_=io["wo_s"][oc]),
                writes=[f"wob{b}"], dma=True, stream=f"wo{b}")
            for qc in range(2):
                aop = aop_[qc]
                aof = aof_[qc]
                e_mm_group(p, aop[:, 0:512],
                           [(wob[b][:, a, :], blend[:, a, 512 * qc:512 * (qc + 1)])
                            for a in range(32)],
                           f"aop{qc}", [f"wob{b}", "blend"])
                p.op("ac", lambda e, aop=aop, aof=aof: e.copy(out=aof[:],
                                                              in_=aop[:, 0:512]),
                     reads=[f"aop{qc}"], writes=[f"aof{qc}"])
                for pc in range(4):
                    tb = pc % 2
                    e_transpose(p, tpf[tb][:], aof[:, 128 * pc:128 * (pc + 1)],
                                identf[:], f"tpf{tb}", [f"aof{qc}", "identf"])
                    p.op("dv", lambda e, pc=pc, tb=tb: e.tensor_copy(
                        out=stage[:, pc, :], in_=tpf[tb][:]),
                        reads=[f"tpf{tb}"], writes=["stage"])
                p.op("gp", lambda e, oc=oc, qc=qc: e.dma_start(
                    out=io["a0"][512 * qc:512 * (qc + 1), 128 * oc:128 * (oc + 1)]
                    .rearrange("(pc pp) o -> pp pc o", pp=128),
                    in_=stage[:]),
                    reads=["stage"], writes=["hbm_a0"], dma=True, stream="out")
        p.hard_fence()
    p.finish(["hbm_a0"])



def build_LA():
    nc = bass.Bass("TRN2", target_bir_lowering=False)
    io = {}
    def inp(name, shape):
        io[name] = nc.dram_tensor(name, list(shape), F32, kind="ExternalInput").ap()
    inp("x0h", (HALO, D))
    inp("kmask", (128, HALO // 128))
    inp("cosq", (128, TOK)); inp("sinq", (128, TOK))
    inp("cosk", (128, HALO)); inp("sink", (128, HALO))
    inp("alph", (128, 1024)); inp("omal", (128, 1024))
    inp("wq_s", (32, 128, 32, 128))
    inp("wk_s", (1, 128, 32, 128))
    inp("wv_r", (128, 32, 128))
    inp("wo_s", (32, 128, 32, 128))
    inp("identf", (128, 128)); inp("onesb", (128, 128)); inp("identb", (128, 128))
    io["a0"] = nc.dram_tensor("a0", [TOK, D], F32, kind="ExternalOutput").ap()
    with ExitStack() as st:
        p = Prog(nc, st)
        emit_attn0(nc, p, st, io)
    return nc


def host_prep_LA(inputs):
    x = np.ascontiguousarray(inputs["L0_attn_x"].reshape(S, D).astype(np.float32))
    wq = inputs["L0_attn_Wq"].astype(np.float32)
    wk = inputs["L0_attn_Wk"].astype(np.float32)
    wv = inputs["L0_attn_Wv"].astype(np.float32)
    wo = inputs["L0_attn_Wo"].astype(np.float32)
    cosT, sinT = rope_cache_T(S, DH)
    cosP = np.pad(cosT, ((0, 0), (256, 256)))
    sinP = np.pad(sinT, ((0, 0), (256, 256)))
    xP = np.pad(x, ((256, 256), (0, 0)))
    shared = {
        "wq_s": stage_stream(wq),
        "wk_s": stage_stream(wk).reshape(1, 128, 32, 128),
        "wv_r": stage_rhs(np.ascontiguousarray(wv.T), 128),
        "wo_s": stage_stream(wo),
        "identf": np.eye(128, dtype=np.float32),
        "identb": np.eye(128, dtype=np.float32),
        "onesb": np.ones((128, 128), dtype=np.float32),
    }
    alpha = (np.arange(256, dtype=np.float32) / 255.0)
    percore = []
    for c in range(NCORE):
        h0 = 1024 * c
        kmask = np.zeros((128, HALO // 128), dtype=np.float32)
        for t in range(HALO // 128):
            rows = h0 - 256 + 128 * t + np.arange(128)
            kmask[:, t] = np.where((rows >= 0) & (rows < S), 0.0, -30000.0)
        al = np.tile(alpha, (128, 4)).reshape(128, 1024)
        om = 1.0 - al
        if c == 0:
            al[:, 0:256] = 1.0
            om[:, 0:256] = 0.0
        percore.append({
            "x0h": np.ascontiguousarray(xP[h0:h0 + HALO]),
            "kmask": kmask,
            "cosq": np.ascontiguousarray(cosP[:, h0 + 256:h0 + 256 + TOK]),
            "sinq": np.ascontiguousarray(sinP[:, h0 + 256:h0 + 256 + TOK]),
            "cosk": np.ascontiguousarray(cosP[:, h0:h0 + HALO]),
            "sink": np.ascontiguousarray(sinP[:, h0:h0 + HALO]),
            "alph": np.ascontiguousarray(al),
            "omal": np.ascontiguousarray(om),
        })
    return shared, percore



def emit_ffn0(nc, p, st, io):
    sb = lambda name, shape, dt: st.enter_context(nc.sbuf_tensor(name, shape, dt))
    NT8 = TOK // 128

    identb = sb("lb_idb", [128, 128], BF16)
    identf = sb("lb_idf", [128, 128], F32)
    p.op("gp", lambda e: e.dma_start(out=identb[:], in_=io["identb"][:, :]),
         writes=["identb"], dma=True, stream="cstb")
    p.op("sync", lambda e: e.dma_start(out=identf[:], in_=io["identf"][:, :]),
         writes=["identf"], dma=True, stream="cst")
    xnT = sb("lb_xnT", [128, 32, TOK], BF16)
    whT = sb("lb_whT", [128, NT8, 128], BF16)
    bcol = sb("lb_bcol", [128, 32], F32)
    p.op("sync", lambda e: e.dma_start(out=bcol[:], in_=io["bcol"][:, :]),
         writes=["bcol"], dma=True, stream="cst")

    with ExitStack() as ph:
        al = lambda name, shape, dt: ph.enter_context(nc.sbuf_tensor(name, shape, dt))
        alp = lambda name, shape, dt: ph.enter_context(nc.psum_tensor(name, shape, dt))
        x0 = al("lb_x0", [128, D], F32)
        xn = al("lb_xn", [128, D], F32)
        xb = al("lb_xb", [128, D], BF16)
        coltmp = al("lb_col", [128, 3], F32)
        p.op("dv", lambda e: e.memset(coltmp[:, 2:3], EPS), writes=["epscol"])
        tpp = [alp("lb_tp0", [128, 128], BF16), alp("lb_tp1", [128, 128], BF16)]
        for t in range(NT8):
            p.op("sync", lambda e, t=t: e.dma_start(
                out=x0[:], in_=io["x0o"][128 * t:128 * (t + 1), :]),
                writes=["x0b0"], dma=True, stream="x")
            e_ln(p, x0[:], xn[:], coltmp, "x0b0", "xnb0", "coltmp",
                 eps_col=coltmp[:, 2:3])
            p.op("ac", lambda e: e.copy(out=xb[:], in_=xn[:]),
                 reads=["xnb0"], writes=["xbf0"])
            for a in range(32):
                tb = a % 2
                e_transpose(p, tpp[tb][:], xb[:, 128 * a:128 * (a + 1)],
                            identb[:], f"tp{tb}", ["xbf0", "identb"])
                if a % 2:
                    p.op("dv", lambda e, a=a, t=t, tb=tb: e.tensor_copy(
                        out=xnT[:, a, 128 * t:128 * (t + 1)], in_=tpp[tb][:]),
                        reads=[f"tp{tb}"], writes=["xnT"])
                else:
                    p.op("ac", lambda e, a=a, t=t, tb=tb: e.copy(
                        out=xnT[:, a, 128 * t:128 * (t + 1)], in_=tpp[tb][:]),
                        reads=[f"tp{tb}"], writes=["xnT"])
        p.hard_fence()

    with ExitStack() as ph:
        al = lambda name, shape, dt: ph.enter_context(nc.sbuf_tensor(name, shape, dt))
        alp = lambda name, shape, dt: ph.enter_context(nc.psum_tensor(name, shape, dt))
        wqfb = al("lb_wqfb", [128, 32, 256], BF16)
        wkfb = al("lb_wkfb", [128, 32, 256], BF16)
        afb = al("lb_afb", [128, 32, 128], BF16)
        p.op("gp", lambda e: e.dma_start(out=wqfb[:], in_=io["wqf_r"][:, :, :]),
             writes=["wqfb"], dma=True, stream="w")
        p.op("gp", lambda e: e.dma_start(out=wkfb[:], in_=io["wkf_r"][:, :, :]),
             writes=["wkfb"], dma=True, stream="w")
        p.op("gp", lambda e: e.dma_start(out=afb[:], in_=io["af_r"][:, :, :]),
             writes=["afb"], dma=True, stream="w")
        qp = alp("lb_qp", [128, 256], F32)
        kp = alp("lb_kp", [128, 256], F32)
        hp = alp("lb_hp", [128, 128], F32)
        tpb = alp("lb_tpb", [128, 128], BF16)
        qf = al("lb_qf", [128, 256], F32)
        kf = al("lb_kf", [128, 256], F32)
        s8 = al("lb_s8", [128, 8], F32)
        scol = al("lb_scol", [128, 2], F32)
        hf = al("lb_hf", [128, 128], F32)
        wexp = al("lb_wexp", [128, 128], F32)
        whb = al("lb_whb", [128, 128], BF16)
        for t in range(NT8):
            e_mm_group(p, qp[:], [(xnT[:, a, 128 * t:128 * (t + 1)], wqfb[:, a, :])
                                  for a in range(32)], "qp", ["xnT", "wqfb"])
            p.op("ac", lambda e: e.copy(out=qf[:], in_=qp[:]),
                 reads=["qp"], writes=["qf"])
            e_mm_group(p, kp[:], [(xnT[:, a, 128 * t:128 * (t + 1)], wkfb[:, a, :])
                                  for a in range(32)], "kp", ["xnT", "wkfb"])
            p.op("ac", lambda e: e.copy(out=kf[:], in_=kp[:]),
                 reads=["kp"], writes=["kf"])
            p.op("dv", lambda e: e.tensor_mul(out=qf[:], in0=qf[:], in1=kf[:]),
                 reads=["qf", "kf"], writes=["qf"])
            for e8 in range(8):
                p.op("dv", lambda e, e8=e8: e.reduce_sum(
                    out=s8[:, e8:e8 + 1], in_=qf[:, 32 * e8:32 * (e8 + 1)], axis=AX.X),
                    reads=["qf"], writes=["s8"])
            p.op("dv", lambda e: e.tensor_scalar_mul(
                out=s8[:], in0=s8[:], scalar1=1.0 / float(np.sqrt(32.0))),
                reads=["s8"], writes=["s8"])
            p.op("dv", lambda e: e.reduce_max(out=scol[:, 0:1], in_=s8[:], axis=AX.X),
                 reads=["s8"], writes=["scol"])
            p.op("ac", lambda e: e.mul(out=scol[:, 0:1], in_=scol[:, 0:1], mul=-1.0),
                 reads=["scol"], writes=["scol"])
            p.op("ac", lambda e: e.activation(out=s8[:], in_=s8[:], func=AF.Exp,
                                              bias=scol[:, 0:1]),
                 reads=["s8", "scol"], writes=["s8"])
            p.op("dv", lambda e: e.reduce_sum(out=scol[:, 1:2], in_=s8[:], axis=AX.X),
                 reads=["s8"], writes=["scol"])
            p.op("dv", lambda e: e.reciprocal(out=scol[:, 1:2], in_=scol[:, 1:2]),
                 reads=["scol"], writes=["scol"])
            p.op("dv", lambda e: e.tensor_scalar(out=s8[:], in0=s8[:],
                                                 scalar1=scol[:, 1:2], scalar2=None,
                                                 op0=ALU.mult),
                 reads=["s8", "scol"], writes=["s8"])
            e_mm_group(p, hp[:], [(xnT[:, a, 128 * t:128 * (t + 1)], afb[:, a, :])
                                  for a in range(32)], "hp", ["xnT", "afb"])
            p.op("ac", lambda e: e.copy(out=hf[:], in_=hp[:]),
                 reads=["hp"], writes=["hf"])
            for r_ in range(16):
                p.op("ac", lambda e, r_=r_: e.copy(out=wexp[:, 8 * r_:8 * (r_ + 1)],
                                                   in_=s8[:]),
                     reads=["s8"], writes=["wexp"])
            p.op("dv", lambda e: e.tensor_mul(out=whb[:], in0=hf[:], in1=wexp[:]),
                 reads=["hf", "wexp"], writes=["whb"])
            e_transpose(p, tpb[:], whb[:], identb[:], "tpb", ["whb", "identb"])
            p.op("dv", lambda e, t=t: e.tensor_copy(out=whT[:, t, :], in_=tpb[:]),
                 reads=["tpb"], writes=["whT"])
        p.hard_fence()

    with ExitStack() as ph:
        al = lambda name, shape, dt: ph.enter_context(nc.sbuf_tensor(name, shape, dt))
        alp = lambda name, shape, dt: ph.enter_context(nc.psum_tensor(name, shape, dt))
        bmrb = al("lb_bmrb", [128, D], BF16)
        p.op("gp", lambda e: e.dma_start(out=bmrb[:], in_=io["bmr"][:, :]),
             writes=["bmrb"], dma=True, stream="w")
        wb = [al("lb_wb0", [128, 32, 128], BF16), al("lb_wb1", [128, 32, 128], BF16)]
        fp_ = [alp("lb_fp0", [128, 512], F32), alp("lb_fp1", [128, 512], F32)]
        ff_ = [al("lb_ff0", [128, 512], F32), al("lb_ff1", [128, 512], F32)]
        tpf = [alp("lb_tpf0", [128, 128], F32), alp("lb_tpf1", [128, 128], F32)]
        stg = [al("lb_st0", [128, 4, 128], F32), al("lb_st1", [128, 4, 128], F32)]
        x0p = [al("lb_x0p0", [128, 4, 128], F32), al("lb_x0p1", [128, 4, 128], F32)]
        a0p = [al("lb_a0p0", [128, 4, 128], F32), al("lb_a0p1", [128, 4, 128], F32)]
        for oc in range(32):
            b = oc % 2
            p.op("gp", lambda e, oc=oc, b=b: e.dma_start(
                out=wb[b][:], in_=io["w_s"][oc]),
                writes=[f"wb{b}"], dma=True, stream=f"wb{b}")
            for tc in range(2):
                sbf = (2 * oc + tc) % 2
                fp = fp_[sbf]
                ff = ff_[sbf]
                steps = [(wb[b][:, a, :], xnT[:, a, 512 * tc:512 * (tc + 1)])
                         for a in range(32)]
                steps.append((bmrb[:, 128 * oc:128 * (oc + 1)],
                              whT[:, 4 * tc:4 * (tc + 1), :]))
                e_mm_group(p, fp[:, 0:512], steps, f"fp{sbf}",
                           [f"wb{b}", "xnT", "bmrb", "whT"])
                p.op("dv", lambda e, oc=oc, fp=fp, ff=ff: e.tensor_scalar(
                    out=ff[:], in0=fp[:, 0:512], scalar1=bcol[:, oc:oc + 1],
                    scalar2=None, op0=ALU.add),
                    reads=[f"fp{sbf}", "bcol"], writes=[f"ff{sbf}"])
                p.op("sync", lambda e, oc=oc, tc=tc, sbf=sbf: e.dma_start(
                    out=x0p[sbf][:],
                    in_=io["x0o"][512 * tc:512 * (tc + 1), 128 * oc:128 * (oc + 1)]
                    .rearrange("(pc pp) o -> pp pc o", pp=128)),
                    writes=[f"x0p{sbf}"], dma=True, stream="xp")
                p.op("sync", lambda e, oc=oc, tc=tc, sbf=sbf: e.dma_start(
                    out=a0p[sbf][:],
                    in_=io["a0"][512 * tc:512 * (tc + 1), 128 * oc:128 * (oc + 1)]
                    .rearrange("(pc pp) o -> pp pc o", pp=128)),
                    writes=[f"a0p{sbf}"], dma=True, stream="ap")
                for pc in range(4):
                    tb = pc % 2
                    e_transpose(p, tpf[tb][:], ff[:, 128 * pc:128 * (pc + 1)],
                                identf[:], f"tpf{tb}", [f"ff{sbf}", "identf"])
                    p.op("dv", lambda e, pc=pc, tb=tb, sbf=sbf: e.tensor_copy(
                        out=stg[sbf][:, pc, :], in_=tpf[tb][:]),
                        reads=[f"tpf{tb}"], writes=[f"stg{sbf}"])
                p.op("dv", lambda e, sbf=sbf: e.tensor_add(
                    out=stg[sbf][:], in0=stg[sbf][:], in1=x0p[sbf][:]),
                    reads=[f"stg{sbf}", f"x0p{sbf}"], writes=[f"stg{sbf}"])
                p.op("dv", lambda e, sbf=sbf: e.tensor_add(
                    out=stg[sbf][:], in0=stg[sbf][:], in1=a0p[sbf][:]),
                    reads=[f"stg{sbf}", f"a0p{sbf}"], writes=[f"stg{sbf}"])
                p.op("gp", lambda e, oc=oc, tc=tc, sbf=sbf: e.dma_start(
                    out=io["x1"][512 * tc:512 * (tc + 1), 128 * oc:128 * (oc + 1)]
                    .rearrange("(pc pp) o -> pp pc o", pp=128),
                    in_=stg[sbf][:]),
                    reads=[f"stg{sbf}"], writes=["hbm_x1"], dma=True, stream="out")
        p.hard_fence()

    with ExitStack() as ph:
        al = lambda name, shape, dt: ph.enter_context(nc.sbuf_tensor(name, shape, dt))
        x1t = al("lb_x1t", [128, D], F32)
        xn1t = al("lb_xn1t", [128, D], F32)
        coltmp = al("lb_col2", [128, 3], F32)
        p.op("dv", lambda e: e.memset(coltmp[:, 2:3], EPS), writes=["epscol2"])
        for t in range(NT8):
            p.op("sync", lambda e, t=t: e.dma_start(
                out=x1t[:], in_=io["x1"][128 * t:128 * (t + 1), :]),
                reads=["hbm_x1"], writes=["x1t"], dma=True, stream="x")
            e_ln(p, x1t[:], xn1t[:], coltmp, "x1t", "xn1t", "coltmp2",
                 eps_col=coltmp[:, 2:3])
            p.op("gp", lambda e, t=t: e.dma_start(
                out=io["xn1"][128 * t:128 * (t + 1), :], in_=xn1t[:]),
                reads=["xn1t"], writes=["hbm_xn1"], dma=True, stream="out")
        p.hard_fence()
    p.finish(["hbm_x1", "hbm_xn1"])


def build_LB():
    nc = bass.Bass("TRN2", target_bir_lowering=False)
    io = {}
    def inp(name, shape):
        io[name] = nc.dram_tensor(name, list(shape), F32, kind="ExternalInput").ap()
    inp("x0o", (TOK, D)); inp("a0", (TOK, D))
    inp("w_s", (32, 128, 32, 128))
    inp("wqf_r", (128, 32, 256)); inp("wkf_r", (128, 32, 256))
    inp("af_r", (128, 32, 128)); inp("bmr", (128, D)); inp("bcol", (128, 32))
    inp("identf", (128, 128)); inp("identb", (128, 128))
    io["x1"] = nc.dram_tensor("x1", [TOK, D], F32, kind="ExternalOutput").ap()
    io["xn1"] = nc.dram_tensor("xn1", [TOK, D], F32, kind="ExternalOutput").ap()
    with ExitStack() as st:
        p = Prog(nc, st)
        emit_ffn0(nc, p, st, io)
    return nc


def host_prep_LB(inputs):
    w = inputs["L0_ffn_W"].astype(np.float32)
    bv = inputs["L0_ffn_b"].astype(np.float32)
    wqf = inputs["L0_ffn_Wq"].astype(np.float32)
    wkf = inputs["L0_ffn_Wk"].astype(np.float32)
    A = inputs["L0_ffn_A"].astype(np.float32)
    Bm = inputs["L0_ffn_Bm"].astype(np.float32)
    A2 = np.ascontiguousarray(A.transpose(1, 0, 2).reshape(128, D))
    Bmr = np.ascontiguousarray(Bm.transpose(2, 0, 1).reshape(128, D))
    shared = {
        "w_s": stage_stream(w),
        "wqf_r": stage_rhs(np.ascontiguousarray(wqf.T), 256),
        "wkf_r": stage_rhs(np.ascontiguousarray(wkf.T), 256),
        "af_r": stage_rhs(np.ascontiguousarray(A2.T), 128),
        "bmr": Bmr,
        "bcol": np.ascontiguousarray(bv.reshape(32, 128).T),
        "identf": np.eye(128, dtype=np.float32),
        "identb": np.eye(128, dtype=np.float32),
    }
    return shared



def _cmul(p, outre, outim, are, aim, bre, bim, t1, keys, sub=False):
    kor, koi, kar, kai, kbr, kbi, kt1 = keys
    p.op("dv", lambda e: e.tensor_mul(out=t1, in0=aim, in1=bim),
         reads=[kai, kbi], writes=[kt1])
    p.op("dv", lambda e: e.tensor_mul(out=outre, in0=are, in1=bre),
         reads=[kar, kbr], writes=[kor])
    p.op("dv", lambda e: e.tensor_sub(out=outre, in0=outre, in1=t1),
         reads=[kor, kt1], writes=[kor])
    p.op("dv", lambda e: e.tensor_mul(out=t1, in0=aim, in1=bre),
         reads=[kai, kbr], writes=[kt1])
    p.op("dv", lambda e: e.tensor_mul(out=outim, in0=are, in1=bim),
         reads=[kar, kbi], writes=[koi])
    p.op("dv", lambda e: e.tensor_add(out=outim, in0=outim, in1=t1),
         reads=[koi, kt1], writes=[koi])


def emit_s4(nc, p, st, io):
    sb = lambda name, shape, dt: st.enter_context(nc.sbuf_tensor(name, shape, dt))
    kpad = io["kpad"]
    scr = io["scr"]

    identf = sb("lc_idf", [128, 128], F32)
    identb = sb("lc_idb", [128, 128], BF16)
    jrev = sb("lc_jrev", [128, 128], F32)
    p.op("sync", lambda e: e.dma_start(out=identf[:], in_=io["identf"][:, :]),
         writes=["identf"], dma=True, stream="cst")
    p.op("gp", lambda e: e.dma_start(out=identb[:], in_=io["identb"][:, :]),
         writes=["identb"], dma=True, stream="cstb")
    p.op("sync", lambda e: e.dma_start(out=jrev[:], in_=io["jrev"][:, :]),
         writes=["jrev"], dma=True, stream="cst")
    dcol = sb("lc_dcol", [128, 1], F32)

    with ExitStack() as ph:
        al = lambda name, shape, dt: ph.enter_context(nc.sbuf_tensor(name, shape, dt))
        alp = lambda name, shape, dt: ph.enter_context(nc.psum_tensor(name, shape, dt))
        o64 = al("lc_o64", [1, 64], F32)
        o128 = al("lc_o128", [1, 128], F32)
        sc1 = al("lc_sc1", [1, 1], F32)
        scol = al("lc_scol", [64, 1], F32)
        small = {}
        for nm in ("lam_re", "lam_im", "p_re", "p_im", "q_re", "q_im",
                   "ct_re", "ct_im", "bp"):
            small[nm] = al("lc_" + nm, [64, 1], F32)
        vct_re = al("lc_vctre", [64, 64], F32)
        vct_im = al("lc_vctim", [64, 64], F32)
        for nm in list(small) + ["vct_re", "vct_im"]:
            t = small.get(nm) or {"vct_re": vct_re, "vct_im": vct_im}[nm]
            p.op("sync", lambda e, nm=nm, t=t: e.dma_start(out=t[:], in_=io[nm][:, :]),
                 writes=[nm], dma=True, stream="cst")
        p.op("sync", lambda e: e.dma_start(out=o64[:], in_=io["ones64"][:, :]),
             writes=["o64"], dma=True, stream="cst")
        p.op("sync", lambda e: e.dma_start(out=o128[:], in_=io["ones128"][:, :]),
             writes=["o128"], dma=True, stream="cst")
        p.op("sync", lambda e: e.dma_start(out=sc1[:], in_=io["lstep"][:, :]),
             writes=["sc1"], dma=True, stream="cst")
        p.op("sync", lambda e: e.dma_start(out=dcol[0:1, :], in_=io["dco"][:, :]),
             writes=["dcol"], dma=True, stream="cst")
        pbd = alp("lc_pbd", [128, 1], F32)
        e_mm_group(p, pbd[:], [(o128[:], dcol[0:1, 0:1])], "pbd", ["o128", "dcol"])
        p.op("dv", lambda e: e.tensor_copy(out=dcol[:], in_=pbd[:]),
             reads=["pbd"], writes=["dcol"])
        p.op("ac", lambda e: e.activation(out=sc1[:], in_=sc1[:], func=AF.Exp),
             reads=["sc1"], writes=["sc1"])
        p.op("dv", lambda e: e.reciprocal(out=sc1[:], in_=sc1[:]),
             reads=["sc1"], writes=["sc1"])
        p.op("ac", lambda e: e.mul(out=sc1[:], in_=sc1[:], mul=2.0),
             reads=["sc1"], writes=["sc1"])
        psc = alp("lc_psc", [64, 1], F32)
        e_mm_group(p, psc[:], [(o64[:], sc1[:])], "psc", ["o64", "sc1"])
        p.op("dv", lambda e: e.tensor_copy(out=scol[:], in_=psc[:]),
             reads=["psc"], writes=["scol"])
        bre = al("lc_bre", [64, 1], F32)
        bim = al("lc_bim", [64, 1], F32)
        pb2 = alp("lc_pb2", [64, 1], F32)
        e_mm_group(p, pb2[:], [(vct_re[:], small["bp"][:])], "pb2", ["vct_re", "bp"])
        p.op("dv", lambda e: e.tensor_copy(out=bre[:], in_=pb2[:]),
             reads=["pb2"], writes=["bre"])
        e_mm_group(p, pb2[:], [(vct_im[:], small["bp"][:])], "pb2", ["vct_im", "bp"])
        p.op("dv", lambda e: e.tensor_copy(out=bim[:], in_=pb2[:]),
             reads=["pb2"], writes=["bim"])
        vre = al("lc_vre", [64, 4], F32)
        vim = al("lc_vim", [64, 4], F32)
        nvre = al("lc_nvre", [64, 4], F32)
        tt1 = al("lc_tt1", [64, 1], F32)
        combos = [("ct_re", "ct_im", bre, bim, "bre", "bim"),
                  ("ct_re", "ct_im", small["p_re"], small["p_im"], "p_re", "p_im"),
                  ("q_re", "q_im", bre, bim, "bre", "bim"),
                  ("q_re", "q_im", small["p_re"], small["p_im"], "p_re", "p_im")]
        for j, (ur, ui, wr_t, wi_t, kwr, kwi) in enumerate(combos):
            urt, uit = small[ur], small[ui]
            p.op("dv", lambda e, urt=urt, wr_t=wr_t, j=j: e.tensor_mul(
                out=vre[:, j:j + 1], in0=urt[:], in1=wr_t[:]),
                reads=[ur, kwr], writes=["vre"])
            p.op("dv", lambda e, uit=uit, wi_t=wi_t: e.tensor_mul(
                out=tt1[:], in0=uit[:], in1=wi_t[:]),
                reads=[ui, kwi], writes=["tt1"])
            p.op("dv", lambda e, j=j: e.tensor_add(
                out=vre[:, j:j + 1], in0=vre[:, j:j + 1], in1=tt1[:]),
                reads=["vre", "tt1"], writes=["vre"])
            p.op("dv", lambda e, urt=urt, wi_t=wi_t, j=j: e.tensor_mul(
                out=vim[:, j:j + 1], in0=urt[:], in1=wi_t[:]),
                reads=[ur, kwi], writes=["vim"])
            p.op("dv", lambda e, uit=uit, wr_t=wr_t: e.tensor_mul(
                out=tt1[:], in0=uit[:], in1=wr_t[:]),
                reads=[ui, kwr], writes=["tt1"])
            p.op("dv", lambda e, j=j: e.tensor_sub(
                out=vim[:, j:j + 1], in0=vim[:, j:j + 1], in1=tt1[:]),
                reads=["vim", "tt1"], writes=["vim"])
        p.op("dv", lambda e: e.tensor_scalar_mul(out=nvre[:], in0=vre[:], scalar1=-1.0),
             reads=["vre"], writes=["nvre"])
        wrow_re = al("lc_wrowre", [1, D], F32)
        wrow_im = al("lc_wrowim", [1, D], F32)
        p.op("sync", lambda e: e.dma_start(out=wrow_re[:], in_=io["wre"][:, :]),
             writes=["wrow_re"], dma=True, stream="cst")
        p.op("sync", lambda e: e.dma_start(out=wrow_im[:], in_=io["wim"][:, :]),
             writes=["wrow_im"], dma=True, stream="cst")
        dre = al("lc_dre", [64, D], F32)
        dim = al("lc_dim", [64, D], F32)
        rec = al("lc_rec", [64, D], F32)
        pwb = alp("lc_pwb", [64, 512], F32)
        for c in range(8):
            sl = slice(512 * c, 512 * (c + 1))
            e_mm_group(p, pwb[:], [(o64[:], wrow_re[:, sl])], "pwb",
                       ["o64", "wrow_re"])
            p.op("dv", lambda e, sl=sl: e.tensor_scalar(
                out=dre[:, sl], in0=pwb[:], scalar1=scol[:, 0:1],
                scalar2=small["lam_re"][:, 0:1], op0=ALU.mult, op1=ALU.subtract),
                reads=["pwb", "scol", "lam_re"], writes=["dre"])
            e_mm_group(p, pwb[:], [(o64[:], wrow_im[:, sl])], "pwb",
                       ["o64", "wrow_im"])
            p.op("dv", lambda e, sl=sl: e.tensor_scalar(
                out=dim[:, sl], in0=pwb[:], scalar1=scol[:, 0:1],
                scalar2=small["lam_im"][:, 0:1], op0=ALU.mult, op1=ALU.subtract),
                reads=["pwb", "scol", "lam_im"], writes=["dim"])
        p.op("ac", lambda e: e.activation(out=rec[:], in_=dre[:], func=AF.Square),
             reads=["dre"], writes=["recA"])
        sq2 = al("lc_sq2", [64, D], F32)
        p.op("ac", lambda e: e.activation(out=sq2[:], in_=dim[:], func=AF.Square),
             reads=["dim"], writes=["sq2"])
        p.op("dv", lambda e: e.tensor_add(out=rec[:], in0=rec[:], in1=sq2[:]),
             reads=["recA", "sq2"], writes=["recB"])
        p.op("dv", lambda e: e.reciprocal(out=rec[:], in_=rec[:]),
             reads=["recB"], writes=["recC"])
        p.op("dv", lambda e: e.tensor_mul(out=dre[:], in0=dre[:], in1=rec[:]),
             reads=["dre", "recC"], writes=["idre"])
        p.op("dv", lambda e: e.tensor_mul(out=dim[:], in0=dim[:], in1=rec[:]),
             reads=["dim", "recC"], writes=["idimN"])
        kre = al("lc_kre", [4, D], F32)
        kim = al("lc_kim", [4, D], F32)
        pk = alp("lc_pk", [4, 512], F32)
        for c in range(8):
            sl = slice(512 * c, 512 * (c + 1))
            e_mm_group(p, pk[:], [(vre[:], dre[:, sl]), (vim[:], dim[:, sl])],
                       "pk", ["vre", "vim", "idre", "idimN"])
            p.op("dv", lambda e, sl=sl: e.tensor_copy(out=kre[:, sl], in_=pk[:]),
                 reads=["pk"], writes=["kre"])
            e_mm_group(p, pk[:], [(vim[:], dre[:, sl]), (nvre[:], dim[:, sl])],
                       "pk", ["vim", "nvre", "idre", "idimN"])
            p.op("dv", lambda e, sl=sl: e.tensor_copy(out=kim[:, sl], in_=pk[:]),
                 reads=["pk"], writes=["kim"])
        scr4 = io["scr4"]
        p.op("gp", lambda e: e.dma_start(out=scr4[0:4, :], in_=kre[:]),
             reads=["kre"], writes=["hbm_scr4a"], dma=True, stream="sc")
        p.op("gp", lambda e: e.dma_start(out=scr4[4:8, :], in_=kim[:]),
             reads=["kim"], writes=["hbm_scr4b"], dma=True, stream="sc")
        k64 = {}
        for j, nm in enumerate(("k00", "k01", "k10", "k11")):
            for ri, part in enumerate(("re", "im")):
                t = al(f"lc_{nm}{part}", [64, 64], F32)
                k64[nm + part] = t
                p.op("sync", lambda e, j=j, ri=ri, t=t: e.dma_start(
                    out=t[:], in_=scr4[4 * ri + j].rearrange("(pp f) -> pp f", f=64)),
                    reads=["hbm_scr4a", "hbm_scr4b"], writes=[nm + part],
                    dma=True, stream="k64")
        crow_re = al("lc_crowre", [64, 64], F32)
        crow_im = al("lc_crowim", [64, 64], F32)
        p.op("sync", lambda e: e.dma_start(out=crow_re[:], in_=io["cre"][:, :]),
             writes=["crow_re"], dma=True, stream="cst")
        p.op("sync", lambda e: e.dma_start(out=crow_im[:], in_=io["cim"][:, :]),
             writes=["crow_im"], dma=True, stream="cst")
        zre = al("lc_zre", [64, 64], F32)
        zim = al("lc_zim", [64, 64], F32)
        t64 = al("lc_t64", [64, 64], F32)
        den = al("lc_den", [64, 64], F32)
        _cmul(p, zre[:], zim[:], k64["k01re"][:], k64["k01im"][:],
              k64["k10re"][:], k64["k10im"][:], t64[:],
              ("zre", "zim", "k01re", "k01im", "k10re", "k10im", "t64"))
        p.op("dv", lambda e: e.tensor_scalar_add(out=k64["k11re"][:],
                                                 in0=k64["k11re"][:], scalar1=1.0),
             reads=["k11re"], writes=["k11re"])
        p.op("ac", lambda e: e.activation(out=den[:], in_=k64["k11re"][:],
                                          func=AF.Square),
             reads=["k11re"], writes=["den"])
        p.op("ac", lambda e: e.activation(out=t64[:], in_=k64["k11im"][:],
                                          func=AF.Square),
             reads=["k11im"], writes=["t64"])
        p.op("dv", lambda e: e.tensor_add(out=den[:], in0=den[:], in1=t64[:]),
             reads=["den", "t64"], writes=["den"])
        p.op("dv", lambda e: e.reciprocal(out=den[:], in_=den[:]),
             reads=["den"], writes=["den"])
        tre = al("lc_tre", [64, 64], F32)
        tim = al("lc_tim", [64, 64], F32)
        p.op("dv", lambda e: e.tensor_mul(out=tre[:], in0=zre[:], in1=k64["k11re"][:]),
             reads=["zre", "k11re"], writes=["tre"])
        p.op("dv", lambda e: e.tensor_mul(out=t64[:], in0=zim[:], in1=k64["k11im"][:]),
             reads=["zim", "k11im"], writes=["t64"])
        p.op("dv", lambda e: e.tensor_add(out=tre[:], in0=tre[:], in1=t64[:]),
             reads=["tre", "t64"], writes=["tre"])
        p.op("dv", lambda e: e.tensor_mul(out=tim[:], in0=zim[:], in1=k64["k11re"][:]),
             reads=["zim", "k11re"], writes=["tim"])
        p.op("dv", lambda e: e.tensor_mul(out=t64[:], in0=zre[:], in1=k64["k11im"][:]),
             reads=["zre", "k11im"], writes=["t64"])
        p.op("dv", lambda e: e.tensor_sub(out=tim[:], in0=tim[:], in1=t64[:]),
             reads=["tim", "t64"], writes=["tim"])
        p.op("dv", lambda e: e.tensor_mul(out=tre[:], in0=tre[:], in1=den[:]),
             reads=["tre", "den"], writes=["tre"])
        p.op("dv", lambda e: e.tensor_mul(out=tim[:], in0=tim[:], in1=den[:]),
             reads=["tim", "den"], writes=["tim"])
        p.op("dv", lambda e: e.tensor_sub(out=tre[:], in0=k64["k00re"][:], in1=tre[:]),
             reads=["k00re", "tre"], writes=["ure"])
        p.op("dv", lambda e: e.tensor_sub(out=tim[:], in0=k64["k00im"][:], in1=tim[:]),
             reads=["k00im", "tim"], writes=["uim"])
        are = al("lc_are", [64, 64], F32)
        aim = al("lc_aim", [64, 64], F32)
        _cmul(p, are[:], aim[:], crow_re[:], crow_im[:], tre[:], tim[:], t64[:],
              ("arere", "areim", "crow_re", "crow_im", "ure", "uim", "t64"))
        e64re = al("lc_e64re", [64, 64], F32)
        e64im = al("lc_e64im", [64, 64], F32)
        e64imn = al("lc_e64imn", [64, 64], F32)
        e64bre = al("lc_e64bre", [64, 64], F32)
        e64bimn = al("lc_e64bimn", [64, 64], F32)
        twre = al("lc_twre", [64, 64], F32)
        twim = al("lc_twim", [64, 64], F32)
        for nm, t in (("e64re", e64re), ("e64im", e64im), ("e64imn", e64imn),
                      ("e64bre", e64bre), ("e64bimn", e64bimn),
                      ("twre", twre), ("twim", twim)):
            p.op("sync", lambda e, nm=nm, t=t: e.dma_start(out=t[:], in_=io[nm][:, :]),
                 writes=[nm], dma=True, stream="cst")
        py = alp("lc_py", [64, 64], F32)
        yre = al("lc_yre", [64, 64], F32)
        yim = al("lc_yim", [64, 64], F32)
        e_mm_group(p, py[:], [(e64re[:], are[:]), (e64imn[:], aim[:])],
                   "py", ["e64re", "e64imn", "arere", "areim"])
        p.op("dv", lambda e: e.tensor_copy(out=yre[:], in_=py[:]),
             reads=["py"], writes=["yre"])
        e_mm_group(p, py[:], [(e64im[:], are[:]), (e64re[:], aim[:])],
                   "py", ["e64im", "e64re", "arere", "areim"])
        p.op("dv", lambda e: e.tensor_copy(out=yim[:], in_=py[:]),
             reads=["py"], writes=["yim"])
        _cmul(p, zre[:], zim[:], yre[:], yim[:], twre[:], twim[:], t64[:],
              ("zre2", "zim2", "yre", "yim", "twre", "twim", "t64"))
        pz = alp("lc_pz", [64, 64], F32)
        ztre = al("lc_ztre", [64, 64], F32)
        ztim = al("lc_ztim", [64, 64], F32)
        e_transpose(p, pz[:], zre[:], identf[0:64, 0:64], "pz", ["zre2", "identf"])
        p.op("dv", lambda e: e.tensor_copy(out=ztre[:], in_=pz[:]),
             reads=["pz"], writes=["ztre"])
        e_transpose(p, pz[:], zim[:], identf[0:64, 0:64], "pz", ["zim2", "identf"])
        p.op("dv", lambda e: e.tensor_copy(out=ztim[:], in_=pz[:]),
             reads=["pz"], writes=["ztim"])
        e_mm_group(p, py[:], [(e64bre[:], ztre[:]), (e64bimn[:], ztim[:])],
                   "py", ["e64bre", "e64bimn", "ztre", "ztim"])
        kmat = al("lc_kmat", [64, 64], F32)
        p.op("dv", lambda e: e.tensor_copy(out=kmat[:], in_=py[:]),
             reads=["py"], writes=["kmat"])
        zt = al("lc_zt", [1, 256], F32)
        p.op("dv", lambda e: e.memset(zt[:], 0.0), writes=["zt"])
        p.op("gp", lambda e: e.dma_start(out=kpad[0:127].rearrange("(a f) -> a f", a=1),
                                         in_=zt[:, 0:127]),
             reads=["zt"], writes=["hbm_kpad"], dma=True, stream="kp")
        p.op("gp", lambda e: e.dma_start(
            out=kpad[4223:4352].rearrange("(a f) -> a f", a=1), in_=zt[:, 0:129]),
            reads=["zt"], writes=["hbm_kpad"], dma=True, stream="kp")
        p.op("gp", lambda e: e.dma_start(
            out=kpad[127:4223].rearrange("(pp f) -> pp f", f=64), in_=kmat[:]),
            reads=["kmat"], writes=["hbm_kpad"], dma=True, stream="kp")
        p.hard_fence()

    with ExitStack() as ph:
        al = lambda name, shape, dt: ph.enter_context(nc.sbuf_tensor(name, shape, dt))
        alp = lambda name, shape, dt: ph.enter_context(nc.psum_tensor(name, shape, dt))
        bd = al("lc_bd", [128, 32, 128], BF16)
        gd = [al("lc_gd0", [128, 128], F32), al("lc_gd1", [128, 128], F32)]
        pbd2 = [alp("lc_pbd0", [128, 128], F32), alp("lc_pbd1", [128, 128], F32)]
        for d_ in range(32):
            b = d_ % 2
            p.op("sync", lambda e, d_=d_, b=b: e.dma_start(
                out=gd[b][:],
                in_=bass.AP(tensor=kpad.tensor, offset=kpad.offset + 128 * d_,
                            ap=[[1, 128], [1, 128]])),
                reads=["hbm_kpad"], writes=[f"gd{b}"], dma=True, stream="kg")
            e_mm_group(p, pbd2[b][:], [(jrev[:], gd[b][:])], f"pbd{b}",
                       ["jrev", f"gd{b}"])
            p.op("dv", lambda e, d_=d_, b=b: e.tensor_copy(out=bd[:, d_, :],
                                                           in_=pbd2[b][:]),
                 reads=[f"pbd{b}"], writes=["bd"])
        xT = al("lc_xT", [128, 32, TOK], BF16)
        xb = al("lc_xb", [128, D], BF16)
        tpp = [alp("lc_tp0", [128, 128], BF16), alp("lc_tp1", [128, 128], BF16)]
        for t in range(TOK // 128):
            p.op("gp", lambda e, t=t: e.dma_start(
                out=xb[:], in_=io["xn1"][128 * t:128 * (t + 1), :]),
                writes=["xb"], dma=True, stream="x")
            for a in range(32):
                tb = a % 2
                e_transpose(p, tpp[tb][:], xb[:, 128 * a:128 * (a + 1)],
                            identb[:], f"tp{tb}", ["xb", "identb"])
                if a % 2:
                    p.op("dv", lambda e, a=a, t=t, tb=tb: e.tensor_copy(
                        out=xT[:, a, 128 * t:128 * (t + 1)], in_=tpp[tb][:]),
                        reads=[f"tp{tb}"], writes=["xT"])
                else:
                    p.op("ac", lambda e, a=a, t=t, tb=tb: e.copy(
                        out=xT[:, a, 128 * t:128 * (t + 1)], in_=tpp[tb][:]),
                        reads=[f"tp{tb}"], writes=["xT"])
        yp_ = [alp("lc_yp0", [128, 512], F32), alp("lc_yp1", [128, 512], F32)]
        yf_ = [al("lc_yf0", [128, 512], F32), al("lc_yf1", [128, 512], F32)]
        tpf = [alp("lc_tpf0", [128, 128], F32), alp("lc_tpf1", [128, 128], F32)]
        stg = [al("lc_st0", [128, 4, 128], F32), al("lc_st1", [128, 4, 128], F32)]
        xnp = [al("lc_xnp0", [128, 4, 128], F32), al("lc_xnp1", [128, 4, 128], F32)]
        for oc in range(32):
            for tc in range(2):
                sbf = (2 * oc + tc) % 2
                yp = yp_[sbf]
                yf = yf_[sbf]
                e_mm_group(p, yp[:, 0:512],
                           [(bd[:, oc - i, :], xT[:, i, 512 * tc:512 * (tc + 1)])
                            for i in range(oc + 1)],
                           f"yp{sbf}", ["bd", "xT"])
                p.op("ac", lambda e, yp=yp, yf=yf: e.copy(out=yf[:], in_=yp[:, 0:512]),
                     reads=[f"yp{sbf}"], writes=[f"yf{sbf}"])
                p.op("sync", lambda e, oc=oc, tc=tc, sbf=sbf: e.dma_start(
                    out=xnp[sbf][:],
                    in_=io["xn1"][512 * tc:512 * (tc + 1), 128 * oc:128 * (oc + 1)]
                    .rearrange("(pc pp) o -> pp pc o", pp=128)),
                    writes=[f"xnp{sbf}"], dma=True, stream="xp")
                for pc in range(4):
                    tb = pc % 2
                    e_transpose(p, tpf[tb][:], yf[:, 128 * pc:128 * (pc + 1)],
                                identf[:], f"tpf{tb}", [f"yf{sbf}", "identf"])
                    p.op("dv", lambda e, pc=pc, tb=tb, sbf=sbf: e.tensor_copy(
                        out=stg[sbf][:, pc, :], in_=tpf[tb][:]),
                        reads=[f"tpf{tb}"], writes=[f"stg{sbf}"])
                p.op("dv", lambda e, sbf=sbf: e.scalar_tensor_tensor(
                    out=stg[sbf][:], in0=xnp[sbf][:], scalar=dcol[:, 0:1],
                    in1=stg[sbf][:], op0=ALU.mult, op1=ALU.add),
                    reads=[f"stg{sbf}", f"xnp{sbf}", "dcol"], writes=[f"stg{sbf}"])
                p.op("gp", lambda e, oc=oc, tc=tc, sbf=sbf: e.dma_start(
                    out=io["a1"][512 * tc:512 * (tc + 1), 128 * oc:128 * (oc + 1)]
                    .rearrange("(pc pp) o -> pp pc o", pp=128),
                    in_=stg[sbf][:]),
                    reads=[f"stg{sbf}"], writes=["hbm_a1"], dma=True, stream="out")
        p.hard_fence()
    p.finish(["hbm_a1"])


def build_LC():
    nc = bass.Bass("TRN2", target_bir_lowering=False)
    io = {}
    def inp(name, shape):
        io[name] = nc.dram_tensor(name, list(shape), F32, kind="ExternalInput").ap()
    inp("xn1", (TOK, D))
    for nm in ("lam_re", "lam_im", "p_re", "p_im", "q_re", "q_im",
               "ct_re", "ct_im", "bp"):
        inp(nm, (64, 1))
    inp("vct_re", (64, 64)); inp("vct_im", (64, 64))
    inp("dco", (1, 1)); inp("lstep", (1, 1))
    inp("wre", (1, D)); inp("wim", (1, D))
    inp("cre", (64, 64)); inp("cim", (64, 64))
    for nm in ("e64re", "e64im", "e64imn", "e64bre", "e64bimn", "twre", "twim"):
        inp(nm, (64, 64))
    inp("ones64", (1, 64)); inp("ones128", (1, 128))
    inp("identf", (128, 128)); inp("identb", (128, 128)); inp("jrev", (128, 128))
    io["kpad"] = nc.dram_tensor("kpad", [4352], F32, kind="Internal").ap()
    io["scr4"] = nc.dram_tensor("scr4", [8, D], F32, kind="Internal").ap()
    io["scr"] = None
    io["a1"] = nc.dram_tensor("a1", [TOK, D], F32, kind="ExternalOutput").ap()
    with ExitStack() as st:
        p = Prog(nc, st)
        emit_s4(nc, p, st, io)
    return nc


def host_prep_LC(inputs):
    def ri(t):
        return np.ascontiguousarray(t[..., 0]), np.ascontiguousarray(t[..., 1])
    lam_re, lam_im = ri(inputs["L1_attn_Lambda_ri"].astype(np.float32))
    p_re, p_im = ri(inputs["L1_attn_p_ri"].astype(np.float32))
    q_re, q_im = ri(inputs["L1_attn_q_ri"].astype(np.float32))
    vc_re, vc_im = ri(inputs["L1_attn_Vc_ri"].astype(np.float32))
    ct_re, ct_im = ri(inputs["L1_attn_Ct_ri"].astype(np.float32).reshape(64, 2)
                      .reshape(1, 64, 2))
    ct_re = ct_re.reshape(64, 1); ct_im = ct_im.reshape(64, 1)
    L = D
    karr = np.arange(L)
    Om = np.exp(-2j * np.pi * (karr / L)).astype(np.complex64)
    w = ((1.0 - Om) / (1.0 + Om)).astype(np.complex64)
    cc = (2.0 / (1.0 + Om)).astype(np.complex64)
    n1 = np.arange(64)
    E64 = np.exp(2j * np.pi * np.outer(n1, n1) / 64.0)
    Tw = np.exp(2j * np.pi * np.outer(n1, n1) / 4096.0)
    shared = {
        "lam_re": lam_re.reshape(64, 1), "lam_im": lam_im.reshape(64, 1),
        "p_re": p_re.reshape(64, 1), "p_im": p_im.reshape(64, 1),
        "q_re": q_re.reshape(64, 1), "q_im": q_im.reshape(64, 1),
        "ct_re": ct_re, "ct_im": ct_im,
        "bp": inputs["L1_attn_Bp"].astype(np.float32).reshape(64, 1),
        "vct_re": np.ascontiguousarray(vc_re.T),
        "vct_im": np.ascontiguousarray(vc_im.T),
        "dco": inputs["L1_attn_D"].astype(np.float32).reshape(1, 1),
        "lstep": inputs["L1_attn_log_step"].astype(np.float32).reshape(1, 1),
        "wre": np.ascontiguousarray(w.real.astype(np.float32)).reshape(1, L),
        "wim": np.ascontiguousarray(w.imag.astype(np.float32)).reshape(1, L),
        "cre": np.ascontiguousarray(cc.real.astype(np.float32)).reshape(64, 64),
        "cim": np.ascontiguousarray(cc.imag.astype(np.float32)).reshape(64, 64),
        "e64re": E64.real.astype(np.float32).copy(),
        "e64im": E64.imag.astype(np.float32).copy(),
        "e64imn": (-E64.imag).astype(np.float32).copy(),
        "e64bre": (E64.real / 4096.0).astype(np.float32).copy(),
        "e64bimn": (-E64.imag / 4096.0).astype(np.float32).copy(),
        "twre": Tw.real.astype(np.float32).copy(),
        "twim": Tw.imag.astype(np.float32).copy(),
        "ones64": np.ones((1, 64), np.float32),
        "ones128": np.ones((1, 128), np.float32),
        "identf": np.eye(128, dtype=np.float32),
        "identb": np.eye(128, dtype=np.float32),
        "jrev": np.eye(128, dtype=np.float32)[::-1].copy(),
    }
    return shared



def emit_ffn1(nc, p, st, io):
    sb = lambda name, shape, dt: st.enter_context(nc.sbuf_tensor(name, shape, dt))
    JC = 112
    NJQ = 4
    JPQ = JC // NJQ

    identf = sb("ld_idf", [128, 128], F32)
    identb = sb("ld_idb", [128, 128], BF16)
    ones128f = sb("ld_o128", [1, 128], F32)
    p.op("sync", lambda e: e.dma_start(out=identf[:], in_=io["identf"][:, :]),
         writes=["identf"], dma=True, stream="cst")
    p.op("gp", lambda e: e.dma_start(out=identb[:], in_=io["identb"][:, :]),
         writes=["identb"], dma=True, stream="cstb")
    p.op("sync", lambda e: e.dma_start(out=ones128f[:], in_=io["ones128"][:, :]),
         writes=["o128"], dma=True, stream="cst")
    c1tb = sb("ld_c1tb", [128, 32, 16], BF16)
    c2tb = sb("ld_c2tb", [16, 14336], BF16)
    p.op("gp", lambda e: e.dma_start(out=c1tb[:], in_=io["c1t_r"][:, :, :]),
         writes=["c1tb"], dma=True, stream="cstb")
    p.op("gp", lambda e: e.dma_start(out=c2tb[:], in_=io["c2t"][:, :]),
         writes=["c2tb"], dma=True, stream="cstb")

    xTh = sb("ld_xTh", [128, 32, 512], BF16)
    hmT = sb("ld_hmT", [128, JPQ, 512], BF16)
    oacc = sb("ld_oacc", [128, 32, 512], F32)
    ztb = sb("ld_ztb", [16, 512], BF16)
    bmaxT = sb("ld_bmaxT", [64, 512], F32)
    xb = sb("ld_xb", [128, D], BF16)
    w1b = [sb("ld_w1b0", [128, 32, 128], BF16), sb("ld_w1b1", [128, 32, 128], BF16)]
    w2b = [sb("ld_w2b0", [128, 8, 128], BF16), sb("ld_w2b1", [128, 8, 128], BF16)]
    brow = [sb("ld_brow0", [1, 512], F32), sb("ld_brow1", [1, 512], F32)]
    bbs = sb("ld_bbs", [128, 512], F32)
    bbadj = sb("ld_bbadj", [128, 512], F32)
    iv = sb("ld_iv", [128, 512], F32)
    hr = sb("ld_hr", [128, 512], BF16)
    mk = sb("ld_mk", [128, 512], BF16)
    bmax = sb("ld_bmax", [128, 64], F32)
    stg = [sb("ld_st0", [128, 4, 128], F32)]
    x1p = [sb("ld_x1p0", [128, 4, 128], F32)]
    a1p = [sb("ld_a1p0", [128, 4, 128], F32)]
    bmaxd = io["bmaxd"]

    for th in range(2):
        t0 = 512 * th
        with ExitStack() as ph:
            alp = lambda name, shape, dt: ph.enter_context(nc.psum_tensor(name, shape, dt))
            tpp = [alp(f"ld_tp0_{th}", [128, 128], BF16), alp(f"ld_tp1_{th}", [128, 128], BF16)]
            for t in range(4):
                p.op("gp", lambda e, th=th, t=t: e.dma_start(
                    out=xb[:], in_=io["xn1"][512 * th + 128 * t:512 * th + 128 * (t + 1), :]),
                    writes=["xb"], dma=True, stream="x")
                for a in range(32):
                    tb = a % 2
                    e_transpose(p, tpp[tb][:], xb[:, 128 * a:128 * (a + 1)],
                                identb[:], f"tp{tb}", ["xb", "identb"])
                    if a % 2:
                        p.op("dv", lambda e, a=a, t=t, tb=tb: e.tensor_copy(
                            out=xTh[:, a, 128 * t:128 * (t + 1)], in_=tpp[tb][:]),
                            reads=[f"tp{tb}"], writes=["xTh"])
                    else:
                        p.op("ac", lambda e, a=a, t=t, tb=tb: e.copy(
                            out=xTh[:, a, 128 * t:128 * (t + 1)], in_=tpp[tb][:]),
                            reads=[f"tp{tb}"], writes=["xTh"])
            p.hard_fence()
        with ExitStack() as ph:
            alp = lambda name, shape, dt: ph.enter_context(nc.psum_tensor(name, shape, dt))
            zp = alp(f"ld_zp_{th}", [16, 512], F32)
            e_mm_group(p, zp[:], [(c1tb[:, a, :], xTh[:, a, :]) for a in range(32)],
                       "zp", ["c1tb", "xTh"])
            p.op("ac", lambda e: e.copy(out=ztb[:], in_=zp[:]),
                 reads=["zp"], writes=["ztb"])
            p.hard_fence()
        with ExitStack() as ph:
            alp = lambda name, shape, dt: ph.enter_context(nc.psum_tensor(name, shape, dt))
            ip = alp(f"ld_ip_{th}", [128, 512], F32)
            pt = alp(f"ld_pt_{th}", [64, 128], F32)
            p.op("dv", lambda e: e.memset(bmax[:, 56:64], 0.0), writes=["bmax"])
            for tt in range(4):
                for j5 in range(28):
                    e_mm_group(p, ip[:], [(ztb[:, 128 * tt:128 * (tt + 1)],
                                           c2tb[:, 512 * j5:512 * (j5 + 1)])],
                               "ip", ["ztb", "c2tb"])
                    for half in range(2):
                        p.op("dv", lambda e, j5=j5, half=half: e.reduce_max(
                            out=bmax[:, 2 * j5 + half:2 * j5 + half + 1],
                            in_=ip[:, 256 * half:256 * (half + 1)], axis=AX.X),
                            reads=["ip"], writes=["bmax"])
                e_transpose(p, pt[:], bmax[:, 0:64], identf[:], "pt",
                            ["bmax", "identf"])
                p.op("dv", lambda e, tt=tt: e.tensor_copy(
                    out=bmaxT[:, 128 * tt:128 * (tt + 1)], in_=pt[:]),
                    reads=["pt"], writes=["bmaxT"])
            p.op("gp", lambda e: e.dma_start(out=bmaxd[:, :], in_=bmaxT[:]),
                 reads=["bmaxT"], writes=["hbm_bmaxd"], dma=True, stream="bm")
            p.hard_fence()
        for jq in range(NJQ):
            with ExitStack() as ph:
                alp = lambda name, shape, dt: ph.enter_context(nc.psum_tensor(name, shape, dt))
                hp_ = [alp(f"ld_hp_{th}_{jq}_0", [128, 512], F32),
                       alp(f"ld_hp_{th}_{jq}_1", [128, 512], F32)]
                ip2_ = [alp(f"ld_ip2_{th}_{jq}_0", [128, 512], F32),
                        alp(f"ld_ip2_{th}_{jq}_1", [128, 512], F32)]
                bb = alp(f"ld_bb_{th}_{jq}", [128, 512], F32)
                for jc in range(JPQ):
                    jg = JPQ * jq + jc
                    wbi = jc % 2
                    p.op("gp", lambda e, jg=jg, wbi=wbi: e.dma_start(
                        out=w1b[wbi][:], in_=io["w1_s"][jg]),
                        writes=[f"w1b{wbi}"], dma=True, stream=f"w1{wbi}")
                    hp = hp_[jc % 2]
                    ip2 = ip2_[jc % 2]
                    e_mm_group(p, hp[:], [(w1b[wbi][:, a, :], xTh[:, a, :])
                                          for a in range(32)],
                               f"hp{jc % 2}", [f"w1b{wbi}", "xTh"])
                    e_mm_group(p, ip2[:], [(c2tb[:, 128 * jg:128 * (jg + 1)], ztb[:])],
                               f"ip2{jc % 2}", ["c2tb", "ztb"])
                    if jg % 2 == 0:
                        bri = (jg // 2) % 2
                        p.op("sync", lambda e, jg=jg, bri=bri: e.dma_start(
                            out=brow[bri][:], in_=bmaxd[jg // 2:jg // 2 + 1, :]),
                            reads=["hbm_bmaxd"], writes=[f"brow{bri}"],
                            dma=True, stream="br")
                        e_mm_group(p, bb[:], [(ones128f[:], brow[bri][:])],
                                   "bb", ["o128", f"brow{bri}"])
                        p.op("ac", lambda e: e.copy(out=bbs[:], in_=bb[:]),
                             reads=["bb"], writes=["bbs"])
                        p.op("dv", lambda e: e.tensor_scalar_mul(
                            out=bbadj[:], in0=bbs[:], scalar1=-1.0),
                            reads=["bbs"], writes=["bbadj"])
                        p.op("dv", lambda e: e.tensor_max(
                            out=bbadj[:], in0=bbadj[:], in1=bbs[:]),
                            reads=["bbadj", "bbs"], writes=["bbadj"])
                        p.op("dv", lambda e: e.scalar_tensor_tensor(
                            out=bbadj[:], in0=bbadj[:], scalar=-1e-5, in1=bbs[:],
                            op0=ALU.mult, op1=ALU.add),
                            reads=["bbadj", "bbs"], writes=["bbadj"])
                    p.op("ac", lambda e, ip2=ip2: e.copy(out=iv[:], in_=ip2[:]),
                         reads=[f"ip2{jc % 2}"], writes=["iv"])
                    p.op("dv", lambda e: e.tensor_tensor(
                        out=mk[:], in0=iv[:], in1=bbadj[:], op=ALU.is_ge),
                        reads=["iv", "bbadj"], writes=["mk"])
                    p.op("ac", lambda e, hp=hp: e.activation(out=hr[:], in_=hp[:],
                                                      func=AF.Relu),
                        reads=[f"hp{jc % 2}"], writes=["hr"])
                    p.op("dv", lambda e, jc=jc: e.tensor_mul(
                        out=hmT[:, jc, :], in0=hr[:], in1=mk[:]),
                        reads=["hr", "mk"], writes=["hmT"])
                p.hard_fence()
            with ExitStack() as ph:
                alp = lambda name, shape, dt: ph.enter_context(nc.psum_tensor(name, shape, dt))
                po = [alp(f"ld_po{m}_{th}_{jq}", [128, 512], F32) for m in range(8)]
                for og in range(4):
                    for jc in range(JPQ):
                        jg = JPQ * jq + jc
                        wbi = jc % 2
                        p.op("gp", lambda e, og=og, jg=jg, wbi=wbi: e.dma_start(
                            out=w2b[wbi][:], in_=io["w2t_s"][og, jg]),
                            writes=[f"w2b{wbi}"], dma=True, stream=f"w2{wbi}")

                        def build(e, jc=jc, wbi=wbi):
                            inst = None
                            for m in range(8):
                                inst = e.matmul(po[m][:], w2b[wbi][:, m, :],
                                                hmT[:, jc, :],
                                                start=(jc == 0), stop=(jc == JPQ - 1))
                            return inst
                        p.op("pe", build, reads=[f"w2b{wbi}", "hmT"],
                             writes=[f"po{m}" for m in range(8)])
                    for m in range(8):
                        ot = 8 * og + m
                        if jq == 0:
                            eng = "dv" if m % 2 else "ac"
                            if eng == "dv":
                                p.op("dv", lambda e, m=m, ot=ot: e.tensor_copy(
                                    out=oacc[:, ot, :], in_=po[m][:]),
                                    reads=[f"po{m}"], writes=["oacc"])
                            else:
                                p.op("ac", lambda e, m=m, ot=ot: e.copy(
                                    out=oacc[:, ot, :], in_=po[m][:]),
                                    reads=[f"po{m}"], writes=["oacc"])
                        else:
                            p.op("dv", lambda e, m=m, ot=ot: e.tensor_add(
                                out=oacc[:, ot, :], in0=oacc[:, ot, :], in1=po[m][:]),
                                reads=[f"po{m}", "oacc"], writes=["oacc"])
                p.hard_fence()
        with ExitStack() as ph:
            alp = lambda name, shape, dt: ph.enter_context(nc.psum_tensor(name, shape, dt))
            tpf = [alp(f"ld_tpf0_{th}", [128, 128], F32), alp(f"ld_tpf1_{th}", [128, 128], F32)]
            for ot in range(32):
                sbf = 0
                p.op("sync", lambda e, th=th, ot=ot, sbf=sbf: e.dma_start(
                    out=x1p[sbf][:],
                    in_=io["x1"][512 * th:512 * (th + 1), 128 * ot:128 * (ot + 1)]
                    .rearrange("(pc pp) o -> pp pc o", pp=128)),
                    writes=[f"x1p{sbf}"], dma=True, stream="xp"),
                p.op("sync", lambda e, th=th, ot=ot, sbf=sbf: e.dma_start(
                    out=a1p[sbf][:],
                    in_=io["a1"][512 * th:512 * (th + 1), 128 * ot:128 * (ot + 1)]
                    .rearrange("(pc pp) o -> pp pc o", pp=128)),
                    writes=[f"a1p{sbf}"], dma=True, stream="ap")
                for pc in range(4):
                    tb = pc % 2
                    e_transpose(p, tpf[tb][:], oacc[:, ot, 128 * pc:128 * (pc + 1)],
                                identf[:], f"tpf{tb}", ["oacc", "identf"])
                    p.op("dv", lambda e, pc=pc, tb=tb, sbf=sbf: e.tensor_copy(
                        out=stg[sbf][:, pc, :], in_=tpf[tb][:]),
                        reads=[f"tpf{tb}"], writes=[f"stg{sbf}"])
                p.op("dv", lambda e, sbf=sbf: e.tensor_add(
                    out=stg[sbf][:], in0=stg[sbf][:], in1=x1p[sbf][:]),
                    reads=[f"stg{sbf}", f"x1p{sbf}"], writes=[f"stg{sbf}"])
                p.op("dv", lambda e, sbf=sbf: e.tensor_add(
                    out=stg[sbf][:], in0=stg[sbf][:], in1=a1p[sbf][:]),
                    reads=[f"stg{sbf}", f"a1p{sbf}"], writes=[f"stg{sbf}"])
                p.op("gp", lambda e, th=th, ot=ot, sbf=sbf: e.dma_start(
                    out=io["out"][512 * th:512 * (th + 1), 128 * ot:128 * (ot + 1)]
                    .rearrange("(pc pp) o -> pp pc o", pp=128),
                    in_=stg[sbf][:]),
                    reads=[f"stg{sbf}"], writes=["hbm_out"], dma=True, stream="out")
            p.hard_fence()
    p.finish(["hbm_out"])


def build_LD():
    nc = bass.Bass("TRN2", target_bir_lowering=False)
    io = {}
    def inp(name, shape):
        io[name] = nc.dram_tensor(name, list(shape), F32, kind="ExternalInput").ap()
    inp("xn1", (TOK, D)); inp("x1", (TOK, D)); inp("a1", (TOK, D))
    inp("c1t_r", (128, 32, 16)); inp("c2t", (16, 14336))
    inp("w1_s", (112, 128, 32, 128))
    inp("w2t_s", (4, 112, 128, 8, 128))
    inp("identf", (128, 128)); inp("identb", (128, 128)); inp("ones128", (1, 128))
    io["bmaxd"] = nc.dram_tensor("bmaxd", [64, 512], F32, kind="Internal").ap()
    io["out"] = nc.dram_tensor("out", [TOK, D], F32, kind="ExternalOutput").ap()
    with ExitStack() as st:
        p = Prog(nc, st)
        emit_ffn1(nc, p, st, io)
    return nc


def host_prep_LD(inputs):
    c1 = inputs["L1_ffn_c1_w"].astype(np.float32)
    c2 = inputs["L1_ffn_c2_w"].astype(np.float32)
    w1 = inputs["L1_ffn_w1"].astype(np.float32)
    w2t = inputs["L1_ffn_w2t"].astype(np.float32)
    shared = {
        "c1t_r": np.ascontiguousarray(c1.T.reshape(32, 128, 16).transpose(1, 0, 2)),
        "c2t": np.ascontiguousarray(c2.T),
        "w1_s": stage_stream(w1),
        "w2t_s": np.ascontiguousarray(
            w2t.reshape(112, 128, 4, 8, 128).transpose(2, 0, 1, 3, 4)),
        "identf": np.eye(128, dtype=np.float32),
        "identb": np.eye(128, dtype=np.float32),
        "ones128": np.ones((1, 128), np.float32),
    }
    return shared



def kernel_multilaunch(**inputs):
    x = np.ascontiguousarray(np.asarray(inputs["L0_attn_x"], np.float32)
                             .reshape(S, D))
    shA, pcA = host_prep_LA(inputs)
    shB = host_prep_LB(inputs)
    shC = host_prep_LC(inputs)
    shD = host_prep_LD(inputs)
    core_ids = list(range(NCORE))

    ncA = build_LA()
    in_maps = [dict(shA, **pcA[c]) for c in range(NCORE)]
    resA = run_bass_kernel_spmd(ncA, in_maps, core_ids=core_ids)
    a0 = [np.ascontiguousarray(np.asarray(resA.results[c]["a0"], np.float32))
          for c in range(NCORE)]

    ncB = build_LB()
    in_maps = [dict(shB, x0o=np.ascontiguousarray(x[TOK * c:TOK * (c + 1)]),
                    a0=a0[c]) for c in range(NCORE)]
    resB = run_bass_kernel_spmd(ncB, in_maps, core_ids=core_ids)
    x1 = [np.ascontiguousarray(np.asarray(resB.results[c]["x1"], np.float32))
          for c in range(NCORE)]
    xn1 = [np.ascontiguousarray(np.asarray(resB.results[c]["xn1"], np.float32))
           for c in range(NCORE)]

    ncC = build_LC()
    in_maps = [dict(shC, xn1=xn1[c]) for c in range(NCORE)]
    resC = run_bass_kernel_spmd(ncC, in_maps, core_ids=core_ids)
    a1 = [np.ascontiguousarray(np.asarray(resC.results[c]["a1"], np.float32))
          for c in range(NCORE)]

    ncD = build_LD()
    in_maps = [dict(shD, xn1=xn1[c], x1=x1[c], a1=a1[c]) for c in range(NCORE)]
    resD = run_bass_kernel_spmd(ncD, in_maps, core_ids=core_ids)
    out = np.concatenate([np.asarray(resD.results[c]["out"], np.float32)
                          for c in range(NCORE)], axis=0)
    return np.ascontiguousarray(out.reshape(1, S, D))



def build_fused():
    nc = bass.Bass("TRN2", target_bir_lowering=False)
    io = {}
    def inp(name, shape):
        io[name] = nc.dram_tensor(name, list(shape), F32, kind="ExternalInput").ap()
    inp("x0h", (HALO, D)); inp("kmask", (128, HALO // 128))
    inp("cosq", (128, TOK)); inp("sinq", (128, TOK))
    inp("cosk", (128, HALO)); inp("sink", (128, HALO))
    inp("alph", (128, 1024))
    inp("wq_s", (32, 128, 32, 128)); inp("wk_s", (1, 128, 32, 128))
    inp("wv_r", (128, 32, 128)); inp("wo_s", (32, 128, 32, 128))
    inp("onesb", (128, 128))
    inp("x0o", (TOK, D)); inp("w_s", (32, 128, 32, 128))
    inp("wqf_r", (128, 32, 256)); inp("wkf_r", (128, 32, 256))
    inp("af_r", (128, 32, 128)); inp("bmr", (128, D)); inp("bcol", (128, 32))
    for nm in ("lam_re", "lam_im", "p_re", "p_im", "q_re", "q_im",
               "ct_re", "ct_im", "bp"):
        inp(nm, (64, 1))
    inp("vct_re", (64, 64)); inp("vct_im", (64, 64))
    inp("dco", (1, 1)); inp("lstep", (1, 1))
    inp("wre", (1, D)); inp("wim", (1, D))
    inp("cre", (64, 64)); inp("cim", (64, 64))
    for nm in ("e64re", "e64im", "e64imn", "e64bre", "e64bimn", "twre", "twim"):
        inp(nm, (64, 64))
    inp("ones64", (1, 64)); inp("ones128", (1, 128)); inp("jrev", (128, 128))
    inp("c1t_r", (128, 32, 16)); inp("c2t", (16, 14336))
    inp("w1_s", (112, 128, 32, 128)); inp("w2t_s", (4, 112, 128, 8, 128))
    inp("identf", (128, 128)); inp("identb", (128, 128))
    for nm in ("a0", "x1", "xn1", "a1"):
        io[nm] = nc.dram_tensor(nm, [TOK, D], F32, kind="Internal").ap()
    io["kpad"] = nc.dram_tensor("kpad", [4352], F32, kind="Internal").ap()
    io["scr4"] = nc.dram_tensor("scr4", [8, D], F32, kind="Internal").ap()
    io["scr"] = None
    io["bmaxd"] = nc.dram_tensor("bmaxd", [64, 512], F32, kind="Internal").ap()
    io["out"] = nc.dram_tensor("out", [TOK, D], F32, kind="ExternalOutput").ap()
    with ExitStack() as st:
        p = Prog(nc, st)
        with ExitStack() as s1:
            emit_attn0(nc, p, s1, io)
            p.hard_fence()
        with ExitStack() as s2:
            emit_ffn0(nc, p, s2, io)
            p.hard_fence()
        with ExitStack() as s3:
            emit_s4(nc, p, s3, io)
            p.hard_fence()
        with ExitStack() as s4:
            emit_ffn1(nc, p, s4, io)
            p.hard_fence()
        p.finish(["hbm_out"])
    return nc


def kernel_fused(**inputs):
    x = np.ascontiguousarray(np.asarray(inputs["L0_attn_x"], np.float32)
                             .reshape(S, D))
    shA, pcA = host_prep_LA(inputs)
    shB = host_prep_LB(inputs)
    shC = host_prep_LC(inputs)
    shD = host_prep_LD(inputs)
    nc = build_fused()
    in_maps = []
    for c in range(NCORE):
        m = dict(shA)
        m.update(pcA[c])
        m.update(shB)
        m.update(shC)
        m.update(shD)
        m["x0o"] = np.ascontiguousarray(x[TOK * c:TOK * (c + 1)])
        in_maps.append(m)
    res = run_bass_kernel_spmd(nc, in_maps, core_ids=list(range(NCORE)))
    out = np.concatenate([np.asarray(res.results[c]["out"], np.float32)
                          for c in range(NCORE)], axis=0)
    return np.ascontiguousarray(out.reshape(1, S, D))


def kernel(**inputs):
    return kernel_fused(**inputs)
